# version 05
# speedup vs baseline: 1.1305x; 1.1249x over previous
import math
import numpy as np
import ml_dtypes
import concourse.bass as bass
import concourse.mybir as mybir
from concourse.bass_utils import run_bass_kernel_spmd

F32 = mybir.dt.float32
BF16 = mybir.dt.bfloat16
U32 = mybir.dt.uint32
AF = mybir.ActivationFunctionType
ALU = mybir.AluOpType
AX = mybir.AxisListType
BF = ml_dtypes.bfloat16

L = 2048
D = 1024
NT = L // 128
ND = D // 128
FF = 4096
NFC = FF // 128
NEG = -3.0e38
N_CORES = 8
H1 = 16


_PREF = [""]
_SCOPE = []


class _phase:
    def __init__(self, pref):
        self.pref = pref

    def __enter__(self):
        _PREF[0] = self.pref + "_"
        _SCOPE.append([])
        return self

    def __exit__(self, *a):
        for cm in reversed(_SCOPE.pop()):
            cm.__exit__(None, None, None)
        _PREF[0] = ""
        return False


def sb(nc, name, shape, dtype):
    cm = nc.sbuf_tensor(_PREF[0] + name, shape, dtype)
    t = cm.__enter__()
    if _SCOPE:
        _SCOPE[-1].append(cm)
    return t


def ps(nc, name, shape, dtype=F32):
    cm = nc.psum_tensor(_PREF[0] + name, shape, dtype)
    t = cm.__enter__()
    if _SCOPE:
        _SCOPE[-1].append(cm)
    return t


def sem(nc, name):
    return nc.semaphore(_PREF[0] + name).__enter__()


def emit_ln(nc, x, xn):


    xt = sb(nc, "xt", [128, ND * L], F32)
    sq = sb(nc, "sq", [128, ND * L], F32)
    ones = sb(nc, "ones", [128, 1], F32)
    onesb = sb(nc, "onesb", [1, 128], F32)
    mrow = sb(nc, "mrow", [1, L], F32)
    qrow = sb(nc, "qrow", [1, L], F32)
    ve = sb(nc, "ve", [1, L], F32)
    rrow = sb(nc, "rrow", [1, L], F32)
    tmp = sb(nc, "tmp", [128, 2 * 1024], F32)
    xnt = sb(nc, "xnt", [128, 2 * 1024], BF16)
    ps_s = ps(nc, "ps_s", [1, 2048], F32)
    ps_b = ps(nc, "ps_b", [128, 2048], F32)
    ld = sem(nc, "ld")
    ap = sem(nc, "ap")
    st = sem(nc, "st")
    st0 = sem(nc, "st0")
    st1 = sem(nc, "st1")

    with nc.Block() as b:
        @b.sync
        def _(sync):
            for c in range(ND):
                sync.dma_start(out=xt[:, c * L:(c + 1) * L],
                               in_=x[c * 128:(c + 1) * 128, :]).then_inc(ld, 16)
            sync.wait_ge(ld, 16 * ND)

        @b.vector
        def _(vector):
            vector.memset(ones[:], 1.0)
            vector.memset(onesb[:], 1.0)

    with nc.Block() as b:
        @b.tensor
        def _(tensor):
            for n in range(4):
                for c in range(ND):
                    tensor.matmul(ps_s[:, n * 512:(n + 1) * 512], ones[:],
                                  xt[:, c * L + n * 512:c * L + (n + 1) * 512],
                                  start=(c == 0), stop=(c == ND - 1))
    with nc.Block() as b:
        @b.vector
        def _(vector):
            vector.tensor_scalar_mul(mrow[:], ps_s[:], 1.0 / D)
    with nc.Block() as b:
        @b.scalar
        def _(scalar):
            for c in range(ND):
                scalar.activation(out=sq[:, c * L:(c + 1) * L],
                                  in_=xt[:, c * L:(c + 1) * L], func=AF.Square)
    with nc.Block() as b:
        @b.tensor
        def _(tensor):
            for n in range(4):
                for c in range(ND):
                    tensor.matmul(ps_s[:, n * 512:(n + 1) * 512], ones[:],
                                  sq[:, c * L + n * 512:c * L + (n + 1) * 512],
                                  start=(c == 0), stop=(c == ND - 1))
    with nc.Block() as b:
        @b.vector
        def _(vector):
            vector.tensor_scalar_mul(qrow[:], ps_s[:], 1.0 / D)
    with nc.Block() as b:
        @b.vector
        def _(vector):
            vector.scalar_tensor_tensor(out=ve[:], in0=mrow[:], scalar=-1.0,
                                        in1=mrow[:], op0=ALU.mult, op1=ALU.mult)
    with nc.Block() as b:
        @b.vector
        def _(vector):
            vector.tensor_add(ve[:], ve[:], qrow[:])
    with nc.Block() as b:
        @b.vector
        def _(vector):
            vector.tensor_scalar_add(ve[:], ve[:], 1e-5)
    with nc.Block() as b:
        @b.scalar
        def _(scalar):
            scalar.activation(out=ve[:], in_=ve[:], func=AF.Ln)
    with nc.Block() as b:
        @b.scalar
        def _(scalar):
            scalar.activation(out=rrow[:], in_=ve[:], func=AF.Exp, scale=-0.5)

    for g in range(2):
        with nc.Block() as b:
            @b.tensor
            def _(tensor):
                for n in range(2):
                    tensor.matmul(ps_b[:, n * 512:(n + 1) * 512], onesb[:],
                                  mrow[:, g * 1024 + n * 512:g * 1024 + (n + 1) * 512],
                                  start=True, stop=True)
                    tensor.matmul(ps_b[:, 1024 + n * 512:1024 + (n + 1) * 512], onesb[:],
                                  rrow[:, g * 1024 + n * 512:g * 1024 + (n + 1) * 512],
                                  start=True, stop=True)
        with nc.Block() as b:
            @b.vector
            def _(vector):
                for c in range(ND):
                    s = c % 2
                    ssem = st0 if s == 0 else st1
                    nprior = (g * ND + c) // 2
                    if nprior > 0:
                        vector.wait_ge(ssem, 16 * nprior)
                    vector.tensor_sub(tmp[:, s * 1024:(s + 1) * 1024],
                                      xt[:, c * L + g * 1024:c * L + (g + 1) * 1024],
                                      ps_b[:, 0:1024]).then_inc(ap, 1)
                    vector.wait_ge(ap, g * 2 * ND + 2 * c + 1)
                    vector.tensor_mul(xnt[:, s * 1024:(s + 1) * 1024],
                                      tmp[:, s * 1024:(s + 1) * 1024],
                                      ps_b[:, 1024:2048]).then_inc(ap, 1)

            @b.sync
            def _(sync):
                for c in range(ND):
                    s = c % 2
                    ssem = st0 if s == 0 else st1
                    sync.wait_ge(ap, g * 2 * ND + 2 * c + 2)
                    sync.dma_start(out=xn[c * 128:(c + 1) * 128, g * 1024:(g + 1) * 1024],
                                   in_=xnt[:, s * 1024:(s + 1) * 1024]).then_inc(ssem, 16)
    with nc.Block() as b:
        @b.sync
        def _(sync):
            sync.wait_ge(st0, 16 * ND)
            sync.wait_ge(st1, 16 * ND)




def build_ln():
    nc = bass.Bass("TRN2", target_bir_lowering=False, debug=False)
    x = nc.dram_tensor("x", [D, L], F32, kind="ExternalInput").ap()
    xn = nc.dram_tensor("xn", [D, L], BF16, kind="ExternalOutput").ap()
    emit_ln(nc, x, xn)
    return nc

def emit_l0ffn(nc, xn, w1T, w2T, b1c, b2c, f0):


    sign1 = sb(nc, "sign1", [128, ND * FF], BF16)
    sign2 = sb(nc, "sign2", [128, NFC * D], BF16)
    xnt = sb(nc, "xnt", [128, 2 * ND * 512], BF16)
    hbuf = sb(nc, "hbuf", [128, NFC * 512], BF16)
    ones = sb(nc, "ones", [128, 1], F32)
    acol = sb(nc, "acol", [128, NFC + ND], F32)
    bcol = sb(nc, "bcol", [128, NFC + ND], F32)
    otile = sb(nc, "otile", [128, 2 * 512], F32)
    ld = sem(nc, "ld")
    wl = sem(nc, "wl")
    pe = sem(nc, "pe")
    ev = sem(nc, "ev")
    st0 = sem(nc, "st0")
    st1 = sem(nc, "st1")
    xl0 = sem(nc, "xl0")
    xl1 = sem(nc, "xl1")

    with nc.psum_tensor("ps_al", [128, 512], F32) as ps_al, \
         nc.sbuf_tensor("wld", [128, 2048], F32) as wld, \
         nc.sbuf_tensor("absb", [128, 2048], F32) as absb:
        with nc.Block() as b:
            @b.vector
            def _(vector):
                vector.memset(ones[:], 1.0)

            @b.sync
            def _(sync):
                sync.dma_start(out=bcol[:, 0:NFC], in_=b1c).then_inc(ld, 16)
                sync.dma_start(out=bcol[:, NFC:], in_=b2c).then_inc(ld, 16)
                sync.wait_ge(ld, 32)

        nwl = 0
        for c in range(ND):
            for g2 in range(2):
                nwl += 1
                with nc.Block() as b:
                    @b.sync
                    def _(sync):
                        sync.dma_start(out=wld[:],
                                       in_=w1T[c * 128:(c + 1) * 128, g2 * 2048:(g2 + 1) * 2048]).then_inc(wl, 16)
                        sync.wait_ge(wl, 16 * nwl)
                with nc.Block() as b:
                    @b.scalar
                    def _(scalar):
                        scalar.activation(out=absb[:], in_=wld[:], func=AF.Abs)
                with nc.Block() as b:
                    @b.scalar
                    def _(scalar):
                        scalar.activation(out=sign1[:, c * FF + g2 * 2048:c * FF + (g2 + 1) * 2048],
                                          in_=wld[:], func=AF.Sign)

                    @b.tensor
                    def _(tensor):
                        for hh in range(16):
                            hc = g2 * 16 + hh
                            tensor.matmul(ps_al[:, hc * 8 + c:hc * 8 + c + 1],
                                          absb[:, hh * 128:(hh + 1) * 128], ones[:],
                                          start=True, stop=True)
        for hc in range(NFC):
            nwl += 1
            with nc.Block() as b:
                @b.sync
                def _(sync):
                    sync.dma_start(out=wld[:, 0:D],
                                   in_=w2T[hc * 128:(hc + 1) * 128, :]).then_inc(wl, 16)
                    sync.wait_ge(wl, 16 * nwl)
            with nc.Block() as b:
                @b.scalar
                def _(scalar):
                    scalar.activation(out=absb[:, 0:D], in_=wld[:, 0:D], func=AF.Abs)
            with nc.Block() as b:
                @b.scalar
                def _(scalar):
                    scalar.activation(out=sign2[:, hc * D:(hc + 1) * D],
                                      in_=wld[:, 0:D], func=AF.Sign)

                @b.tensor
                def _(tensor):
                    for dm in range(ND):
                        tensor.matmul(ps_al[:, 256 + dm * 32 + hc:256 + dm * 32 + hc + 1],
                                      absb[:, dm * 128:(dm + 1) * 128], ones[:],
                                      start=True, stop=True)
        with nc.Block() as b:
            @b.vector
            def _(vector):
                vector.tensor_reduce(out=acol[:, 0:NFC],
                                     in_=ps_al[:, 0:256].rearrange("p (a b) -> p a b", a=NFC),
                                     axis=AX.X, op=ALU.add)
        with nc.Block() as b:
            @b.vector
            def _(vector):
                vector.tensor_reduce(out=acol[:, NFC:NFC + ND],
                                     in_=ps_al[:, 256:512].rearrange("p (a b) -> p a b", a=ND),
                                     axis=AX.X, op=ALU.add)
        with nc.Block() as b:
            @b.vector
            def _(vector):
                vector.tensor_scalar_mul(acol[:, 0:NFC], acol[:, 0:NFC], 1.0 / D)
        with nc.Block() as b:
            @b.vector
            def _(vector):
                vector.tensor_scalar_mul(acol[:, NFC:NFC + ND], acol[:, NFC:NFC + ND], 1.0 / FF)

    with nc.psum_tensor("ps_m", [128, 2 * 512], F32) as ps_m:
        pe_n = 0
        ev_n = 0
        st_n = 0
        with nc.Block() as b:
            evacs = []

            @b.sync
            def _(sync):
                for tc in range(4):
                    xls = xl0 if tc % 2 == 0 else xl1
                    if tc >= 2:
                        sync.wait_ge(pe, (tc - 2) * (NFC + ND) + NFC)
                    for c in range(ND):
                        sync.dma_start(
                            out=xnt[:, (tc % 2) * ND * 512 + c * 512:(tc % 2) * ND * 512 + (c + 1) * 512],
                            in_=xn[c * 128:(c + 1) * 128, tc * 512:(tc + 1) * 512]).then_inc(xls, 16)

            kindseq = (["h"] * NFC + ["o"] * ND) * 4
            nh_pre = [0]
            no_pre = [0]
            for kk in kindseq:
                nh_pre.append(nh_pre[-1] + (kk == "h"))
                no_pre.append(no_pre[-1] + (kk == "o"))
            ov = sem(nc, "ov")

            @b.tensor
            def _(tensor):
                nonlocal pe_n
                for tc in range(4):
                    xls = xl0 if tc % 2 == 0 else xl1
                    tensor.wait_ge(xls, 16 * ND * (tc // 2 + 1))
                    for m in range(NFC):
                        bank = (tc * (NFC + ND) + m) % 2
                        grp = tc * (NFC + ND) + m
                        if grp >= 2:
                            tensor.wait_ge(ev, nh_pre[grp - 1])
                            if no_pre[grp - 1]:
                                tensor.wait_ge(ov, no_pre[grp - 1])
                        for c in range(ND):
                            mm = tensor.matmul(ps_m[:, bank * 512:(bank + 1) * 512],
                                               sign1[:, c * FF + m * 128:c * FF + (m + 1) * 128],
                                               xnt[:, (tc % 2) * ND * 512 + c * 512:(tc % 2) * ND * 512 + (c + 1) * 512],
                                               start=(c == 0), stop=(c == ND - 1))
                        pe_n += 1
                        mm.then_inc(pe, 1)
                        evacs.append(("h", tc, m, pe_n, bank))
                    for dm in range(ND):
                        bank = (tc * (NFC + ND) + NFC + dm) % 2
                        grp = tc * (NFC + ND) + NFC + dm
                        tensor.wait_ge(ev, max(nh_pre[grp - 1], (tc + 1) * NFC))
                        if no_pre[max(grp - 1, 0)]:
                            tensor.wait_ge(ov, no_pre[grp - 1])
                        for hc in range(NFC):
                            mm = tensor.matmul(ps_m[:, bank * 512:(bank + 1) * 512],
                                               sign2[:, hc * D + dm * 128:hc * D + (dm + 1) * 128],
                                               hbuf[:, hc * 512:(hc + 1) * 512],
                                               start=(hc == 0), stop=(hc == NFC - 1))
                        pe_n += 1
                        mm.then_inc(pe, 1)
                        evacs.append(("o", tc, dm, pe_n, bank))

            @b.scalar
            def _(scalar):
                nonlocal ev_n
                for kind, tc, idx, pe_target, bank in evacs:
                    if kind != "h":
                        continue
                    scalar.wait_ge(pe, pe_target)
                    scalar.activation(out=hbuf[:, idx * 512:(idx + 1) * 512],
                                      in_=ps_m[:, bank * 512:(bank + 1) * 512],
                                      func=AF.Gelu,
                                      bias=bcol[:, idx:idx + 1],
                                      scale=acol[:, idx:idx + 1]).then_inc(ev, 1)
                    ev_n += 1

            @b.vector
            def _(vector):
                for kind, tc, idx, pe_target, bank in evacs:
                    if kind != "o":
                        continue
                    os_ = (tc * ND + idx) % 2
                    ssem = st0 if os_ == 0 else st1
                    nprior = (tc * ND + idx) // 2
                    if nprior > 0:
                        vector.wait_ge(ssem, 16 * nprior)
                    vector.wait_ge(pe, pe_target)
                    vector.tensor_scalar(out=otile[:, os_ * 512:(os_ + 1) * 512],
                                         in0=ps_m[:, bank * 512:(bank + 1) * 512],
                                         scalar1=acol[:, NFC + idx:NFC + idx + 1],
                                         scalar2=bcol[:, NFC + idx:NFC + idx + 1],
                                         op0=ALU.mult, op1=ALU.add).then_inc(ov, 1)

            @b.gpsimd
            def _(gpsimd):
                n_out = 0
                for ei, (kind, tc, idx, pe_target, bank) in enumerate(evacs):
                    if kind != "o":
                        continue
                    os_ = (tc * ND + idx) % 2
                    ssem = st0 if os_ == 0 else st1
                    gpsimd.wait_ge(ov, no_pre[ei + 1])
                    gpsimd.dma_start(out=f0[idx * 128:(idx + 1) * 128, tc * 512:(tc + 1) * 512],
                                     in_=otile[:, os_ * 512:(os_ + 1) * 512]).then_inc(ssem, 16)
                    n_out += 1
                gpsimd.wait_ge(st0, 16 * ((n_out + 1) // 2))
                gpsimd.wait_ge(st1, 16 * (n_out // 2))




def build_l0ffn():
    nc = bass.Bass("TRN2", target_bir_lowering=False, debug=False)
    xn = nc.dram_tensor("xn", [D, L], BF16, kind="ExternalInput").ap()
    w1T = nc.dram_tensor("w1T", [D, FF], F32, kind="ExternalInput").ap()
    w2T = nc.dram_tensor("w2T", [FF, D], F32, kind="ExternalInput").ap()
    b1c = nc.dram_tensor("b1c", [128, NFC], F32, kind="ExternalInput").ap()
    b2c = nc.dram_tensor("b2c", [128, ND], F32, kind="ExternalInput").ap()
    f0 = nc.dram_tensor("f0", [D, L], F32, kind="ExternalOutput").ap()
    emit_l0ffn(nc, xn, w1T, w2T, b1c, b2c, f0)
    return nc

def emit_res_ln(nc, x, a0, f0, x1, xn):


    xt = sb(nc, "xt", [128, ND * L], F32)
    sq = sb(nc, "sq", [128, ND * L], F32)
    ones = sb(nc, "ones", [128, 1], F32)
    onesb = sb(nc, "onesb", [1, 128], F32)
    mrow = sb(nc, "mrow", [1, L], F32)
    qrow = sb(nc, "qrow", [1, L], F32)
    ve = sb(nc, "ve", [1, L], F32)
    rrow = sb(nc, "rrow", [1, L], F32)
    tmp = sb(nc, "tmp", [128, 2 * 1024], F32)
    xnt = sb(nc, "xnt", [128, 2 * 1024], BF16)
    ps_s = ps(nc, "ps_s", [1, 2048], F32)
    ps_b = ps(nc, "ps_b", [128, 2048], F32)
    ld = sem(nc, "ld")
    ap = sem(nc, "ap")
    st = sem(nc, "st")
    st0 = sem(nc, "st0")
    st1 = sem(nc, "st1")

    with nc.Block() as b:
        @b.sync
        def _(sync):
            for c in range(ND):
                sync.dma_start(out=xt[:, c * L:(c + 1) * L],
                               in_=x[c * 128:(c + 1) * 128, :]).then_inc(ld, 16)
            for c in range(ND):
                sync.dma_start(out=sq[:, c * L:(c + 1) * L],
                               in_=a0[c * 128:(c + 1) * 128, :]).then_inc(ld, 16)
            sync.wait_ge(ld, 16 * 2 * ND)

        @b.vector
        def _(vector):
            vector.memset(ones[:], 1.0)
            vector.memset(onesb[:], 1.0)
    with nc.Block() as b:
        @b.vector
        def _(vector):
            for c in range(ND):
                vector.tensor_add(xt[:, c * L:(c + 1) * L], xt[:, c * L:(c + 1) * L],
                                  sq[:, c * L:(c + 1) * L])
    with nc.Block() as b:
        @b.sync
        def _(sync):
            for c in range(ND):
                sync.dma_start(out=sq[:, c * L:(c + 1) * L],
                               in_=f0[c * 128:(c + 1) * 128, :]).then_inc(ld, 16)
            sync.wait_ge(ld, 16 * 3 * ND)
    with nc.Block() as b:
        @b.vector
        def _(vector):
            for c in range(ND):
                vector.tensor_add(xt[:, c * L:(c + 1) * L], xt[:, c * L:(c + 1) * L],
                                  sq[:, c * L:(c + 1) * L])
    with nc.Block() as b:
        @b.sync
        def _(sync):
            for c in range(ND):
                sync.dma_start(out=x1[c * 128:(c + 1) * 128, :],
                               in_=xt[:, c * L:(c + 1) * L]).then_inc(st, 16)
            sync.wait_ge(st, 16 * ND)

    with nc.Block() as b:
        @b.tensor
        def _(tensor):
            for n in range(4):
                for c in range(ND):
                    tensor.matmul(ps_s[:, n * 512:(n + 1) * 512], ones[:],
                                  xt[:, c * L + n * 512:c * L + (n + 1) * 512],
                                  start=(c == 0), stop=(c == ND - 1))
    with nc.Block() as b:
        @b.vector
        def _(vector):
            vector.tensor_scalar_mul(mrow[:], ps_s[:], 1.0 / D)
    with nc.Block() as b:
        @b.scalar
        def _(scalar):
            for c in range(ND):
                scalar.activation(out=sq[:, c * L:(c + 1) * L],
                                  in_=xt[:, c * L:(c + 1) * L], func=AF.Square)
    with nc.Block() as b:
        @b.tensor
        def _(tensor):
            for n in range(4):
                for c in range(ND):
                    tensor.matmul(ps_s[:, n * 512:(n + 1) * 512], ones[:],
                                  sq[:, c * L + n * 512:c * L + (n + 1) * 512],
                                  start=(c == 0), stop=(c == ND - 1))
    with nc.Block() as b:
        @b.vector
        def _(vector):
            vector.tensor_scalar_mul(qrow[:], ps_s[:], 1.0 / D)
    with nc.Block() as b:
        @b.vector
        def _(vector):
            vector.scalar_tensor_tensor(out=ve[:], in0=mrow[:], scalar=-1.0,
                                        in1=mrow[:], op0=ALU.mult, op1=ALU.mult)
    with nc.Block() as b:
        @b.vector
        def _(vector):
            vector.tensor_add(ve[:], ve[:], qrow[:])
    with nc.Block() as b:
        @b.vector
        def _(vector):
            vector.tensor_scalar_add(ve[:], ve[:], 1e-5)
    with nc.Block() as b:
        @b.scalar
        def _(scalar):
            scalar.activation(out=ve[:], in_=ve[:], func=AF.Ln)
    with nc.Block() as b:
        @b.scalar
        def _(scalar):
            scalar.activation(out=rrow[:], in_=ve[:], func=AF.Exp, scale=-0.5)
    for g in range(2):
        with nc.Block() as b:
            @b.tensor
            def _(tensor):
                for n in range(2):
                    tensor.matmul(ps_b[:, n * 512:(n + 1) * 512], onesb[:],
                                  mrow[:, g * 1024 + n * 512:g * 1024 + (n + 1) * 512],
                                  start=True, stop=True)
                    tensor.matmul(ps_b[:, 1024 + n * 512:1024 + (n + 1) * 512], onesb[:],
                                  rrow[:, g * 1024 + n * 512:g * 1024 + (n + 1) * 512],
                                  start=True, stop=True)
        with nc.Block() as b:
            @b.vector
            def _(vector):
                for c in range(ND):
                    s = c % 2
                    ssem = st0 if s == 0 else st1
                    nprior = (g * ND + c) // 2
                    if nprior > 0:
                        vector.wait_ge(ssem, 16 * nprior)
                    vector.tensor_sub(tmp[:, s * 1024:(s + 1) * 1024],
                                      xt[:, c * L + g * 1024:c * L + (g + 1) * 1024],
                                      ps_b[:, 0:1024]).then_inc(ap, 1)
                    vector.wait_ge(ap, g * 2 * ND + 2 * c + 1)
                    vector.tensor_mul(xnt[:, s * 1024:(s + 1) * 1024],
                                      tmp[:, s * 1024:(s + 1) * 1024],
                                      ps_b[:, 1024:2048]).then_inc(ap, 1)

            @b.gpsimd
            def _(gpsimd):
                for c in range(ND):
                    s = c % 2
                    ssem = st0 if s == 0 else st1
                    gpsimd.wait_ge(ap, g * 2 * ND + 2 * c + 2)
                    gpsimd.dma_start(out=xn[c * 128:(c + 1) * 128, g * 1024:(g + 1) * 1024],
                                     in_=xnt[:, s * 1024:(s + 1) * 1024]).then_inc(ssem, 16)
    with nc.Block() as b:
        @b.sync
        def _(sync):
            sync.wait_ge(st0, 16 * ND)
            sync.wait_ge(st1, 16 * ND)




def build_res_ln():
    nc = bass.Bass("TRN2", target_bir_lowering=False, debug=False)
    x = nc.dram_tensor("x", [D, L], F32, kind="ExternalInput").ap()
    a0 = nc.dram_tensor("a0", [D, L], F32, kind="ExternalInput").ap()
    f0 = nc.dram_tensor("f0", [D, L], F32, kind="ExternalInput").ap()
    x1 = nc.dram_tensor("x1", [D, L], F32, kind="ExternalOutput").ap()
    xn = nc.dram_tensor("xn", [D, L], BF16, kind="ExternalOutput").ap()
    emit_res_ln(nc, x, a0, f0, x1, xn)
    return nc

def emit_res2(nc, x1, a1, f1, out):

    xt = sb(nc, "xt", [128, ND * L], F32)
    yt = sb(nc, "yt", [128, ND * L], F32)
    ld = sem(nc, "ld")
    st = sem(nc, "st")
    with nc.Block() as b:
        @b.sync
        def _(sync):
            for c in range(ND):
                sync.dma_start(out=xt[:, c * L:(c + 1) * L],
                               in_=x1[c * 128:(c + 1) * 128, :]).then_inc(ld, 16)
            for c in range(ND):
                sync.dma_start(out=yt[:, c * L:(c + 1) * L],
                               in_=a1[c * 128:(c + 1) * 128, :]).then_inc(ld, 16)
            sync.wait_ge(ld, 16 * 2 * ND)
    with nc.Block() as b:
        @b.vector
        def _(vector):
            for c in range(ND):
                vector.tensor_add(xt[:, c * L:(c + 1) * L], xt[:, c * L:(c + 1) * L],
                                  yt[:, c * L:(c + 1) * L])
    with nc.Block() as b:
        @b.sync
        def _(sync):
            for c in range(ND):
                sync.dma_start(out=yt[:, c * L:(c + 1) * L],
                               in_=f1[c * 128:(c + 1) * 128, :]).then_inc(ld, 16)
            sync.wait_ge(ld, 16 * 3 * ND)
    with nc.Block() as b:
        @b.vector
        def _(vector):
            for c in range(ND):
                vector.tensor_add(xt[:, c * L:(c + 1) * L], xt[:, c * L:(c + 1) * L],
                                  yt[:, c * L:(c + 1) * L])
    with nc.Block() as b:
        @b.sync
        def _(sync):
            for c in range(ND):
                sync.dma_start(out=out[c * 128:(c + 1) * 128, :],
                               in_=xt[:, c * L:(c + 1) * L]).then_inc(st, 16)
            sync.wait_ge(st, 16 * ND)




def build_res2():
    nc = bass.Bass("TRN2", target_bir_lowering=False, debug=False)
    x1 = nc.dram_tensor("x1", [D, L], F32, kind="ExternalInput").ap()
    a1 = nc.dram_tensor("a1", [D, L], F32, kind="ExternalInput").ap()
    f1 = nc.dram_tensor("f1", [D, L], F32, kind="ExternalInput").ap()
    out = nc.dram_tensor("out", [D, L], F32, kind="ExternalOutput").ap()
    emit_res2(nc, x1, a1, f1, out)
    return nc

def emit_l1ffn(nc, xn, fc1A, fc1B, fc2A, fc2B, lg1, lg2, riota1, riota2, f1):

    s_a1 = nc.dram_tensor("s_a1", [32, 128, 64], BF16, kind="Internal").ap()
    s_a2 = nc.dram_tensor("s_a2", [32, 512, 64], BF16, kind="Internal").ap()
    s_b1 = nc.dram_tensor("s_b1", [32, 64, FF], BF16, kind="Internal").ap()
    w1s = nc.dram_tensor("w1s", [D, FF], BF16, kind="Internal",
                         addr_space="Shared").ap()
    w2s = nc.dram_tensor("w2s", [FF, D], BF16, kind="Internal",
                         addr_space="Shared").ap()
    s_a2t = nc.dram_tensor("s_a2t", [32, 64, 512], BF16, kind="Internal").ap()
    w1loc = nc.dram_tensor("w1loc", [128, FF], BF16, kind="Internal").ap()
    w2loc = nc.dram_tensor("w2loc", [512, D], BF16, kind="Internal").ap()

    GEC = 1.5957691216
    GEB = 0.044715

    lgw = sb(nc, "lgw", [1, 2 * 256], F32)
    mval = sb(nc, "mval", [1, 2 * 32], F32)
    mv8 = sb(nc, "mv8", [1, 8], F32)
    mid = sb(nc, "mid", [1, 2 * 32], U32)
    midf = sb(nc, "midf", [1, 2 * 32], F32)
    wv = sb(nc, "wv", [1, 2 * 32], F32)
    selbc = sb(nc, "selbc", [128, 2 * 32], F32)
    wvbc = sb(nc, "wvbc", [128, 2 * 32], F32)
    onesb = sb(nc, "onesb", [1, 128], F32)
    iota_f = sb(nc, "iota_f", [128, 1], F32)
    riota1s = sb(nc, "riota1s", [128, 1], F32)
    riota2s = sb(nc, "riota2s", [128, 1], F32)
    ident = sb(nc, "ident", [128, 128], BF16)
    idxu = sb(nc, "idxu", [128, 8], U32)
    idxf = sb(nc, "idxf", [128, 8], F32)
    ssc = sb(nc, "ssc", [1, 8], F32)

    ld = sem(nc, "ld")
    gat = sem(nc, "gat")
    sto = sem(nc, "sto")
    ap = sem(nc, "ap")
    ini = sem(nc, "ini")

    with nc.Block() as b:
        @b.sync
        def _(sync):
            sync.dma_start(out=lgw[:, 0:256], in_=lg1).then_inc(ld, 16)
            sync.dma_start(out=lgw[:, 256:512], in_=lg2).then_inc(ld, 16)
            sync.dma_start(out=riota1s[:], in_=riota1).then_inc(ld, 16)
            sync.dma_start(out=riota2s[:], in_=riota2).then_inc(ld, 16)
            sync.wait_ge(ld, 64)

        @b.gpsimd
        def _(gpsimd):
            gpsimd.iota(iota_f[:], [[0, 1]], base=0, channel_multiplier=1,
                        allow_small_or_imprecise_dtypes=True)
            gpsimd.memset(ident[:], 0.0).then_inc(ini, 1)
            gpsimd.wait_ge(ini, 1)
            gpsimd.affine_select(out=ident[:], in_=ident[:],
                                 compare_op=ALU.not_equal, fill=1.0, base=0,
                                 pattern=[[-1, 128]], channel_multiplier=1)

        @b.vector
        def _(vector):
            vector.memset(onesb[:], 1.0)
    for fi in range(2):
        for r in range(4):
            with nc.Block() as b:
                @b.vector
                def _(vector):
                    vector.max(out=mv8[:], in_=lgw[:, fi * 256:(fi + 1) * 256])
            with nc.Block() as b:
                @b.vector
                def _(vector):
                    vector.max_index(out=mid[:, fi * 32 + r * 8:fi * 32 + (r + 1) * 8],
                                     in_max=mv8[:], in_values=lgw[:, fi * 256:(fi + 1) * 256])
            with nc.Block() as b:
                @b.vector
                def _(vector):
                    vector.tensor_copy(out=mval[:, fi * 32 + r * 8:fi * 32 + (r + 1) * 8],
                                       in_=mv8[:])
            with nc.Block() as b:
                @b.vector
                def _(vector):
                    vector.match_replace(out=lgw[:, fi * 256:(fi + 1) * 256],
                                         in_to_replace=mv8[:],
                                         in_values=lgw[:, fi * 256:(fi + 1) * 256],
                                         imm_value=-1e30)
        with nc.Block() as b:
            @b.vector
            def _(vector):
                vector.tensor_scalar_mul(ssc[:, 0:1], mval[:, fi * 32:fi * 32 + 1], -1.0)
        with nc.Block() as b:
            @b.scalar
            def _(scalar):
                scalar.activation(out=wv[:, fi * 32:(fi + 1) * 32],
                                  in_=mval[:, fi * 32:(fi + 1) * 32],
                                  func=AF.Exp, bias=ssc[:, 0:1])
        with nc.Block() as b:
            @b.vector
            def _(vector):
                vector.reduce_sum(out=ssc[:, 1:2], in_=wv[:, fi * 32:(fi + 1) * 32],
                                  axis=AX.X)
        with nc.Block() as b:
            @b.vector
            def _(vector):
                vector.reciprocal(out=ssc[:, 2:3], in_=ssc[:, 1:2])
        with nc.Block() as b:
            @b.vector
            def _(vector):
                vector.tensor_scalar_mul(wv[:, fi * 32:(fi + 1) * 32],
                                         wv[:, fi * 32:(fi + 1) * 32], ssc[:, 2:3])
        with nc.Block() as b:
            @b.vector
            def _(vector):
                vector.tensor_copy(out=midf[:, fi * 32:(fi + 1) * 32],
                                   in_=mid[:, fi * 32:(fi + 1) * 32])
    with nc.psum_tensor("ps_bc", [128, 128], F32) as ps_bc:
        with nc.Block() as b:
            @b.tensor
            def _(tensor):
                tensor.matmul(ps_bc[:, 0:64], onesb[:], midf[:], start=True, stop=True)
                tensor.matmul(ps_bc[:, 64:128], onesb[:], wv[:], start=True, stop=True)
        with nc.Block() as b:
            @b.vector
            def _(vector):
                vector.tensor_copy(out=selbc[:], in_=ps_bc[:, 0:64])
        with nc.Block() as b:
            @b.vector
            def _(vector):
                vector.tensor_copy(out=wvbc[:], in_=ps_bc[:, 64:128])

    cm_b2 = nc.sbuf_tensor("b2buf", [64, 32 * D], BF16)
    b2buf = cm_b2.__enter__()
    specs = []
    for k in range(32):
        specs.append(("b1", k))
        specs.append(("a2", k))
        specs.append(("a1", k))
        specs.append(("b2", k))
    STO = {"b1": 2, "a2": 1, "a1": 1, "b2": 0}
    slot_pfx = {0: [0], 1: [0]}
    b2_pfx = [0]
    for gi, (kind, _) in enumerate(specs):
        for sl in (0, 1):
            slot_pfx[sl].append(slot_pfx[sl][-1] + (STO[kind] if gi % 2 == sl else 0))
        b2_pfx.append(b2_pfx[-1] + (1 if kind == "b2" else 0))
    sto2 = [sem(nc, "sto_s0"), sem(nc, "sto_s1")]
    gat2 = [sem(nc, "gat_s0"), sem(nc, "gat_s1")]
    b2e = sem(nc, "b2e")

    with nc.sbuf_tensor("gbuf", [128, 2 * FF], F32) as gbuf:
        with nc.Block() as b:
            @b.vector
            def _(vector):
                for gi, (kind, k) in enumerate(specs):
                    fi = 0 if kind in ("b1", "a1") else 1
                    rk = {"b1": 64.0, "a2": 128.0, "a1": 32.0, "b2": 64.0}[kind]
                    si = gi % 8
                    if gi >= 8:
                        vector.wait_ge(gat2[gi % 2], 16 * ((gi - 8) // 2 + 1))
                    riot = (riota1s if kind == "a1" else
                            riota2s if kind == "a2" else iota_f)
                    vector.scalar_tensor_tensor(out=idxf[:, si:si + 1],
                                                in0=selbc[:, fi * 32 + k:fi * 32 + k + 1],
                                                scalar=rk, in1=riot[:],
                                                op0=ALU.mult, op1=ALU.add).then_inc(ap, 1)
                    vector.wait_ge(ap, 2 * gi + 1)
                    vector.tensor_copy(out=idxu[:, si:si + 1],
                                       in_=idxf[:, si:si + 1]).then_inc(ap, 1)

            @b.gpsimd
            def _(gpsimd):
                for gi, (kind, k) in enumerate(specs):
                    si = gi % 8
                    off = (gi % 2) * FF
                    gpsimd.wait_ge(ap, 2 * gi + 2)
                    if gi >= 2:
                        gpsimd.wait_ge(sto2[gi % 2], 16 * slot_pfx[gi % 2][gi - 1])
                        gpsimd.wait_ge(b2e, b2_pfx[gi - 1])
                    if kind == "b1":
                        gpsimd.indirect_dma_start(
                            out=gbuf[0:64, off:off + FF], out_offset=None,
                            in_=fc1B.rearrange("p r f -> (p r) f"),
                            in_offset=bass.IndirectOffsetOnAxis(ap=idxu[0:64, si:si + 1], axis=0),
                        ).then_inc(gat2[gi % 2], 16)
                        gpsimd.wait_ge(gat2[gi % 2], 16 * (gi // 2 + 1))
                        for hh in range(2):
                            gpsimd.dma_start(
                                out=s_b1[k][:, hh * 2048:(hh + 1) * 2048],
                                in_=gbuf[0:64, off + hh * 2048:off + (hh + 1) * 2048],
                            ).then_inc(sto2[gi % 2], 16)
                    elif kind == "a2":
                        gpsimd.indirect_dma_start(
                            out=gbuf[0:16, off:off + 2048], out_offset=None,
                            in_=fc2A.rearrange("p h r -> (p h r)").rearrange("(x y) -> x y", y=2048),
                            in_offset=bass.IndirectOffsetOnAxis(ap=idxu[0:16, si:si + 1], axis=0),
                        ).then_inc(gat2[gi % 2], 16)
                        gpsimd.wait_ge(gat2[gi % 2], 16 * (gi // 2 + 1))
                        gpsimd.dma_start(
                            out=s_a2[k].rearrange("h r -> (h r)").rearrange("(x y) -> x y", y=2048),
                            in_=gbuf[0:16, off:off + 2048]).then_inc(sto2[gi % 2], 16)
                    elif kind == "a1":
                        gpsimd.indirect_dma_start(
                            out=gbuf[0:4, off:off + 2048], out_offset=None,
                            in_=fc1A.rearrange("p d r -> (p d r)").rearrange("(x y) -> x y", y=2048),
                            in_offset=bass.IndirectOffsetOnAxis(ap=idxu[0:4, si:si + 1], axis=0),
                        ).then_inc(gat2[gi % 2], 16)
                        gpsimd.wait_ge(gat2[gi % 2], 16 * (gi // 2 + 1))
                        gpsimd.dma_start(
                            out=s_a1[k].rearrange("d r -> (d r)").rearrange("(x y) -> x y", y=2048),
                            in_=gbuf[0:4, off:off + 2048]).then_inc(sto2[gi % 2], 16)
                    else:
                        gpsimd.indirect_dma_start(
                            out=gbuf[0:64, off:off + D], out_offset=None,
                            in_=fc2B.rearrange("p r d -> (p r) d"),
                            in_offset=bass.IndirectOffsetOnAxis(ap=idxu[0:64, si:si + 1], axis=0),
                        ).then_inc(gat2[gi % 2], 16)

            @b.scalar
            def _(scalar):
                for gi, (kind, k) in enumerate(specs):
                    if kind != "b2":
                        continue
                    off = (gi % 2) * FF
                    scalar.wait_ge(gat2[gi % 2], 16 * (gi // 2 + 1))
                    scalar.activation(out=b2buf[:, k * D:(k + 1) * D],
                                      in_=gbuf[0:64, off:off + D],
                                      func=AF.Identity,
                                      scale=wvbc[0:64, 32 + k:32 + k + 1]).then_inc(b2e, 1)

    a1l = sem(nc, "a1l")
    t2l = [sem(nc, "t2l0"), sem(nc, "t2l1")]
    t2p = sem(nc, "t2p")
    t2e = sem(nc, "t2e")
    t2s = sem(nc, "t2s")
    b1l = [sem(nc, "b1l0"), sem(nc, "b1l1")]
    cpe = sem(nc, "cpe")
    cev = sem(nc, "cev")
    cst = [sem(nc, "cst0"), sem(nc, "cst1")]
    cm_a1 = nc.sbuf_tensor("a1t", [64, 32 * 128], BF16)
    a1t = cm_a1.__enter__()
    with nc.sbuf_tensor("a2ld", [128, 2 * 2048], BF16) as a2ld, \
         nc.sbuf_tensor("a2t", [64, 512], BF16) as a2t, \
         nc.psum_tensor("ps_t2", [64, 512], BF16) as ps_t2:
        with nc.Block() as b:
            @b.sync
            def _(sync):
                sync.wait_ge(sto2[0], 16 * slot_pfx[0][-1])
                sync.wait_ge(sto2[1], 16 * slot_pfx[1][-1])
                sync.wait_ge(b2e, b2_pfx[-1])
                for k in range(32):
                    sl = k % 2
                    if k >= 2:
                        sync.wait_ge(t2p, k - 1)
                    sync.dma_start(
                        out=a2ld[:, sl * 2048:sl * 2048 + 256].rearrange("p (c r) -> p c r", c=4),
                        in_=s_a2[k].rearrange("(c p) r -> p c r", p=128)).then_inc(t2l[sl], 16)

            @b.tensor
            def _(tensor):
                for k in range(32):
                    sl = k % 2
                    tensor.wait_ge(t2l[sl], 16 * (k // 2 + 1))
                    tensor.wait_ge(t2e, k)
                    for c in range(4):
                        mm = tensor.transpose(ps_t2[:, c * 128:(c + 1) * 128],
                                              a2ld[:, sl * 2048 + c * 64:sl * 2048 + (c + 1) * 64],
                                              ident[:])
                    mm.then_inc(t2p, 1)

            @b.vector
            def _(vector):
                for k in range(32):
                    vector.wait_ge(t2p, k + 1)
                    if k >= 1:
                        vector.wait_ge(t2s, 16 * k)
                    vector.tensor_copy(out=a2t[:], in_=ps_t2[:]).then_inc(t2e, 1)

            @b.gpsimd
            def _(gpsimd):
                for k in range(32):
                    gpsimd.wait_ge(t2e, k + 1)
                    gpsimd.dma_start(out=s_a2t[k], in_=a2t[:]).then_inc(t2s, 16)
                gpsimd.wait_ge(t2s, 16 * 32)

        with nc.Block() as b:
            @b.sync
            def _(sync):
                for k in range(32):
                    sl = k % 2
                    if k >= 2:
                        sync.wait_ge(t2p, 32 + k - 1)
                    sync.dma_start(
                        out=a2ld[:, sl * 2048:sl * 2048 + 64],
                        in_=s_a1[k]).then_inc(t2l[sl], 16)

            @b.tensor
            def _(tensor):
                for k in range(32):
                    sl = k % 2
                    tensor.wait_ge(t2l[sl], 16 * (16 + k // 2 + 1))
                    tensor.wait_ge(t2e, 32 + k)
                    mm = tensor.transpose(ps_t2[:, 0:128],
                                          a2ld[:, sl * 2048:sl * 2048 + 64],
                                          ident[:])
                    mm.then_inc(t2p, 1)

            @b.vector
            def _(vector):
                for k in range(32):
                    vector.wait_ge(t2p, 32 + k + 1)
                    vector.tensor_copy(out=a1t[:, k * 128:(k + 1) * 128],
                                       in_=ps_t2[:, 0:128]).then_inc(t2e, 1)

    with nc.sbuf_tensor("b1s", [64, 2 * 32 * 512], BF16) as b1s, \
         nc.sbuf_tensor("wtile", [128, 2 * 512], BF16) as wtile, \
         nc.psum_tensor("ps_c", [128, 2 * 512], F32) as ps_c:
        with nc.Block() as b:
            @b.vector
            def _(vector):
                for k in range(32):
                    vector.tensor_scalar_mul(a1t[:, k * 128:(k + 1) * 128],
                                             a1t[:, k * 128:(k + 1) * 128],
                                             wvbc[0:64, k:k + 1])
        with nc.Block() as b:
            @b.sync
            def _(sync):
                for fch in range(8):
                    sl = fch % 2
                    if fch >= 2:
                        sync.wait_ge(cpe, fch - 1)
                    for k in range(32):
                        sync.dma_start(
                            out=b1s[:, sl * 32 * 512 + k * 512:sl * 32 * 512 + (k + 1) * 512],
                            in_=s_b1[k][:, fch * 512:(fch + 1) * 512]).then_inc(b1l[sl], 16)

            @b.tensor
            def _(tensor):
                for fch in range(8):
                    sl = fch % 2
                    tensor.wait_ge(b1l[sl], 16 * 32 * (fch // 2 + 1))
                    grp = fch
                    bank = grp % 2
                    if grp >= 2:
                        tensor.wait_ge(cev, grp - 1)
                    for k in range(32):
                        mm = tensor.matmul(ps_c[:, bank * 512:(bank + 1) * 512],
                                           a1t[:, k * 128:(k + 1) * 128],
                                           b1s[:, sl * 32 * 512 + k * 512:sl * 32 * 512 + (k + 1) * 512],
                                           start=(k == 0), stop=(k == 31))
                    mm.then_inc(cpe, 1)

            @b.scalar
            def _(scalar):
                for grp in range(8):
                    bank = grp % 2
                    scalar.wait_ge(cpe, grp + 1)
                    if grp >= 2:
                        scalar.wait_ge(cst[bank], 16 * (grp // 2))
                    scalar.activation(out=wtile[:, bank * 512:(bank + 1) * 512],
                                      in_=ps_c[:, bank * 512:(bank + 1) * 512],
                                      func=AF.Identity).then_inc(cev, 1)

            @b.gpsimd
            def _(gpsimd):
                for grp in range(8):
                    fch = grp
                    bank = grp % 2
                    gpsimd.wait_ge(cev, grp + 1)
                    gpsimd.dma_start(out=w1loc[:, fch * 512:(fch + 1) * 512],
                                     in_=wtile[:, bank * 512:(bank + 1) * 512]).then_inc(cst[bank], 16)

        with nc.Block() as b:
            @b.sync
            def _(sync):
                for hc in range(4):
                    sl = hc % 2
                    if hc >= 2:
                        sync.wait_ge(cpe, 8 + (hc - 1) * 2)
                    for k in range(32):
                        sync.dma_start(
                            out=b1s[:, sl * 32 * 512 + k * 128:sl * 32 * 512 + k * 128 + 128],
                            in_=s_a2t[k][:, hc * 128:(hc + 1) * 128]).then_inc(b1l[sl], 16)

            @b.tensor
            def _(tensor):
                for hc in range(4):
                    sl = hc % 2
                    tensor.wait_ge(b1l[sl], 16 * 32 * (4 + hc // 2 + 1))
                    for dg in range(2):
                        grp = 8 + hc * 2 + dg
                        bank = grp % 2
                        tensor.wait_ge(cev, grp - 1)
                        for k in range(32):
                            mm = tensor.matmul(ps_c[:, bank * 512:(bank + 1) * 512],
                                               b1s[:, sl * 32 * 512 + k * 128:sl * 32 * 512 + k * 128 + 128],
                                               b2buf[:, k * D + dg * 512:k * D + (dg + 1) * 512],
                                               start=(k == 0), stop=(k == 31))
                        mm.then_inc(cpe, 1)

            @b.scalar
            def _(scalar):
                for grp in range(8, 16):
                    bank = grp % 2
                    scalar.wait_ge(cpe, grp + 1)
                    scalar.wait_ge(cst[bank], 16 * (grp // 2))
                    scalar.activation(out=wtile[:, bank * 512:(bank + 1) * 512],
                                      in_=ps_c[:, bank * 512:(bank + 1) * 512],
                                      func=AF.Identity).then_inc(cev, 1)

            @b.gpsimd
            def _(gpsimd):
                for grp in range(8, 16):
                    gl = grp - 8
                    hc, dg = divmod(gl, 2)
                    bank = grp % 2
                    gpsimd.wait_ge(cev, grp + 1)
                    gpsimd.dma_start(out=w2loc[hc * 128:(hc + 1) * 128, dg * 512:(dg + 1) * 512],
                                     in_=wtile[:, bank * 512:(bank + 1) * 512]).then_inc(cst[bank], 16)

    cm_a1.__exit__(None, None, None)
    cm_b2.__exit__(None, None, None)

    ccs1 = sem(nc, "ccs1")
    ccs2 = sem(nc, "ccs2")
    with nc.Block() as b:
        @b.gpsimd
        def _(gpsimd):
            gpsimd.wait_ge(cst[0], 16 * 8)
            gpsimd.wait_ge(cst[1], 16 * 8)
            gpsimd.collective_compute(
                "AllGather", ALU.bypass,
                replica_groups=[list(range(N_CORES))],
                ins=[w1loc.opt()], outs=[w1s.opt()]).then_inc(ccs1)
            gpsimd.collective_compute(
                "AllGather", ALU.bypass,
                replica_groups=[list(range(N_CORES))],
                ins=[w2loc.opt()], outs=[w2s.opt()]).then_inc(ccs2)
            gpsimd.wait_ge(ccs1, 1)
            gpsimd.wait_ge(ccs2, 1)

    w1l = sem(nc, "w1l")
    xl = [sem(nc, "xl0"), sem(nc, "xl1")]
    fst = [sem(nc, "fst0"), sem(nc, "fst1")]
    with nc.sbuf_tensor("w1r", [128, ND * FF], BF16) as w1r, \
         nc.sbuf_tensor("w2r", [128, NFC * D], BF16) as w2r, \
         nc.sbuf_tensor("xnt", [128, 2 * ND * 512], BF16) as xnt, \
         nc.sbuf_tensor("hbuf", [128, NFC * 512], BF16) as hbuf, \
         nc.sbuf_tensor("gsc", [128, 3 * 512], F32) as gsc, \
         nc.sbuf_tensor("otile", [128, 2 * 512], F32) as otile, \
         nc.psum_tensor("ps_mf", [128, 2 * 512], F32) as ps_m:
        with nc.Block() as b:
            @b.sync
            def _(sync):
                for c in range(ND):
                    sync.dma_start(out=w1r[:, c * FF:(c + 1) * FF],
                                   in_=w1s[c * 128:(c + 1) * 128, :]).then_inc(w1l, 16)
                for hc in range(NFC):
                    sync.dma_start(out=w2r[:, hc * D:(hc + 1) * D],
                                   in_=w2s[hc * 128:(hc + 1) * 128, :]).then_inc(w1l, 16)
                sync.wait_ge(w1l, 16 * (ND + NFC))
        pe_n = 0
        sqe = sem(nc, "sqe")
        sge = sem(nc, "sge")
        fme = sem(nc, "fme")
        oev = sem(nc, "oev")
        with nc.Block() as b:
            evacs = []

            @b.sync
            def _(sync):
                for tc in range(4):
                    if tc >= 2:
                        sync.wait_ge(cpe, 16 + (tc - 2) * (NFC + ND) + NFC)
                    for c in range(ND):
                        sync.dma_start(
                            out=xnt[:, (tc % 2) * ND * 512 + c * 512:(tc % 2) * ND * 512 + (c + 1) * 512],
                            in_=xn[c * 128:(c + 1) * 128, tc * 512:(tc + 1) * 512]).then_inc(xl[tc % 2], 16)

            @b.tensor
            def _(tensor):
                nonlocal pe_n
                pe_n = 16
                nh = 0
                no = 0
                kinds = []
                for tc in range(4):
                    tensor.wait_ge(xl[tc % 2], 16 * ND * (tc // 2 + 1))
                    for m in range(NFC):
                        grp = tc * (NFC + ND) + m
                        bank = grp % 2
                        if grp >= 2:
                            pk, pi = kinds[grp - 2]
                            tensor.wait_ge(fme if pk == "h" else oev, pi + 1)
                        for c in range(ND):
                            mm = tensor.matmul(ps_m[:, bank * 512:(bank + 1) * 512],
                                               w1r[:, c * FF + m * 128:c * FF + (m + 1) * 128],
                                               xnt[:, (tc % 2) * ND * 512 + c * 512:(tc % 2) * ND * 512 + (c + 1) * 512],
                                               start=(c == 0), stop=(c == ND - 1))
                        pe_n += 1
                        mm.then_inc(cpe, 1)
                        evacs.append(("h", tc, m, pe_n, bank, nh))
                        kinds.append(("h", nh))
                        nh += 1
                    for dm in range(ND):
                        grp = tc * (NFC + ND) + NFC + dm
                        bank = grp % 2
                        pk, pi = kinds[grp - 2]
                        tensor.wait_ge(fme if pk == "h" else oev, pi + 1)
                        tensor.wait_ge(fme, (tc + 1) * NFC)
                        for hc in range(NFC):
                            mm = tensor.matmul(ps_m[:, bank * 512:(bank + 1) * 512],
                                               w2r[:, hc * D + dm * 128:hc * D + (dm + 1) * 128],
                                               hbuf[:, hc * 512:(hc + 1) * 512],
                                               start=(hc == 0), stop=(hc == NFC - 1))
                        pe_n += 1
                        mm.then_inc(cpe, 1)
                        evacs.append(("o", tc, dm, pe_n, bank, no))
                        kinds.append(("o", no))
                        no += 1

            @b.scalar
            def _(scalar):
                for (kind, tc, idx, pe_target, bank, ni) in evacs:
                    scalar.wait_ge(cpe, pe_target)
                    if kind == "h":
                        scalar.activation(out=gsc[:, 0:512],
                                          in_=ps_m[:, bank * 512:(bank + 1) * 512],
                                          func=AF.Square).then_inc(sqe, 1)
                        scalar.wait_ge(ap, 256 + 2 * ni + 2)
                        scalar.activation(out=gsc[:, 1024:1536],
                                          in_=gsc[:, 512:1024],
                                          func=AF.Sigmoid, scale=GEC).then_inc(sge, 1)
                    else:
                        os_ = (tc * ND + idx) % 2
                        nprior = (tc * ND + idx) // 2
                        if nprior > 0:
                            scalar.wait_ge(fst[os_], 16 * nprior)
                        scalar.activation(out=otile[:, os_ * 512:(os_ + 1) * 512],
                                          in_=ps_m[:, bank * 512:(bank + 1) * 512],
                                          func=AF.Identity).then_inc(oev, 1)

            @b.vector
            def _(vector):
                for (kind, tc, idx, pe_target, bank, ni) in evacs:
                    if kind != "h":
                        continue
                    vector.wait_ge(sqe, ni + 1)
                    vector.tensor_scalar(out=gsc[:, 512:1024], in0=gsc[:, 0:512],
                                         scalar1=GEB, scalar2=1.0,
                                         op0=ALU.mult, op1=ALU.add).then_inc(ap, 1)
                    vector.wait_ge(ap, 256 + 2 * ni + 1)
                    vector.tensor_mul(gsc[:, 512:1024], gsc[:, 512:1024],
                                      ps_m[:, bank * 512:(bank + 1) * 512]).then_inc(ap, 1)
                    vector.wait_ge(sge, ni + 1)
                    vector.tensor_mul(hbuf[:, idx * 512:(idx + 1) * 512],
                                      gsc[:, 1024:1536],
                                      ps_m[:, bank * 512:(bank + 1) * 512]).then_inc(fme, 1)

            @b.gpsimd
            def _(gpsimd):
                n_out = 0
                for (kind, tc, idx, pe_target, bank, ni) in evacs:
                    if kind != "o":
                        continue
                    os_ = (tc * ND + idx) % 2
                    gpsimd.wait_ge(oev, ni + 1)
                    gpsimd.dma_start(out=f1[idx * 128:(idx + 1) * 128, tc * 512:(tc + 1) * 512],
                                     in_=otile[:, os_ * 512:(os_ + 1) * 512]).then_inc(fst[os_], 16)
                    n_out += 1
                gpsimd.wait_ge(fst[0], 16 * ((n_out + 1) // 2))
                gpsimd.wait_ge(fst[1], 16 * (n_out // 2))




def build_l1ffn():
    nc = bass.Bass("TRN2", target_bir_lowering=False, debug=False)
    xn = nc.dram_tensor("xn", [D, L], BF16, kind="ExternalInput").ap()
    fc1A = nc.dram_tensor("fc1A", [256, D, 64], F32, kind="ExternalInput").ap()
    fc1B = nc.dram_tensor("fc1B", [256, 64, FF], F32, kind="ExternalInput").ap()
    fc2A = nc.dram_tensor("fc2A", [256, FF, 64], F32, kind="ExternalInput").ap()
    fc2B = nc.dram_tensor("fc2B", [256, 64, D], F32, kind="ExternalInput").ap()
    lg1 = nc.dram_tensor("lg1", [1, 256], F32, kind="ExternalInput").ap()
    lg2 = nc.dram_tensor("lg2", [1, 256], F32, kind="ExternalInput").ap()
    riota1 = nc.dram_tensor("riota1", [128, 1], F32, kind="ExternalInput").ap()
    riota2 = nc.dram_tensor("riota2", [128, 1], F32, kind="ExternalInput").ap()
    f1 = nc.dram_tensor("f1", [D, L], F32, kind="ExternalOutput").ap()
    emit_l1ffn(nc, xn, fc1A, fc1B, fc2A, fc2B, lg1, lg2, riota1, riota2, f1)
    return nc

def cev_before(evacs, ei):
    return 3 * ei


def n_h(evacs, ei):
    return sum(1 for e in evacs[:ei] if e[0] == "h")


def emit_l1attn(nc, xn, wqkvT, brow, ctab, stab, lepw, a1):


    SCALE = 0.125
    EBIAS = -30.0

    qfm = sb(nc, "qfm", [128, 8 * L], BF16)
    kfm = sb(nc, "kfm", [128, 8 * L], BF16)
    vfm = sb(nc, "vfm", [128, 8 * L], BF16)
    vbuf = sb(nc, "vbuf", [128, 16 * 1056], BF16)
    cost = sb(nc, "cost", [128, 16 * 32], F32)
    sint = sb(nc, "sint", [128, 16 * 32], F32)
    lw = sb(nc, "lw", [128, 32], F32)
    br = sb(nc, "br", [1, 3072], BF16)
    ones1 = sb(nc, "ones1", [1, 128], F32)
    ones1b = sb(nc, "ones1b", [1, 128], BF16)
    nb30 = sb(nc, "nb30", [128, 1], F32)
    ident = sb(nc, "ident", [128, 128], BF16)

    ld = sem(nc, "ld")
    wl = sem(nc, "wl")
    xl = [sem(nc, "xl0"), sem(nc, "xl1")]
    qpe = sem(nc, "qpe")
    cev = sem(nc, "cev")
    ape = sem(nc, "ape")
    ini = sem(nc, "ini")

    with nc.sbuf_tensor("wq", [128, 8 * 3072], BF16) as wq, \
         nc.sbuf_tensor("xs", [128, 128 * 8], BF16) as xs, \
         nc.sbuf_tensor("mn", [128, 1024], F32) as mnb, \
         nc.sbuf_tensor("ex", [128, 1024], F32) as exb, \
         nc.sbuf_tensor("rp", [128, 2048], BF16) as rpb, \
         nc.sbuf_tensor("vb", [128, 1024], BF16) as vbb, \
         nc.psum_tensor("ps_q", [128, 3072], F32) as ps_q, \
         nc.psum_tensor("ps_t", [128, 1024], BF16) as ps_t:
        with nc.Block() as b:
            @b.gpsimd
            def _(gpsimd):
                for c in range(ND):
                    for hh in range(2):
                        gpsimd.dma_start(
                            out=wq[:, c * 3072 + hh * 1536:c * 3072 + (hh + 1) * 1536],
                            in_=wqkvT[c * 128:(c + 1) * 128, hh * 1536:(hh + 1) * 1536],
                        ).then_inc(wl, 16)
                gpsimd.dma_start(out=br[:], in_=brow).then_inc(wl, 16)
                gpsimd.memset(ident[:], 0.0).then_inc(ini, 1)
                gpsimd.wait_ge(ini, 1)
                gpsimd.affine_select(out=ident[:], in_=ident[:],
                                     compare_op=ALU.not_equal, fill=1.0, base=0,
                                     pattern=[[-1, 128]], channel_multiplier=1)

            @b.sync
            def _(sync):
                sync.dma_start(out=cost[:], in_=ctab).then_inc(ld, 16)
                sync.dma_start(out=sint[:], in_=stab).then_inc(ld, 16)
                sync.dma_start(out=lw[:], in_=lepw).then_inc(ld, 16)
                sync.wait_ge(ld, 48)

            @b.vector
            def _(vector):
                vector.memset(ones1[:], 1.0)
                vector.memset(ones1b[:], 1.0)
                vector.memset(nb30[:], -30.0)

        nDVE = [0]
        nACT = [0]
        nPE = [0]
        with nc.Block() as b:
            @b.sync
            def _(sync):
                for tt in range(NT):
                    if tt >= 1:
                        sync.wait_ge(qpe, (tt - 1) * 9 + 6)
                    for c in range(ND):
                        sync.dma_start(
                            out=xs[:, c * 128:(c + 1) * 128],
                            in_=xn[c * 128:(c + 1) * 128, tt * 128:(tt + 1) * 128],
                        ).then_inc(xl[0], 16)

            @b.tensor
            def _(tensor):
                for tt in range(NT):
                    tensor.wait_ge(xl[0], 16 * ND * (tt + 1))
                    if tt == 0:
                        tensor.wait_ge(wl, 16 * 17)
                    if tt >= 1:
                        tensor.wait_ge(ape, (tt - 1) * 16 + 10)
                        tensor.wait_ge(cev, (tt - 1) * 29 + 5)
                    for fo in range(6):
                        for c in range(ND):
                            tensor.matmul(ps_q[:, fo * 512:(fo + 1) * 512],
                                          xs[:, c * 128:(c + 1) * 128],
                                          wq[:, c * 3072 + fo * 512:c * 3072 + (fo + 1) * 512],
                                          start=(c == 0), stop=False)
                        mm = tensor.matmul(ps_q[:, fo * 512:(fo + 1) * 512],
                                           ones1b[:], br[:, fo * 512:(fo + 1) * 512],
                                           start=False, stop=True)
                        mm.then_inc(qpe, 1)
                    tensor.wait_ge(ape, tt * 16 + 8)
                    if tt >= 1:
                        tensor.wait_ge(cev, tt * 29)
                    for ct in range(8):
                        mm = tensor.transpose(ps_t[:, ct * 128:(ct + 1) * 128],
                                              rpb[:, ct * 128:(ct + 1) * 128], ident[:])
                    mm.then_inc(qpe, 1)
                    tensor.wait_ge(ape, tt * 16 + 16)
                    tensor.wait_ge(cev, tt * 29 + 13)
                    for ct in range(8):
                        mm = tensor.transpose(ps_t[:, ct * 128:(ct + 1) * 128],
                                              rpb[:, 1024 + ct * 128:1024 + (ct + 1) * 128],
                                              ident[:])
                    mm.then_inc(qpe, 1)
                    tensor.wait_ge(cev, tt * 29 + 21)
                    for ct in range(8):
                        mm = tensor.transpose(ps_t[:, ct * 128:(ct + 1) * 128],
                                              vbb[:, ct * 128:(ct + 1) * 128], ident[:])
                    mm.then_inc(qpe, 1)

            @b.vector
            def _(vector):
                for tt in range(NT):
                    base = tt * 16
                    nin = 0
                    for g in range(2):
                        gs = slice(g * 1024, (g + 1) * 1024)
                        vector.wait_ge(qpe, tt * 9 + 2 + 2 * g)
                        if g == 0 and tt > 0:
                            pass
                        vector.tensor_scalar_min(exb[:], ps_q[:, gs], 0.0).then_inc(ape, 1)
                        nin += 1
                        vector.wait_ge(cev, tt * 29 + g + 1)
                        vector.scalar_tensor_tensor(out=exb[:], in0=ps_q[:, gs],
                                                    scalar=0.0, in1=vbb[:],
                                                    op0=ALU.max, op1=ALU.add).then_inc(ape, 1)
                        nin += 1
                        ep3 = exb[:].rearrange("p (h d two) -> p h two d", two=2, d=32)
                        x1 = ep3[:, :, 0, :]
                        x2 = ep3[:, :, 1, :]
                        cosb = cost[:, tt * 32:(tt + 1) * 32].unsqueeze(1).to_broadcast([128, 16, 32])
                        sinb = sint[:, tt * 32:(tt + 1) * 32].unsqueeze(1).to_broadcast([128, 16, 32])
                        rp3 = rpb[:, gs].rearrange("p (h d two) -> p h two d", two=2, d=32)
                        sc1 = mnb[:, 0:512].rearrange("p (h d) -> p h d", d=32)
                        sc2 = mnb[:, 512:1024].rearrange("p (h d) -> p h d", d=32)
                        for (dst, m1, t1, m2, t2, op) in (
                                (0, x1, cosb, x2, sinb, ALU.subtract),
                                (1, x1, sinb, x2, cosb, ALU.add)):
                            vector.wait_ge(ape, base + nin)
                            vector.tensor_mul(sc1, m1, t1).then_inc(ape, 1)
                            nin += 1
                            vector.wait_ge(ape, base + nin)
                            vector.tensor_mul(sc2, m2, t2).then_inc(ape, 1)
                            nin += 1
                            vector.wait_ge(ape, base + nin)
                            vector.tensor_tensor(rp3[:, :, dst, :], sc1, sc2, op).then_inc(ape, 1)
                            nin += 1
                            vector.wait_ge(ape, base + nin)

            @b.scalar
            def _(scalar):
                for tt in range(NT):
                    base = tt * 29
                    for g in range(2):
                        scalar.wait_ge(ape, tt * 16 + 1 + g * 8)
                        scalar.activation(out=vbb[:], in_=exb[:], func=AF.Exp).then_inc(cev, 1)
                    scalar.wait_ge(qpe, tt * 9 + 6)
                    scalar.wait_ge(ape, tt * 16 + 10)
                    scalar.activation(out=vbb[:], in_=ps_q[:, 2048:3072],
                                      func=AF.Identity).then_inc(cev, 1)
                    scalar.activation(out=vbuf[:, tt * 1056:tt * 1056 + 1056].rearrange(
                                          "p (h e) -> p h e", e=66)[:, :, 0:32],
                                      in_=ps_q[:, 2048:3072].rearrange("p (h e) -> p h e", e=64)[:, :, 0:32],
                                      func=AF.Identity).then_inc(cev, 1)
                    scalar.activation(out=vbuf[:, tt * 1056:tt * 1056 + 1056].rearrange(
                                          "p (h e) -> p h e", e=66)[:, :, 32:64],
                                      in_=ps_q[:, 2048:3072].rearrange("p (h e) -> p h e", e=64)[:, :, 32:64],
                                      func=AF.Identity).then_inc(cev, 1)
                    scalar.wait_ge(qpe, tt * 9 + 7)
                    for ct in range(8):
                        scalar.activation(out=qfm[:, ct * L + tt * 128:ct * L + (tt + 1) * 128],
                                          in_=ps_t[:, ct * 128:(ct + 1) * 128],
                                          func=AF.Identity).then_inc(cev, 1)
                    scalar.wait_ge(qpe, tt * 9 + 8)
                    for ct in range(8):
                        scalar.activation(out=kfm[:, ct * L + tt * 128:ct * L + (tt + 1) * 128],
                                          in_=ps_t[:, ct * 128:(ct + 1) * 128],
                                          func=AF.Identity).then_inc(cev, 1)
                    scalar.wait_ge(qpe, tt * 9 + 9)
                    for ct in range(8):
                        scalar.activation(out=vfm[:, ct * L + tt * 128:ct * L + (tt + 1) * 128],
                                          in_=ps_t[:, ct * 128:(ct + 1) * 128],
                                          func=AF.Identity).then_inc(cev, 1)
        with nc.Block() as b:
            @b.vector
            def _(vector):
                for tt in range(NT):
                    vector.memset(
                        vbuf[:, tt * 1056:tt * 1056 + 1056].rearrange(
                            "p (h e) -> p h e", e=66)[:, :, 64:66], 1.0)

    with nc.sbuf_tensor("lptmp", [128, L], BF16) as lptmp:
      with nc.Block() as b:
        @b.vector
        def _(vector):
            nn = 0
            for ct in range(8):
                sl = slice(ct * L, (ct + 1) * L)
                vector.tensor_scalar_mul(lptmp[:], vfm[:, sl],
                                         lw[:, 8 + ct:8 + ct + 1]).then_inc(ape, 1)
                nn += 1
                vector.wait_ge(ape, 256 + nn)
                vector.scalar_tensor_tensor(out=lptmp[:, 1:L],
                                            in0=vfm[:, ct * L:(ct + 1) * L - 1],
                                            scalar=lw[:, ct:ct + 1],
                                            in1=lptmp[:, 1:L],
                                            op0=ALU.mult, op1=ALU.add).then_inc(ape, 1)
                nn += 1
                vector.wait_ge(ape, 256 + nn)
                vector.scalar_tensor_tensor(out=lptmp[:, 0:L - 1],
                                            in0=vfm[:, ct * L + 1:(ct + 1) * L],
                                            scalar=lw[:, 16 + ct:16 + ct + 1],
                                            in1=lptmp[:, 0:L - 1],
                                            op0=ALU.mult, op1=ALU.add).then_inc(ape, 1)
                nn += 1
                vector.wait_ge(ape, 256 + nn)
                vector.tensor_scalar_add(lptmp[:], lptmp[:],
                                         lw[:, 24 + ct:24 + ct + 1]).then_inc(ape, 1)
                nn += 1
                vector.wait_ge(ape, 256 + nn)
                vector.tensor_copy(out=vfm[:, sl], in_=lptmp[:]).then_inc(ape, 1)
                nn += 1
                vector.wait_ge(ape, 256 + nn)

    avp = sem(nc, "avp")
    nrc = sem(nc, "nrc")
    bcp = sem(nc, "bcp")
    rbe = sem(nc, "rbe")
    nrm = sem(nc, "nrm")
    fst = [sem(nc, "fst0"), sem(nc, "fst1")]
    with nc.sbuf_tensor("atb", [128, 2 * 16 * 512], BF16) as atb, \
         nc.sbuf_tensor("rc", [1, 2 * 512], F32) as rc, \
         nc.sbuf_tensor("rb", [64, 2 * 512], F32) as rb, \
         nc.sbuf_tensor("ob", [128, 2 * 512], F32) as ob, \
         nc.psum_tensor("ps_s", [128, 2 * 512], F32) as ps_s, \
         nc.psum_tensor("ps_a", [128, 2 * 512], F32) as ps_a, \
         nc.psum_tensor("ps_b2", [64, 2 * 512], F32) as ps_b2:
        with nc.Block() as b:
            pairs = [(h, tq) for h in range(H1) for tq in range(4)]

            @b.tensor
            def _(tensor):
                for p, (h, tq) in enumerate(pairs):
                    hb = (h % 2) * 64
                    ht = h // 2
                    asl = p % 2
                    for kc in range(16):
                        bank = kc % 2
                        gexp = 16 * p + kc
                        if gexp >= 2:
                            tensor.wait_ge(cev, 464 + gexp - 1)
                        tensor.matmul(ps_s[:, bank * 512:(bank + 1) * 512],
                                      kfm[hb:hb + 64, ht * L + kc * 128:ht * L + (kc + 1) * 128],
                                      qfm[hb:hb + 64, ht * L + tq * 512:ht * L + (tq + 1) * 512],
                                      start=True, stop=True).then_inc(qpe, 1)
                        if kc >= 1:
                            tensor.wait_ge(cev, 464 + 16 * p + kc)
                            if kc == 1 and p >= 2:
                                tensor.wait_ge(nrm, p - 1)
                            mm = tensor.matmul(ps_a[0:65, asl * 512:(asl + 1) * 512],
                                               vbuf[:, (kc - 1) * 1056 + h * 66:(kc - 1) * 1056 + h * 66 + 65],
                                               atb[:, asl * 16 * 512 + (kc - 1) * 512:asl * 16 * 512 + kc * 512],
                                               start=(kc == 1), stop=False)
                    tensor.wait_ge(cev, 464 + 16 * p + 16)
                    mm = tensor.matmul(ps_a[0:65, asl * 512:(asl + 1) * 512],
                                       vbuf[:, 15 * 1056 + h * 66:15 * 1056 + h * 66 + 65],
                                       atb[:, asl * 16 * 512 + 15 * 512:asl * 16 * 512 + 16 * 512],
                                       start=False, stop=True)
                    mm.then_inc(avp, 1)
                    tensor.wait_ge(nrc, 2 * p + 1)
                    if p >= 2:
                        tensor.wait_ge(rbe, p - 1)
                    tensor.matmul(ps_b2[:, asl * 512:(asl + 1) * 512],
                                  ones1[:, 0:64], rc[:, asl * 512:(asl + 1) * 512],
                                  start=True, stop=True).then_inc(bcp, 1)

            @b.scalar
            def _(scalar):
                for p, (h, tq) in enumerate(pairs):
                    asl = p % 2
                    for kc in range(16):
                        gexp = 16 * p + kc
                        scalar.wait_ge(qpe, 144 + gexp + 1)
                        if kc == 0 and p >= 2:
                            scalar.wait_ge(avp, p - 1)
                        scalar.activation(out=atb[:, asl * 16 * 512 + kc * 512:asl * 16 * 512 + (kc + 1) * 512],
                                          in_=ps_s[:, (kc % 2) * 512:(kc % 2 + 1) * 512],
                                          func=AF.Exp, bias=nb30[:], scale=SCALE).then_inc(cev, 1)
                    scalar.wait_ge(bcp, p + 1)
                    if p >= 2:
                        scalar.wait_ge(nrm, p - 1)
                    scalar.activation(out=rb[:, asl * 512:(asl + 1) * 512],
                                      in_=ps_b2[0:64, asl * 512:(asl + 1) * 512],
                                      func=AF.Identity).then_inc(rbe, 1)

            @b.vector
            def _(vector):
                for p, (h, tq) in enumerate(pairs):
                    asl = p % 2
                    hb = (h % 2) * 64
                    vector.wait_ge(avp, p + 1)
                    if p >= 2:
                        vector.wait_ge(bcp, p - 1)
                    vector.reciprocal(out=rc[:, asl * 512:(asl + 1) * 512],
                                      in_=ps_a[64:65, asl * 512:(asl + 1) * 512]).then_inc(nrc, 1)
                    vector.wait_ge(rbe, p + 1)
                    if p >= 2:
                        vector.wait_ge(fst[asl], 16 * (p // 2))
                    vector.tensor_mul(ob[hb:hb + 64, asl * 512:(asl + 1) * 512],
                                      ps_a[0:64, asl * 512:(asl + 1) * 512],
                                      rb[:, asl * 512:(asl + 1) * 512]).then_inc(nrc, 1)
                    vector.wait_ge(nrc, 2 * p + 2)
                    vector.tensor_add(ob[hb:hb + 64, asl * 512:(asl + 1) * 512],
                                      ob[hb:hb + 64, asl * 512:(asl + 1) * 512],
                                      vfm[hb:hb + 64, (h // 2) * L + tq * 512:(h // 2) * L + (tq + 1) * 512]
                                      ).then_inc(nrm, 1)

            @b.gpsimd
            def _(gpsimd):
                for p, (h, tq) in enumerate(pairs):
                    asl = p % 2
                    gpsimd.wait_ge(nrm, p + 1)
                    hb2 = (h % 2) * 64
                    gpsimd.dma_start(out=a1[h * 64:(h + 1) * 64, tq * 512:(tq + 1) * 512],
                                     in_=ob[hb2:hb2 + 64, asl * 512:(asl + 1) * 512]).then_inc(fst[asl], 16)
                gpsimd.wait_ge(fst[0], 16 * 32)
                gpsimd.wait_ge(fst[1], 16 * 32)


SCALE = float((64 + 32) ** (-0.5))
GSW = 1056




def build_l1attn():
    nc = bass.Bass("TRN2", target_bir_lowering=False, debug=False)
    xn = nc.dram_tensor("xn", [D, L], BF16, kind="ExternalInput").ap()
    wqkvT = nc.dram_tensor("wqkvT", [D, 3072], F32, kind="ExternalInput").ap()
    brow = nc.dram_tensor("brow", [1, 3072], F32, kind="ExternalInput").ap()
    ctab = nc.dram_tensor("ctab", [128, 16 * 32], F32, kind="ExternalInput").ap()
    stab = nc.dram_tensor("stab", [128, 16 * 32], F32, kind="ExternalInput").ap()
    lepw = nc.dram_tensor("lepw", [128, 4 * 8], F32, kind="ExternalInput").ap()
    a1 = nc.dram_tensor("a1", [D, L], F32, kind="ExternalOutput").ap()
    emit_l1attn(nc, xn, wqkvT, brow, ctab, stab, lepw, a1)
    return nc

def wc_of(tt):
    need = tt * 128 + 64
    return min(2048, 512 * ((need + 511) // 512))


WSHAPES = [("wdkv", (D, 256)), ("wuk", (256, 512)), ("wuv", (256, 512)),
           ("wdq", (D, 512)), ("wuq", (512, 512)), ("wqr", (512, 256)),
           ("wkr", (D, 32)), ("wout", (512, D)), ("iwq", (D, 256)),
           ("iwk", (D, 64))]


def emit_l0attn(nc, xn, win, iwrow, tabs, a0):
    gs = nc.dram_tensor("gs", [L, GSW], BF16, kind="Internal").ap()

    kfm = sb(nc, "kfm", [128, 4 * L], BF16)
    vsh = sb(nc, "vsh", [128, 17 * 512], BF16)
    vf0 = sb(nc, "vf0", [128, 512], BF16)
    qcfm = sb(nc, "qcfm", [128, 4 * L], BF16)
    qctm = sb(nc, "qctm", [128, 16 * 512], BF16)
    qtfm = sb(nc, "qtfm", [128, 2 * L], BF16)
    qotm = sb(nc, "qotm", [128, 16 * 256], BF16)
    qofm = sb(nc, "qofm", [128, 2 * 128], BF16)
    qifm = sb(nc, "qifm", [128, 2 * L], BF16)
    kir = sb(nc, "kir", [128, L], BF16)
    ktr = sb(nc, "ktr", [128, L], BF16)
    kor = sb(nc, "kor", [128, 96], BF16)
    wo = sb(nc, "wo", [128, 4 * D], BF16)
    tcs = sb(nc, "tcs", [128, 512], BF16)
    tss = sb(nc, "tss", [128, 512], BF16)
    trp = sb(nc, "trp", [128, 2048], F32)
    isb = sb(nc, "isb", [128, 2048], F32)
    rbuf = sb(nc, "rbuf", [128, 2048], F32)
    vals = sb(nc, "vals", [128, 32], F32)
    midx = sb(nc, "midx", [128, 32], U32)
    mv8 = sb(nc, "mv8", [128, 8], F32)
    wcol = sb(nc, "wcol", [128, 4], F32)
    gb = sb(nc, "gb", [128, 8 * 544], BF16)
    dsc = sb(nc, "dsc", [128, 256], F32)
    kap = sb(nc, "kap", [128, 32], F32)
    pb = sb(nc, "pb", [128, 32], F32)
    prod = sb(nc, "prod", [128, 512], F32)
    AE = sb(nc, "AE", [128, 8 * 32], BF16)
    scb = sb(nc, "scb", [128, 2 * 352], F32)
    ab = sb(nc, "ab", [128, 2 * 352], BF16)
    atb = sb(nc, "atb", [128, 2 * 384], BF16)
    eac = sb(nc, "eac", [128, 512], F32)
    att = sb(nc, "att", [128, 512], BF16)
    afm = sb(nc, "afm", [128, 4 * 128], BF16)
    otb = sb(nc, "otb", [128, 2 * 128], F32)
    mband = sb(nc, "mband", [128, 192], F32)
    mband0 = sb(nc, "mband0", [128, 192], F32)
    mf0 = sb(nc, "mf0", [128, 96], F32)
    mrow = sb(nc, "mrow", [128, 2], F32)
    srw8 = sb(nc, "srw8", [128, 8], F32)
    rcp8 = sb(nc, "rcp8", [128, 8], F32)
    onesc = sb(nc, "onesc", [128, 1], F32)
    ones1 = sb(nc, "ones1", [1, 128], F32)
    iwsb = sb(nc, "iwsb", [1, 4], F32)
    ident = sb(nc, "ident", [128, 128], BF16)

    ld = sem(nc, "ld")
    wl = sem(nc, "wl")
    pe = sem(nc, "pe")
    ae = sem(nc, "ae")
    ve = sem(nc, "ve")
    ge = sem(nc, "ge")
    g2 = [sem(nc, "gt0"), sem(nc, "gt1")]
    gw = sem(nc, "gw")
    gws = [sem(nc, "gwa"), sem(nc, "gwb")]
    stm = [sem(nc, "st0"), sem(nc, "st1")]

    C = {"pe": 0, "ae": 0, "ve": 0, "ge": 0, "gw": 0, "st0": 0, "st1": 0,
         "g20": 0, "g21": 0, "wl": 0, "ld": 0, "gwa": 0, "gwb": 0}

    with nc.Block() as b:
        @b.gpsimd
        def _(gpsimd):
            gpsimd.memset(ident[:], 0.0).then_inc(ge, 1)
            gpsimd.memset(mband[:], 0.0).then_inc(ge, 1)
            gpsimd.memset(mf0[:], 0.0).then_inc(ge, 1)
            gpsimd.memset(vsh[0:64, 0:512], 0.0).then_inc(ge, 1)
            gpsimd.wait_ge(ge, 4)
            gpsimd.affine_select(out=ident[:], in_=ident[:], compare_op=ALU.not_equal,
                                 fill=1.0, base=0, pattern=[[-1, 128]],
                                 channel_multiplier=1).then_inc(ge, 1)
            gpsimd.wait_ge(ge, 5)
            gpsimd.affine_select(out=mband[:], in_=mband[:], compare_op=ALU.is_ge,
                                 fill=NEG, base=-1, pattern=[[1, 192]],
                                 channel_multiplier=-1).then_inc(ge, 1)
            gpsimd.wait_ge(ge, 6)
            gpsimd.affine_select(out=mband[:], in_=mband[:], compare_op=ALU.is_ge,
                                 fill=NEG, base=64, pattern=[[-1, 192]],
                                 channel_multiplier=1).then_inc(ge, 1)
            gpsimd.wait_ge(ge, 7)
            gpsimd.affine_select(out=mband0[:], in_=mband[:], compare_op=ALU.is_ge,
                                 fill=NEG, base=-64, pattern=[[0, 192]],
                                 channel_multiplier=1).then_inc(ge, 1)
            gpsimd.wait_ge(ge, 8)
            gpsimd.affine_select(out=mf0[:], in_=mf0[:], compare_op=ALU.is_ge,
                                 fill=NEG, base=-1, pattern=[[1, 96]],
                                 channel_multiplier=-1).then_inc(ge, 1)
            gpsimd.wait_ge(ge, 9)
            gpsimd.memset(mf0[0:64, :], 0.0).then_inc(ge, 1)
        C["ge"] = 10

        @b.sync
        def _(sync):
            for nm, dst in [("csl", tcs), ("ssl", tss), ("ramp", trp)]:
                sync.dma_start(out=dst[:], in_=tabs[nm]).then_inc(ld, 16)
                C["ld"] += 16
            sync.dma_start(out=iwsb[:], in_=iwrow).then_inc(ld, 16)
            C["ld"] += 16
            sync.wait_ge(ld, C["ld"])

        @b.vector
        def _(vector):
            vector.memset(onesc[:], 1.0)
            vector.memset(ones1[:], 1.0)

    import contextlib
    isb_r = None
    _s1 = contextlib.ExitStack()
    qrtm = _s1.enter_context(nc.sbuf_tensor("qrtm", [128, 16 * 256], BF16))
    krtm = _s1.enter_context(nc.sbuf_tensor("krtm", [128, 16 * 32], BF16))
    cqfm = _s1.enter_context(nc.sbuf_tensor("cqfm", [128, 4 * L], BF16))
    ckvfm = _s1.enter_context(nc.sbuf_tensor("ckvfm", [128, 2 * L], BF16))
    musq = qrtm
    muk = krtm

    def load_w(gpsimd_or_sync, wname, wbuf, wcols, Kc):
        for ch in range(Kc):
            gpsimd_or_sync.dma_start(out=wbuf[:, ch * wcols:(ch + 1) * wcols],
                                     in_=win[wname][ch * 128:(ch + 1) * 128, :]
                                     ).then_inc(wl, 16)
            C["wl"] += 16

    def fm_proj(wname, Kc, Mtot, wcols, rhs_sb, rhs_stride, out_sb, out_stride,
                wbuf, xns=None):
        pe0 = C["pe"]
        ae0 = C["ae"]
        ld0 = C["ld"]
        nm_ = max(1, Mtot // 128)
        mw = min(128, Mtot)
        with nc.Block() as blk:
            @blk.gpsimd
            def _(gpsimd):
                load_w(gpsimd, wname, wbuf, wcols, Kc)
                gpsimd.wait_ge(wl, C["wl"])

            if rhs_sb is None:
                @blk.sync
                def _(sync):
                    for n in range(4):
                        if n >= 1:
                            sync.wait_ge(pe, pe0 + n * nm_)
                        for cch in range(Kc):
                            sync.dma_start(
                                out=xns[:, cch * 512:(cch + 1) * 512],
                                in_=xn[cch * 128:(cch + 1) * 128, n * 512:(n + 1) * 512]
                                ).then_inc(ld, 16)
                            C["ld"] += 16

            @blk.tensor
            def _(tensor):
                g = 0
                for n in range(4):
                    if rhs_sb is None:
                        tensor.wait_ge(ld, ld0 + 16 * Kc * (n + 1))
                    if n == 0:
                        tensor.wait_ge(wl, C["wl"])
                    for m in range(nm_):
                        slot = g % 2
                        if g >= 2:
                            tensor.wait_ge(ae, ae0 + g - 1)
                        for cch in range(Kc):
                            rhs = (xns[:, cch * 512:(cch + 1) * 512]
                                   if rhs_sb is None else
                                   rhs_sb[:, cch * rhs_stride + n * 512:cch * rhs_stride + (n + 1) * 512])
                            mm = tensor.matmul(
                                ps_p[0:mw, slot * 512:(slot + 1) * 512],
                                wbuf[:, cch * wcols + m * mw:cch * wcols + m * mw + mw],
                                rhs, start=(cch == 0), stop=(cch == Kc - 1))
                        g += 1
                        mm.then_inc(pe, 1)

            @blk.scalar
            def _(scalar):
                g = 0
                for n in range(4):
                    for m in range(nm_):
                        slot = g % 2
                        scalar.wait_ge(pe, pe0 + g + 1)
                        scalar.activation(
                            out=out_sb[0:mw, m * out_stride + n * 512:m * out_stride + (n + 1) * 512],
                            in_=ps_p[0:mw, slot * 512:(slot + 1) * 512],
                            func=AF.Identity).then_inc(ae, 1)
                        g += 1
        C["pe"] += nm_ * 4
        C["ae"] += nm_ * 4

    def tm_proj(wname, Kc, Mtot, wcols, lhs_sb, lhs_stride, out_sb, wbuf,
                xns=None, skip_wload=False):
        pe0 = C["pe"]
        ae0 = C["ae"]
        ld0 = C["ld"]
        with nc.Block() as blk:
            @blk.gpsimd
            def _(gpsimd):
                if not skip_wload:
                    load_w(gpsimd, wname, wbuf, wcols, Kc)
                gpsimd.wait_ge(wl, C["wl"])

            if lhs_sb is None:
                @blk.sync
                def _(sync):
                    for tt in range(NT):
                        if tt >= 1:
                            sync.wait_ge(pe, pe0 + tt)
                        for cch in range(Kc):
                            sync.dma_start(
                                out=xns[:, cch * 128:(cch + 1) * 128],
                                in_=xn[cch * 128:(cch + 1) * 128, tt * 128:(tt + 1) * 128]
                                ).then_inc(ld, 16)
                            C["ld"] += 16

            @blk.tensor
            def _(tensor):
                for tt in range(NT):
                    slot = tt % 2
                    if lhs_sb is None:
                        tensor.wait_ge(ld, ld0 + 16 * Kc * (tt + 1))
                    if tt == 0:
                        tensor.wait_ge(wl, C["wl"])
                    if tt >= 2:
                        tensor.wait_ge(ae, ae0 + tt - 1)
                    for cch in range(Kc):
                        lhs = (xns[:, cch * 128:(cch + 1) * 128]
                               if lhs_sb is None else
                               lhs_sb[:, cch * lhs_stride + tt * 128:cch * lhs_stride + (tt + 1) * 128])
                        mm = tensor.matmul(
                            ps_p[:, slot * 512:slot * 512 + Mtot], lhs,
                            wbuf[:, cch * wcols:(cch + 1) * wcols],
                            start=(cch == 0), stop=(cch == Kc - 1))
                    mm.then_inc(pe, 1)

            @blk.scalar
            def _(scalar):
                for tt in range(NT):
                    slot = tt % 2
                    scalar.wait_ge(pe, pe0 + tt + 1)
                    scalar.activation(out=out_sb[:, tt * Mtot:(tt + 1) * Mtot],
                                      in_=ps_p[:, slot * 512:slot * 512 + Mtot],
                                      func=AF.Identity).then_inc(ae, 1)
        C["pe"] += NT
        C["ae"] += NT

    with nc.psum_tensor("ps_p", [128, 2 * 512], F32) as ps_p, \
         nc.psum_tensor("ps_ta", [128, 512], BF16) as ps_t2, \
         nc.psum_tensor("ps_w4", [128, 4], F32) as ps_w4:
        with nc.sbuf_tensor("wbuf", [128, 4096], BF16) as wbuf, \
             nc.sbuf_tensor("xns", [128, 4096], BF16) as xns:
            fm_proj("wdkv", 8, 256, 256, None, 0, ckvfm, L, wbuf, xns)
            fm_proj("wdq", 8, 512, 512, None, 0, cqfm, L, wbuf, xns)
            fm_proj("iwq", 8, 256, 256, None, 0, qifm, L, wbuf, xns)
            fm_proj("iwk", 8, 64, 64, None, 0, kir, 0, wbuf, xns)
            tm_proj("wkr", 8, 32, 32, None, 0, krtm, wbuf, xns)
            with nc.Block() as blk:
                @blk.gpsimd
                def _(gpsimd):
                    load_w(gpsimd, "wout", wo, D, 4)
                    gpsimd.wait_ge(wl, C["wl"])
        with nc.sbuf_tensor("wbuf2", [128, 4096], BF16) as wbuf, \
             nc.sbuf_tensor("ktb", [128, 2 * 512], BF16) as ktb:
            fm_proj("wuk", 2, 512, 512, ckvfm, L, kfm, L, wbuf)
            pe0 = C["pe"]
            ae0 = C["ae"]
            gwk = {0: C["gwa"], 1: C["gwb"]}
            with nc.Block() as blk:
                @blk.tensor
                def _(tensor):
                    for tt in range(NT):
                        slot = tt % 2
                        if tt >= 2:
                            tensor.wait_ge(gws[slot], gwk[slot] + 16 * (tt // 2))
                        for cch in range(2):
                            mm = tensor.matmul(
                                ps_p[:, slot * 512:(slot + 1) * 512],
                                ckvfm[:, cch * L + tt * 128:cch * L + (tt + 1) * 128],
                                wbuf[:, cch * 512:(cch + 1) * 512],
                                start=(cch == 0), stop=(cch == 1))
                        mm.then_inc(pe, 1)

                @blk.scalar
                def _(scalar):
                    for tt in range(NT):
                        slot = tt % 2
                        scalar.wait_ge(pe, pe0 + tt + 1)
                        scalar.activation(out=ktb[:, slot * 512:(slot + 1) * 512],
                                          in_=ps_p[:, slot * 512:(slot + 1) * 512],
                                          func=AF.Identity).then_inc(ae, 1)

                @blk.gpsimd
                def _(gpsimd):
                    for tt in range(NT):
                        slot = tt % 2
                        gpsimd.wait_ge(ae, ae0 + tt + 1)
                        gpsimd.dma_start(out=gs[tt * 128:(tt + 1) * 128, 0:512],
                                         in_=ktb[:, slot * 512:(slot + 1) * 512]
                                         ).then_inc(gws[slot], 16)
            C["pe"] += NT
            C["ae"] += NT
            C["gwa"] += 16 * 8
            C["gwb"] += 16 * 8
            pe0 = C["pe"]
            ae0 = C["ae"]
            gwk = {0: C["gwa"], 1: C["gwb"]}
            with nc.Block() as blk:
                @blk.gpsimd
                def _(gpsimd):
                    load_w(gpsimd, "wuv", wbuf, 512, 2)
                    gpsimd.wait_ge(wl, C["wl"])
                    for tt in range(NT):
                        slot = tt % 2
                        gpsimd.wait_ge(ae, ae0 + tt + 1)
                        gpsimd.dma_start(out=gs[tt * 128:(tt + 1) * 128, 544:1056],
                                         in_=ktb[:, slot * 512:(slot + 1) * 512]
                                         ).then_inc(gws[slot], 16)

                @blk.tensor
                def _(tensor):
                    for tt in range(NT):
                        slot = tt % 2
                        if tt == 0:
                            tensor.wait_ge(wl, C["wl"])
                        if tt >= 2:
                            tensor.wait_ge(gws[slot], gwk[slot] + 16 * (tt // 2))
                        for cch in range(2):
                            mm = tensor.matmul(
                                ps_p[:, slot * 512:(slot + 1) * 512],
                                ckvfm[:, cch * L + tt * 128:cch * L + (tt + 1) * 128],
                                wbuf[:, cch * 512:(cch + 1) * 512],
                                start=(cch == 0), stop=(cch == 1))
                        mm.then_inc(pe, 1)

                @blk.scalar
                def _(scalar):
                    scalar.wait_ge(gws[0], gwk[0])
                    scalar.wait_ge(gws[1], gwk[1])
                    for tt in range(NT):
                        slot = tt % 2
                        scalar.wait_ge(pe, pe0 + tt + 1)
                        scalar.activation(out=ktb[:, slot * 512:(slot + 1) * 512],
                                          in_=ps_p[:, slot * 512:(slot + 1) * 512],
                                          func=AF.Identity).then_inc(ae, 1)
            C["pe"] += NT
            C["ae"] += NT
            C["gwa"] += 16 * 8
            C["gwb"] += 16 * 8
            with nc.Block() as blk:
                @blk.gpsimd
                def _(gpsimd):
                    gpsimd.wait_ge(gws[0], C["gwa"])
                    gpsimd.wait_ge(gws[1], C["gwb"])
                    for g in range(17):
                        p0 = g * 128 - 64
                        lo = max(p0, 0)
                        hi = min(p0 + 128, L)
                        gpsimd.dma_start(
                            out=vsh[lo - p0:hi - p0, g * 512:(g + 1) * 512],
                            in_=gs[lo:hi, 544:1056]).then_inc(gw, 16)
                        C["gw"] += 16
                    gpsimd.dma_start(out=vf0[:], in_=gs[0:128, 544:1056]).then_inc(gw, 16)
                    C["gw"] += 16
                    gpsimd.wait_ge(gw, C["gw"])
        with nc.sbuf_tensor("wbuf3", [128, 4096], BF16) as wbuf:
            fm_proj("wuq", 4, 512, 512, cqfm, L, qcfm, L, wbuf)
            tm_proj("wuq", 4, 512, 512, cqfm, L, qctm, wbuf, skip_wload=True)
            tm_proj("wqr", 4, 256, 256, cqfm, L, qrtm, wbuf)
        with nc.sbuf_tensor("qttm", [128, 16 * 256], BF16) as qttm, \
             nc.sbuf_tensor("kttm", [128, 16 * 32], BF16) as kttm, \
             nc.sbuf_tensor("tabc", [128, 256], F32) as tabc, \
             nc.sbuf_tensor("tabs", [128, 256], F32) as tabs_b, \
             nc.sbuf_tensor("rsc", [128, 256], F32) as rsc, \
             nc.sbuf_tensor("kdg", [128, 32], BF16) as kdg:

            def softplus_block(src, dst, width):
                ae0 = C["ae"]
                with nc.Block() as blk:
                    @blk.scalar
                    def _(scalar):
                        scalar.activation(out=rbuf[:, 0:width], in_=src, func=AF.Abs
                                          ).then_inc(ae, 1)
                        scalar.wait_ge(ae, ae0 + 1)
                        scalar.activation(out=isb[:, 0:width], in_=rbuf[:, 0:width],
                                          func=AF.Exp, scale=-1.0).then_inc(ae, 1)
                        scalar.wait_ge(ae, ae0 + 2)
                        scalar.activation(out=rbuf[:, 0:width], in_=isb[:, 0:width],
                                          func=AF.Ln, bias=onesc[:]).then_inc(ae, 1)

                    @blk.vector
                    def _(vector):
                        vector.wait_ge(ae, ae0 + 3)
                        vector.scalar_tensor_tensor(out=dst, in0=src, scalar=0.0,
                                                    in1=rbuf[:, 0:width],
                                                    op0=ALU.max, op1=ALU.add
                                                    ).then_inc(ve, 1)
                C["ae"] += 3
                C["ve"] += 1

            for half in range(2):
                softplus_block(qrtm[:, half * 2048:(half + 1) * 2048],
                               musq[:, half * 2048:(half + 1) * 2048], 2048)
            softplus_block(krtm[:], muk[:], 512)

            def load_tab(cn, sn):
                ld0 = C["ld"]
                with nc.Block() as blk:
                    @blk.sync
                    def _(sync):
                        sync.dma_start(out=tabc[:], in_=tabs[cn]).then_inc(ld, 16)
                        sync.dma_start(out=tabs_b[:], in_=tabs[sn]).then_inc(ld, 16)
                        sync.wait_ge(ld, ld0 + 32)
                C["ld"] += 32

            def rot_block(mu, nh, out_sb):
                with nc.Block() as blk:
                    @blk.vector
                    def _(vector):
                        for tt in range(NT):
                            mu3 = mu[:, tt * nh * 32:(tt + 1) * nh * 32].rearrange(
                                "p (h f sx) -> p h f sx", f=2, sx=16)
                            o3 = out_sb[:, tt * nh * 32:(tt + 1) * nh * 32].rearrange(
                                "p (h f sx) -> p h f sx", f=2, sx=16)
                            cb = tabc[:, tt * 16:(tt + 1) * 16].unsqueeze(1).to_broadcast([128, nh, 16])
                            sbb = tabs_b[:, tt * 16:(tt + 1) * 16].unsqueeze(1).to_broadcast([128, nh, 16])
                            m1 = mu3[:, :, 0, :]
                            m2 = mu3[:, :, 1, :]
                            s1 = rsc[:, 0:nh * 16].rearrange("p (h sx) -> p h sx", sx=16)
                            s2 = rsc[:, 128:128 + nh * 16].rearrange("p (h sx) -> p h sx", sx=16)
                            for (dst, a1, t1, a2, t2, op) in (
                                    (0, m1, cb, m2, sbb, ALU.subtract),
                                    (1, m1, sbb, m2, cb, ALU.add)):
                                vector.tensor_mul(s1, a1, t1).then_inc(ve, 1)
                                C["ve"] += 1
                                vector.wait_ge(ve, C["ve"])
                                vector.tensor_mul(s2, a2, t2).then_inc(ve, 1)
                                C["ve"] += 1
                                vector.wait_ge(ve, C["ve"])
                                vector.tensor_tensor(o3[:, :, dst, :], s1, s2, op).then_inc(ve, 1)
                                C["ve"] += 1
                                vector.wait_ge(ve, C["ve"])

            load_tab("c2t", "s2t")
            rot_block(musq, 8, qttm)
            pe0 = C["pe"]
            ae0 = C["ae"]
            with nc.Block() as blk:
                @blk.tensor
                def _(tensor):
                    for tt in range(NT):
                        if tt >= 1:
                            tensor.wait_ge(ae, ae0 + 2 * tt)
                        for chunk in range(2):
                            mm = tensor.transpose(ps_t2[:, chunk * 128:(chunk + 1) * 128],
                                                  qttm[:, tt * 256 + chunk * 128:tt * 256 + (chunk + 1) * 128],
                                                  ident[:])
                        mm.then_inc(pe, 1)

                @blk.scalar
                def _(scalar):
                    for tt in range(NT):
                        scalar.wait_ge(pe, pe0 + tt + 1)
                        scalar.activation(out=qtfm[:, 0 * L + tt * 128:0 * L + (tt + 1) * 128],
                                          in_=ps_t2[:, 0:128], func=AF.Identity).then_inc(ae, 1)
                        scalar.activation(out=qtfm[:, 1 * L + tt * 128:1 * L + (tt + 1) * 128],
                                          in_=ps_t2[:, 128:256], func=AF.Identity).then_inc(ae, 1)
            C["pe"] += NT
            C["ae"] += 2 * NT
            load_tab("ct_", "st_")
            rot_block(musq, 8, qotm)
            with nc.Block() as blk:
                @blk.vector
                def _(vector):
                    mu3 = muk[:, 0:32].rearrange("p (f sx) -> p f sx", f=2)
                    m1 = mu3[:, 0, :]
                    m2 = mu3[:, 1, :]
                    for (dst, a1, t1, a2, t2, op) in (
                            (0, m1, tabc[:, 0:16], m2, tabs_b[:, 0:16], ALU.subtract),
                            (1, m1, tabs_b[:, 0:16], m2, tabc[:, 0:16], ALU.add)):
                        vector.tensor_mul(rsc[:, 0:16], a1, t1).then_inc(ve, 1)
                        C["ve"] += 1
                        vector.wait_ge(ve, C["ve"])
                        vector.tensor_mul(rsc[:, 16:32], a2, t2).then_inc(ve, 1)
                        C["ve"] += 1
                        vector.wait_ge(ve, C["ve"])
                        vector.tensor_tensor(kdg[:, dst * 16:(dst + 1) * 16],
                                             rsc[:, 0:16], rsc[:, 16:32], op).then_inc(ve, 1)
                        C["ve"] += 1
                        vector.wait_ge(ve, C["ve"])
            load_tab("cp63", "sp63")
            rot_block(muk, 1, kttm)
            pe0 = C["pe"]
            ae0 = C["ae"]
            with nc.Block() as blk:
                @blk.tensor
                def _(tensor):
                    for tt in range(NT):
                        if tt >= 1:
                            tensor.wait_ge(ae, ae0 + tt)
                        mm = tensor.transpose(ps_t2[0:32, 0:128],
                                              kttm[:, tt * 32:(tt + 1) * 32], ident[:])
                        mm.then_inc(pe, 1)

                @blk.scalar
                def _(scalar):
                    for tt in range(NT):
                        scalar.wait_ge(pe, pe0 + tt + 1)
                        scalar.activation(out=ktr[0:32, tt * 128:(tt + 1) * 128],
                                          in_=ps_t2[0:32, 0:128], func=AF.Identity
                                          ).then_inc(ae, 1)
            C["pe"] += NT
            C["ae"] += NT
            pe0 = C["pe"]
            ae0 = C["ae"]
            with nc.Block() as blk:
                @blk.tensor
                def _(tensor):
                    for chunk in range(2):
                        mm = tensor.transpose(ps_t2[:, chunk * 128:(chunk + 1) * 128],
                                              qotm[:, chunk * 128:(chunk + 1) * 128],
                                              ident[:])
                    mm2 = tensor.transpose(ps_t2[0:32, 384:512], kdg[:], ident[:])
                    mm2.then_inc(pe, 1)

                @blk.scalar
                def _(scalar):
                    scalar.wait_ge(pe, pe0 + 1)
                    scalar.activation(out=qofm[:, 0:128], in_=ps_t2[:, 0:128],
                                      func=AF.Identity).then_inc(ae, 1)
                    scalar.activation(out=qofm[:, 128:256], in_=ps_t2[:, 128:256],
                                      func=AF.Identity).then_inc(ae, 1)
                    scalar.activation(out=kor[0:32, 0:96], in_=ps_t2[0:32, 384:480],
                                      func=AF.Identity).then_inc(ae, 1)
            C["pe"] += 1
            C["ae"] += 3
            pe0 = C["pe"]
            with nc.Block() as blk:
                @blk.gpsimd
                def _(gpsimd):
                    for dst0 in (32, 64, 96):
                        gpsimd.dma_start(out=ktr[dst0:dst0 + 32, :], in_=ktr[0:32, :]
                                         ).then_inc(gw, 16)
                        C["gw"] += 16
                        gpsimd.dma_start(out=kor[dst0:dst0 + 32, :], in_=kor[0:32, :]
                                         ).then_inc(gw, 16)
                        C["gw"] += 16
                    gpsimd.dma_start(out=kir[64:128, :], in_=kir[0:64, :]).then_inc(gw, 16)
                    C["gw"] += 16
                    for tt in range(NT):
                        gpsimd.dma_start(out=gs[tt * 128:(tt + 1) * 128, 512:544],
                                         in_=muk[:, tt * 32:(tt + 1) * 32]).then_inc(gw, 16)
                        C["gw"] += 16
                    gpsimd.wait_ge(gw, C["gw"])

                @blk.tensor
                def _(tensor):
                    mm = tensor.matmul(ps_w4[:], ones1[:], iwsb[:], start=True, stop=True)
                    mm.then_inc(pe, 1)

                @blk.scalar
                def _(scalar):
                    scalar.wait_ge(pe, pe0 + 1)
                    scalar.activation(out=wcol[:], in_=ps_w4[:], func=AF.Identity
                                      ).then_inc(ae, 1)
            C["pe"] += 1
            C["ae"] += 1
    _s1.close()

    prev_dots_end = 0
    prev_eav_end = 0
    prev_combine_end = 0
    with nc.psum_tensor("ps_i", [128, 2 * 512], F32) as ps_i, \
         nc.psum_tensor("ps_sc", [128, 2 * 512], F32) as ps_sc, \
         nc.psum_tensor("ps_av", [128, 512], F32) as ps_av, \
         nc.psum_tensor("ps_tp", [128, 512], BF16) as ps_tp, \
         nc.psum_tensor("ps_o", [128, 2 * 512], F32) as ps_o:
        for tt in range(NT):
            wc = wc_of(tt)
            nch = wc // 512
            pe0 = C["pe"]
            ae0 = C["ae"]
            ve0 = C["ve"]
            with nc.Block() as blk:
                @blk.tensor
                def _(tensor):
                    for n in range(nch):
                        for h in range(4):
                            g = n * 4 + h
                            slot = g % 2
                            if g >= 2:
                                tensor.wait_ge(ae, ae0 + g - 1)
                            hb = (h % 2) * 64
                            mm = tensor.matmul(
                                ps_i[:, slot * 512:(slot + 1) * 512],
                                qifm[hb:hb + 64, (h // 2) * L + tt * 128:(h // 2) * L + (tt + 1) * 128],
                                kir[hb:hb + 64, n * 512:(n + 1) * 512],
                                start=True, stop=True)
                            mm.then_inc(pe, 1)

                @blk.scalar
                def _(scalar):
                    for n in range(nch):
                        for h in range(4):
                            g = n * 4 + h
                            slot = g % 2
                            scalar.wait_ge(pe, pe0 + g + 1)
                            if n >= 1:
                                scalar.wait_ge(ve, ve0 + (n - 1) * 4 + h + 1)
                            scalar.activation(out=rbuf[:, h * 512:(h + 1) * 512],
                                              in_=ps_i[:, slot * 512:(slot + 1) * 512],
                                              func=AF.Relu).then_inc(ae, 1)

                @blk.vector
                def _(vector):
                    for n in range(nch):
                        vector.wait_ge(ae, ae0 + n * 4 + 4)
                        vector.tensor_scalar(out=isb[:, n * 512:(n + 1) * 512],
                                             in0=rbuf[:, 0:512], scalar1=wcol[:, 0:1],
                                             scalar2=None, op0=ALU.mult).then_inc(ve, 1)
                        for h in range(1, 4):
                            vector.wait_ge(ve, ve0 + n * 4 + h)
                            vector.scalar_tensor_tensor(
                                out=isb[:, n * 512:(n + 1) * 512],
                                in0=rbuf[:, h * 512:(h + 1) * 512],
                                scalar=wcol[:, h:h + 1],
                                in1=isb[:, n * 512:(n + 1) * 512],
                                op0=ALU.mult, op1=ALU.add).then_inc(ve, 1)
                        vector.wait_ge(ve, ve0 + n * 4 + 4)
            C["pe"] += nch * 4
            C["ae"] += nch * 4
            C["ve"] += nch * 4

            with nc.Block() as blk:
                @blk.gpsimd
                def _(gpsimd):
                    gpsimd.affine_select(out=isb[:, 0:wc], in_=isb[:, 0:wc],
                                         compare_op=ALU.is_ge, fill=NEG,
                                         base=tt * 128 - 64, pattern=[[-1, wc]],
                                         channel_multiplier=1).then_inc(ge, 1)
            C["ge"] += 1

            ve0 = C["ve"]
            with nc.Block() as blk:
                @blk.vector
                def _(vector):
                    vector.tensor_scalar(out=rbuf[:, 0:wc], in0=isb[:, 0:wc],
                                         scalar1=0.0, scalar2=None,
                                         op0=ALU.is_equal).then_inc(ve, 1)
                    vector.wait_ge(ve, ve0 + 1)
                    vector.tensor_mul(rbuf[:, 0:wc], rbuf[:, 0:wc], trp[:, 0:wc]
                                      ).then_inc(ve, 1)
                    vector.wait_ge(ve, ve0 + 2)
                    vector.tensor_add(isb[:, 0:wc], isb[:, 0:wc], rbuf[:, 0:wc]
                                      ).then_inc(ve, 1)
                    vector.wait_ge(ve, ve0 + 3)
            C["ve"] += 3

            ve0 = C["ve"]
            with nc.Block() as blk:
                @blk.vector
                def _(vector):
                    nv = 0
                    for r in range(4):
                        vector.max(out=mv8[:], in_=isb[:, 0:wc]).then_inc(ve, 1)
                        nv += 1
                        vector.wait_ge(ve, ve0 + nv)
                        vector.max_index(out=midx[:, r * 8:(r + 1) * 8], in_max=mv8[:],
                                         in_values=isb[:, 0:wc]).then_inc(ve, 1)
                        nv += 1
                        vector.wait_ge(ve, ve0 + nv)
                        vector.tensor_copy(out=vals[:, r * 8:(r + 1) * 8], in_=mv8[:]
                                           ).then_inc(ve, 1)
                        nv += 1
                        vector.wait_ge(ve, ve0 + nv)
                        if r < 3:
                            vector.match_replace(out=isb[:, 0:wc], in_to_replace=mv8[:],
                                                 in_values=isb[:, 0:wc], imm_value=NEG
                                                 ).then_inc(ve, 1)
                            nv += 1
                            vector.wait_ge(ve, ve0 + nv)
                    vector.tensor_scalar(out=pb[:], in0=vals[:], scalar1=-1e37,
                                         scalar2=NEG, op0=ALU.is_lt, op1=ALU.mult
                                         ).then_inc(ve, 1)
                    nv += 1
                    vector.wait_ge(ve, ve0 + nv)
            C["ve"] += 16

            ve0 = C["ve"]
            g0 = {0: C["g20"], 1: C["g21"]}
            pdots = prev_dots_end
            with nc.Block() as blk:
                @blk.gpsimd
                def _(gpsimd):
                    for bi in range(8):
                        slot = bi % 2
                        if bi >= 2:
                            gpsimd.wait_ge(ve, ve0 + 12 * 4 * (bi - 1))
                        elif pdots:
                            gpsimd.wait_ge(ve, pdots)
                        for j4 in range(4):
                            j = bi * 4 + j4
                            gpsimd.indirect_dma_start(
                                out=gb[:, (slot * 4 + j4) * 544:(slot * 4 + j4 + 1) * 544],
                                out_offset=None, in_=gs,
                                in_offset=bass.IndirectOffsetOnAxis(
                                    ap=midx[:, j:j + 1], axis=0),
                            ).then_inc(g2[slot], 16)
                            C["g2%d" % slot] += 16

                @blk.vector
                def _(vector):
                    nv = 0
                    for bi in range(8):
                        slot = bi % 2
                        vector.wait_ge(g2[slot], g0[slot] + 16 * 4 * (bi // 2 + 1))
                        for j4 in range(4):
                            j = bi * 4 + j4
                            go = (slot * 4 + j4) * 544
                            vector.tensor_mul(prod[:], gb[:, go:go + 512],
                                              qctm[:, tt * 512:(tt + 1) * 512]
                                              ).then_inc(ve, 1)
                            nv += 1
                            vector.wait_ge(ve, ve0 + nv)
                            vector.tensor_reduce(out=dsc[:, j * 8:(j + 1) * 8],
                                                 in_=prod[:].rearrange("p (h d) -> p h d", d=64),
                                                 axis=AX.X, op=ALU.add).then_inc(ve, 1)
                            nv += 1
                            vector.wait_ge(ve, ve0 + nv)
                            mug = gb[:, go + 512:go + 544].rearrange("p (f s) -> p f s", f=2)
                            for (dst, a1, t1, a2, t2, op) in (
                                    (0, mug[:, 0, :], tcs[:, j * 16:(j + 1) * 16],
                                     mug[:, 1, :], tss[:, j * 16:(j + 1) * 16], ALU.subtract),
                                    (1, mug[:, 0, :], tss[:, j * 16:(j + 1) * 16],
                                     mug[:, 1, :], tcs[:, j * 16:(j + 1) * 16], ALU.add)):
                                vector.tensor_mul(rbuf[:, 0:16], a1, t1).then_inc(ve, 1)
                                nv += 1
                                vector.wait_ge(ve, ve0 + nv)
                                vector.tensor_mul(rbuf[:, 16:32], a2, t2).then_inc(ve, 1)
                                nv += 1
                                vector.wait_ge(ve, ve0 + nv)
                                vector.tensor_tensor(kap[:, dst * 16:(dst + 1) * 16],
                                                     rbuf[:, 0:16], rbuf[:, 16:32], op
                                                     ).then_inc(ve, 1)
                                nv += 1
                                vector.wait_ge(ve, ve0 + nv)
                            vector.tensor_mul(
                                prod[:, 0:256].rearrange("p (h s) -> p h s", s=32),
                                qotm[:, tt * 256:(tt + 1) * 256].rearrange("p (h s) -> p h s", s=32),
                                kap[:].unsqueeze(1).to_broadcast([128, 8, 32])
                                ).then_inc(ve, 1)
                            nv += 1
                            vector.wait_ge(ve, ve0 + nv)
                            vector.tensor_reduce(out=rbuf[:, 32:40],
                                                 in_=prod[:, 0:256].rearrange("p (h s) -> p h s", s=32),
                                                 axis=AX.X, op=ALU.add).then_inc(ve, 1)
                            nv += 1
                            vector.wait_ge(ve, ve0 + nv)
                            vector.tensor_add(dsc[:, j * 8:(j + 1) * 8],
                                              dsc[:, j * 8:(j + 1) * 8], rbuf[:, 32:40]
                                              ).then_inc(ve, 1)
                            nv += 1
                            vector.wait_ge(ve, ve0 + nv)
                            vector.scalar_tensor_tensor(
                                out=dsc[:, j * 8:(j + 1) * 8],
                                in0=pb[:, j:j + 1].to_broadcast([128, 8]),
                                scalar=1.0,
                                in1=dsc[:, j * 8:(j + 1) * 8],
                                op0=ALU.mult, op1=ALU.add).then_inc(ve, 1)
                            nv += 1
                            vector.wait_ge(ve, ve0 + nv)
            C["ve"] += 12 * 32
            prev_dots_end = C["ve"]

            pe0 = C["pe"]
            ae0 = C["ae"]
            ve0 = C["ve"]
            nch_t = 3 if tt == 0 else 2
            g0 = {0: C["g20"], 1: C["g21"]}
            def pe_score(h):
                return pe0 + (h + 1 if h < 3 else 3 * h - 3)

            def pe_tp(h):
                return pe0 + (4 + 3 * h if h <= 5 else (21 if h == 6 else 23))

            def pe_av(h):
                return pe_tp(h) - pe0 + 1 + pe0

            act_order = []
            for _h in range(8):
                act_order += [("e", _h), ("c", _h)]
                if _h >= 2:
                    act_order += [("tb", _h - 2)]
            act_order += [("tb", 6), ("tb", 7)]
            _ae_map = {}
            _pos = 0
            for kind, _h in act_order:
                _pos += (nch_t if kind == "tb" else 1)
                _ae_map[(kind, _h)] = _pos
            AE_TOTAL = _pos

            def ae_exp(h):
                return ae0 + _ae_map[("e", h)]

            def ae_aec(h):
                return ae0 + _ae_map[("c", h)]

            def ae_atbe(h):
                return ae0 + _ae_map[("tb", h)]

            def ae_afm():
                return ae0 + AE_TOTAL + 4

            def ae_oev(dm):
                return ae0 + AE_TOTAL + 4 + dm + 1

            NSEC = 7 if tt == 0 else 6

            def ve_sec(h, k=None):
                return ve0 + NSEC * h + (NSEC if k is None else k)

            def ve_recip(h):
                return ve0 + 8 * NSEC + h + 1

            def ve_eav(bi):
                return ve0 + 8 * NSEC + 8 + 1 + 8 * (bi + 1)

            ve_att_done = ve0 + 8 * NSEC + 8 + 1 + 64 + 16
            with nc.Block() as blk:
                @blk.tensor
                def _(tensor):
                    def emit_score(h):
                        slot = h % 2
                        hb = (h % 2) * 64
                        pt = h // 2
                        qb = (h % 4) * 32
                        qpt = h // 4
                        if h >= 2:
                            tensor.wait_ge(ve, ve_sec(h - 2))
                            tensor.wait_ge(ae, ae_exp(h - 2))
                        if tt == 0:
                            tensor.matmul(
                                ps_sc[:, slot * 512 + 64:slot * 512 + 192],
                                qcfm[hb:hb + 64, pt * L + tt * 128:pt * L + (tt + 1) * 128],
                                kfm[hb:hb + 64, pt * L + 0:pt * L + 128],
                                start=True, stop=False)
                            tensor.matmul(
                                ps_sc[:, slot * 512 + 64:slot * 512 + 192],
                                qtfm[qb:qb + 32, qpt * L + tt * 128:qpt * L + (tt + 1) * 128],
                                ktr[qb:qb + 32, 0:128],
                                start=False, stop=False, tile_position=(qb, 0))
                            tensor.matmul(
                                ps_sc[:, slot * 512 + 192:slot * 512 + 288],
                                qcfm[hb:hb + 64, pt * L + 0:pt * L + 128],
                                kfm[hb:hb + 64, pt * L + 0:pt * L + 96],
                                start=False, stop=False)
                            mm = tensor.matmul(
                                ps_sc[:, slot * 512 + 192:slot * 512 + 288],
                                qofm[qb:qb + 32, qpt * 128:(qpt + 1) * 128],
                                kor[qb:qb + 32, 0:96],
                                start=False, stop=True, tile_position=(qb, 0))
                        else:
                            w0 = tt * 128 - 64
                            tensor.matmul(
                                ps_sc[:, slot * 512:slot * 512 + 192],
                                qcfm[hb:hb + 64, pt * L + tt * 128:pt * L + (tt + 1) * 128],
                                kfm[hb:hb + 64, pt * L + w0:pt * L + w0 + 192],
                                start=True, stop=False)
                            mm = tensor.matmul(
                                ps_sc[:, slot * 512:slot * 512 + 192],
                                qtfm[qb:qb + 32, qpt * L + tt * 128:qpt * L + (tt + 1) * 128],
                                ktr[qb:qb + 32, w0:w0 + 192],
                                start=False, stop=True, tile_position=(qb, 0))
                        mm.then_inc(pe, 1)

                    def emit_tp(h):
                        slot = h % 2
                        tensor.wait_ge(ae, ae_aec(h))
                        if h >= 1:
                            tensor.wait_ge(ae, ae_atbe(h - 1))
                        if h <= 1 and prev_combine_end:
                            tensor.wait_ge(ve, prev_combine_end)
                        for chunk in range(nch_t):
                            cw = 128 if chunk < 2 else 96
                            mm = tensor.transpose(
                                ps_tp[0:cw, chunk * 128:chunk * 128 + 128],
                                ab[:, slot * 352 + chunk * 128:slot * 352 + chunk * 128 + cw],
                                ident[:])
                        mm.then_inc(pe, 1)

                    def emit_av(h):
                        slot = h % 2
                        tensor.wait_ge(ae, ae_atbe(h))
                        if h == 0 and prev_combine_end:
                            tensor.wait_ge(ve, prev_combine_end)
                        mm = tensor.matmul(ps_av[:, h * 64:(h + 1) * 64],
                                           atb[:, slot * 384:slot * 384 + 128],
                                           vsh[:, tt * 512 + h * 64:tt * 512 + (h + 1) * 64],
                                           start=(h == 0), stop=False)
                        mm = tensor.matmul(ps_av[:, h * 64:(h + 1) * 64],
                                           atb[0:64, slot * 384 + 128:slot * 384 + 256],
                                           vsh[0:64, (tt + 1) * 512 + h * 64:(tt + 1) * 512 + (h + 1) * 64],
                                           start=False, stop=(tt != 0 and h == 7))
                        if tt == 0:
                            mm = tensor.matmul(ps_av[:, h * 64:(h + 1) * 64],
                                               atb[0:96, slot * 384 + 256:slot * 384 + 384],
                                               vf0[0:96, h * 64:(h + 1) * 64],
                                               start=False, stop=(h == 7))
                        mm.then_inc(pe, 1)

                    for h in range(8):
                        emit_score(h)
                        if h >= 2:
                            emit_tp(h - 2)
                            emit_av(h - 2)
                    for h in (6, 7):
                        emit_tp(h)
                        emit_av(h)

                @blk.vector
                def _(vector):
                    for h in range(8):
                        slot = h % 2
                        vector.wait_ge(pe, pe_score(h))
                        if h >= 2:
                            vector.wait_ge(ae, ae_exp(h - 2))
                        ko = 0
                        if tt == 0:
                            vector.memset(ps_sc[:, slot * 512:slot * 512 + 64], 0.0
                                          ).then_inc(ve, 1)
                            ko = 1
                            vector.wait_ge(ve, ve_sec(h, ko))
                        vector.tensor_add(scb[:, slot * 352:slot * 352 + 192],
                                          ps_sc[:, slot * 512:slot * 512 + 192],
                                          mband0[:] if tt == 0 else mband[:]
                                          ).then_inc(ve, 1)
                        vector.wait_ge(ve, ve_sec(h, ko + 1))
                        vector.tensor_copy(
                            out=scb[:, slot * 352 + 192:slot * 352 + 224],
                            in_=dsc[:].rearrange("p (j h) -> p h j", h=8)[:, h, :]
                            ).then_inc(ve, 1)
                        vector.wait_ge(ve, ve_sec(h, ko + 2))
                        if tt == 0:
                            vector.tensor_add(scb[:, slot * 352 + 256:slot * 352 + 352],
                                              ps_sc[:, slot * 512 + 192:slot * 512 + 288],
                                              mf0[:]).then_inc(ve, 1)
                        else:
                            vector.memset(scb[:, slot * 352 + 256:slot * 352 + 352], NEG
                                          ).then_inc(ve, 1)
                        vector.wait_ge(ve, ve_sec(h, ko + 3))
                        vector.memset(scb[:, slot * 352 + 224:slot * 352 + 256], NEG
                                      ).then_inc(ve, 1)
                        vector.wait_ge(ve, ve_sec(h, ko + 4))
                        vector.reduce_max(out=mrow[:, 0:1],
                                          in_=scb[:, slot * 352:(slot + 1) * 352],
                                          axis=AX.X).then_inc(ve, 1)
                        vector.wait_ge(ve, ve_sec(h, ko + 5))
                        if h >= 1:
                            vector.wait_ge(ae, ae_exp(h - 1))
                        vector.tensor_scalar_mul(mrow[:, 1:2], mrow[:, 0:1], -SCALE
                                                 ).then_inc(ve, 1)
                        vector.wait_ge(ve, ve_sec(h, ko + 6))

                @blk.scalar
                def _(scalar):
                    for kind, h in act_order:
                        slot = h % 2
                        if kind == "e":
                            scalar.wait_ge(ve, ve_sec(h))
                            if h >= 2:
                                scalar.wait_ge(pe, pe_tp(h - 2))
                            scalar.activation(out=ab[:, slot * 352:(slot + 1) * 352],
                                              in_=scb[:, slot * 352:(slot + 1) * 352],
                                              func=AF.Exp, bias=mrow[:, 1:2], scale=SCALE,
                                              accum_out=srw8[:, h:h + 1]).then_inc(ae, 1)
                        elif kind == "c":
                            scalar.wait_ge(ae, ae_exp(h))
                            scalar.activation(out=AE[:, h * 32:(h + 1) * 32],
                                              in_=ab[:, slot * 352 + 192:slot * 352 + 224],
                                              func=AF.Identity).then_inc(ae, 1)
                        else:
                            scalar.wait_ge(pe, pe_tp(h))
                            if h >= 2:
                                scalar.wait_ge(pe, pe_av(h - 2))
                            for chunk in range(nch_t):
                                cw = 128 if chunk < 2 else 96
                                scalar.activation(
                                    out=atb[0:cw, slot * 384 + chunk * 128:slot * 384 + (chunk + 1) * 128],
                                    in_=ps_tp[0:cw, chunk * 128:(chunk + 1) * 128],
                                    func=AF.Identity).then_inc(ae, 1)

                @blk.vector
                def _(vector):
                    for h in range(8):
                        vector.wait_ge(ae, ae_exp(h))
                        vector.reciprocal(out=rcp8[:, h:h + 1], in_=srw8[:, h:h + 1]
                                          ).then_inc(ve, 1)
                        vector.wait_ge(ve, ve_recip(h))

                @blk.gpsimd
                def _(gpsimd):
                    for bi in range(8):
                        slot = bi % 2
                        if bi >= 2:
                            gpsimd.wait_ge(ve, ve_eav(bi - 2))
                        else:
                            gpsimd.wait_ge(ve, prev_dots_end)
                        for j4 in range(4):
                            j = bi * 4 + j4
                            gpsimd.indirect_dma_start(
                                out=gb[:, (slot * 4 + j4) * 544:(slot * 4 + j4) * 544 + 512],
                                out_offset=None, in_=gs,
                                in_offset=bass.IndirectOffsetOnAxis(
                                    ap=midx[:, j:j + 1], axis=0),
                                element_offset=544,
                            ).then_inc(g2[slot], 16)
                            C["g2%d" % slot] += 16

                @blk.vector
                def _(vector):
                    nv = 8 * NSEC + 8
                    vector.wait_ge(ae, ae_aec(7))
                    vector.memset(eac[:], 0.0).then_inc(ve, 1)
                    nv += 1
                    vector.wait_ge(ve, ve0 + nv)
                    for bi in range(8):
                        slot = bi % 2
                        vector.wait_ge(g2[slot], g0[slot] + 16 * 4 * (bi // 2 + 1))
                        for j4 in range(4):
                            j = bi * 4 + j4
                            go = (slot * 4 + j4) * 544
                            scl = AE[:].rearrange("p (h j) -> p h j", j=32)[:, :, j:j + 1].to_broadcast([128, 8, 64])
                            vector.tensor_mul(prod[:].rearrange("p (h d) -> p h d", d=64),
                                              gb[:, go:go + 512].rearrange("p (h d) -> p h d", d=64),
                                              scl).then_inc(ve, 1)
                            nv += 1
                            vector.wait_ge(ve, ve0 + nv)
                            vector.tensor_add(eac[:], eac[:], prod[:]).then_inc(ve, 1)
                            nv += 1
                            vector.wait_ge(ve, ve0 + nv)
                    vector.wait_ge(pe, pe_av(7))
                    for h in range(8):
                        vector.tensor_add(prod[:, 0:64], ps_av[:, h * 64:(h + 1) * 64],
                                          eac[:, h * 64:(h + 1) * 64]).then_inc(ve, 1)
                        nv += 1
                        vector.wait_ge(ve, ve0 + nv)
                        vector.tensor_scalar_mul(att[:, h * 64:(h + 1) * 64],
                                                 prod[:, 0:64], rcp8[:, h:h + 1]
                                                 ).then_inc(ve, 1)
                        nv += 1
                        vector.wait_ge(ve, ve0 + nv)

                @blk.tensor
                def _(tensor):
                    tensor.wait_ge(ve, ve_att_done)
                    tensor.wait_ge(ae, ae_atbe(7))
                    for chunk in range(4):
                        mm = tensor.transpose(ps_tp[:, chunk * 128:(chunk + 1) * 128],
                                              att[:, chunk * 128:(chunk + 1) * 128],
                                              ident[:])
                    mm.then_inc(pe, 1)

                @blk.scalar
                def _(scalar):
                    scalar.wait_ge(pe, pe0 + 25)
                    for chunk in range(4):
                        scalar.activation(out=afm[:, chunk * 128:(chunk + 1) * 128],
                                          in_=ps_tp[:, chunk * 128:(chunk + 1) * 128],
                                          func=AF.Identity).then_inc(ae, 1)

                @blk.tensor
                def _(tensor):
                    tensor.wait_ge(ae, ae_afm())
                    for dm in range(8):
                        slot = dm % 2
                        if dm >= 2:
                            tensor.wait_ge(ae, ae_oev(dm - 2))
                        for kc in range(4):
                            mm = tensor.matmul(ps_o[:, slot * 512:slot * 512 + 128],
                                               wo[:, kc * D + dm * 128:kc * D + (dm + 1) * 128],
                                               afm[:, kc * 128:(kc + 1) * 128],
                                               start=(kc == 0), stop=(kc == 3))
                        mm.then_inc(pe, 1)

                @blk.scalar
                def _(scalar):
                    for dm in range(8):
                        slot = dm % 2
                        scalar.wait_ge(pe, pe0 + 25 + dm + 1)
                        nst = tt * 8 + dm
                        if nst >= 2:
                            scalar.wait_ge(stm[nst % 2], 16 * (nst // 2))
                        scalar.activation(out=otb[:, slot * 128:slot * 128 + 128],
                                          in_=ps_o[:, slot * 512:slot * 512 + 128],
                                          func=AF.Identity).then_inc(ae, 1)

                @blk.gpsimd
                def _(gpsimd):
                    for dm in range(8):
                        slot = dm % 2
                        nst = tt * 8 + dm
                        gpsimd.wait_ge(ae, ae_oev(dm))
                        gpsimd.dma_start(
                            out=a0[dm * 128:(dm + 1) * 128, tt * 128:(tt + 1) * 128],
                            in_=otb[:, slot * 128:slot * 128 + 128]
                            ).then_inc(stm[nst % 2], 16)
                        C["st%d" % (nst % 2)] += 16
            C["pe"] += 33
            C["ae"] += AE_TOTAL + 4 + 8
            C["ve"] += 8 * (7 if tt == 0 else 6) + 8 + 1 + 64 + 16
            prev_eav_end = ve_eav(7)
            prev_combine_end = ve_att_done

        with nc.Block() as blk:
            @blk.sync
            def _(sync):
                sync.wait_ge(stm[0], C["st0"])
                sync.wait_ge(stm[1], C["st1"])


def build_l0attn():
    nc = bass.Bass("TRN2", target_bir_lowering=False, debug=False)
    xn = nc.dram_tensor("xn", [D, L], BF16, kind="ExternalInput").ap()
    win = {nm: nc.dram_tensor(nm, list(shp), F32, kind="ExternalInput").ap()
           for nm, shp in WSHAPES}
    iwrow = nc.dram_tensor("iwrow", [1, 4], F32, kind="ExternalInput").ap()
    tabs = {nm: nc.dram_tensor(nm, [128, cols],
                               BF16 if nm in ("csl", "ssl") else F32,
                               kind="ExternalInput").ap()
            for nm, cols in [("c2t", 256), ("s2t", 256), ("ct_", 256), ("st_", 256),
                             ("cp63", 256), ("sp63", 256), ("csl", 512), ("ssl", 512),
                             ("ramp", 2048)]}
    a0 = nc.dram_tensor("a0", [D, L], F32, kind="ExternalOutput").ap()
    emit_l0attn(nc, xn, win, iwrow, tabs, a0)
    return nc



def build_fused():
    nc = bass.Bass("TRN2", target_bir_lowering=False, debug=False)
    x = nc.dram_tensor("x", [D, L], F32, kind="ExternalInput").ap()
    w1T = nc.dram_tensor("w1T", [D, FF], F32, kind="ExternalInput").ap()
    w2T = nc.dram_tensor("w2T", [FF, D], F32, kind="ExternalInput").ap()
    b1c = nc.dram_tensor("b1c", [128, NFC], F32, kind="ExternalInput").ap()
    b2c = nc.dram_tensor("b2c", [128, ND], F32, kind="ExternalInput").ap()
    win = {nm: nc.dram_tensor(nm, list(shp), F32, kind="ExternalInput").ap()
           for nm, shp in WSHAPES}
    iwrow = nc.dram_tensor("iwrow", [1, 4], F32, kind="ExternalInput").ap()
    tabs = {nm: nc.dram_tensor(nm, [128, cols],
                               BF16 if nm in ("csl", "ssl") else F32,
                               kind="ExternalInput").ap()
            for nm, cols in [("c2t", 256), ("s2t", 256), ("ct_", 256), ("st_", 256),
                             ("cp63", 256), ("sp63", 256), ("csl", 512), ("ssl", 512),
                             ("ramp", 2048)]}
    fc1A = nc.dram_tensor("fc1A", [256, D, 64], F32, kind="ExternalInput").ap()
    fc1B = nc.dram_tensor("fc1B", [256, 64, FF], F32, kind="ExternalInput").ap()
    fc2A = nc.dram_tensor("fc2A", [256, FF, 64], F32, kind="ExternalInput").ap()
    fc2B = nc.dram_tensor("fc2B", [256, 64, D], F32, kind="ExternalInput").ap()
    lg1 = nc.dram_tensor("lg1", [1, 256], F32, kind="ExternalInput").ap()
    lg2 = nc.dram_tensor("lg2", [1, 256], F32, kind="ExternalInput").ap()
    riota1 = nc.dram_tensor("riota1", [128, 1], F32, kind="ExternalInput").ap()
    riota2 = nc.dram_tensor("riota2", [128, 1], F32, kind="ExternalInput").ap()
    wqkvT = nc.dram_tensor("wqkvT", [D, 3072], F32, kind="ExternalInput").ap()
    brow = nc.dram_tensor("brow", [1, 3072], F32, kind="ExternalInput").ap()
    ctab = nc.dram_tensor("ctab", [128, 16 * 32], F32, kind="ExternalInput").ap()
    stab = nc.dram_tensor("stab", [128, 16 * 32], F32, kind="ExternalInput").ap()
    lepw = nc.dram_tensor("lepw", [128, 4 * 8], F32, kind="ExternalInput").ap()
    out = nc.dram_tensor("out", [D, L], F32, kind="ExternalOutput").ap()
    xn0 = nc.dram_tensor("xn0", [D, L], BF16, kind="Internal").ap()
    f0 = nc.dram_tensor("f0", [D, L], F32, kind="Internal").ap()
    a0 = nc.dram_tensor("a0", [D, L], F32, kind="Internal").ap()
    x1 = nc.dram_tensor("x1", [D, L], F32, kind="Internal").ap()
    xn1 = nc.dram_tensor("xn1", [D, L], BF16, kind="Internal").ap()
    f1 = nc.dram_tensor("f1", [D, L], F32, kind="Internal").ap()
    a1 = nc.dram_tensor("a1", [D, L], F32, kind="Internal").ap()

    with _phase("p0"):
        emit_ln(nc, x, xn0)
    with _phase("p1"):
        emit_l0ffn(nc, xn0, w1T, w2T, b1c, b2c, f0)
    with _phase("p2"):
        emit_l0attn(nc, xn0, win, iwrow, tabs, a0)
    with _phase("p3"):
        emit_res_ln(nc, x, a0, f0, x1, xn1)
    with _phase("p4"):
        emit_l1ffn(nc, xn1, fc1A, fc1B, fc2A, fc2B, lg1, lg2, riota1, riota2, f1)
    with _phase("p5"):
        emit_l1attn(nc, xn1, wqkvT, brow, ctab, stab, lepw, a1)
    with _phase("p6"):
        emit_res2(nc, x1, a1, f1, out)
    return nc


def _fm(a):
    return np.ascontiguousarray(a.T)


LAUNCH_CACHE = {}


def _get(name, builder):
    if name not in LAUNCH_CACHE:
        LAUNCH_CACHE[name] = builder()
    return LAUNCH_CACHE[name]


def _host_tabs_l0():
    th = 1.0 / 10000.0 ** (2.0 * np.arange(16, dtype=np.float32) / 32)
    t = np.arange(L, dtype=np.float32)[:, None]

    def tile128(m):
        return np.ascontiguousarray(
            m.reshape(16, 128, 16).transpose(1, 0, 2).reshape(128, 256)
        ).astype(np.float32)

    tabs = {}
    tabs["c2t"] = tile128(np.cos(2 * t * th))
    tabs["s2t"] = tile128(np.sin(2 * t * th))
    tabs["ct_"] = tile128(np.cos(t * th))
    tabs["st_"] = tile128(np.sin(t * th))
    tabs["cp63"] = tile128(np.cos((t + 63) * th))
    tabs["sp63"] = tile128(np.sin((t + 63) * th))
    j = np.arange(32, dtype=np.float32)[:, None]
    tabs["csl"] = np.broadcast_to(
        np.cos((64 + j) * th).reshape(1, 512), (128, 512)).astype(BF).copy()
    tabs["ssl"] = np.broadcast_to(
        np.sin((64 + j) * th).reshape(1, 512), (128, 512)).astype(BF).copy()
    ramp = -(np.arange(L, dtype=np.float64) + 1) * 1e-35
    tabs["ramp"] = np.broadcast_to(
        ramp.astype(np.float32).reshape(1, L), (128, L)).copy()
    return tabs


def _host_tabs_l1(w_lepe, b_lepe):
    pos = np.arange(L, dtype=np.float32)[:, None]
    div = np.exp(np.arange(0, 64, 2, dtype=np.float32) * (-np.log(10000.0) / 64))
    ang = pos * div
    ctab = np.ascontiguousarray(
        np.cos(ang).astype(np.float32).reshape(16, 128, 32).transpose(1, 0, 2).reshape(128, 512))
    stab = np.ascontiguousarray(
        np.sin(ang).astype(np.float32).reshape(16, 128, 32).transpose(1, 0, 2).reshape(128, 512))
    wl = w_lepe.reshape(D, 3)
    lepw = np.zeros((128, 32), np.float32)
    for ct in range(8):
        for j in range(3):
            lepw[:, j * 8 + ct] = wl[ct * 128:(ct + 1) * 128, j]
        lepw[:, 24 + ct] = b_lepe[ct * 128:(ct + 1) * 128]
    return ctab, stab, lepw


def kernel(**inputs):
    x = inputs["L0_attn_x"]
    cores = list(range(N_CORES))
    nc = _get("fused", build_fused)
    base = {"w1T": np.ascontiguousarray(inputs["L0_ffn_w1"].T),
            "w2T": np.ascontiguousarray(inputs["L0_ffn_w2"].T),
            "b1c": np.ascontiguousarray(inputs["L0_ffn_b1"].reshape(NFC, 128).T),
            "b2c": np.ascontiguousarray(inputs["L0_ffn_b2"].reshape(ND, 128).T),
            "wdkv": inputs["L0_attn_W_dkv"], "wuk": inputs["L0_attn_W_uk"],
            "wuv": inputs["L0_attn_W_uv"], "wdq": inputs["L0_attn_W_dq"],
            "wuq": inputs["L0_attn_W_uq"], "wqr": inputs["L0_attn_W_qr"],
            "wkr": inputs["L0_attn_W_kr"], "wout": inputs["L0_attn_W_out"],
            "iwq": inputs["L0_attn_idx_wq"], "iwk": inputs["L0_attn_idx_wk"],
            "iwrow": inputs["L0_attn_idx_w"].reshape(1, 4),
            "fc1A": inputs["L1_ffn_fc1_A"], "fc1B": inputs["L1_ffn_fc1_B"],
            "fc2A": inputs["L1_ffn_fc2_A"], "fc2B": inputs["L1_ffn_fc2_B"],
            "lg1": inputs["L1_ffn_fc1_logits"].reshape(1, 256),
            "lg2": inputs["L1_ffn_fc2_logits"].reshape(1, 256),
            "wqkvT": np.ascontiguousarray(inputs["L1_attn_w_qkv"].T),
            "brow": inputs["L1_attn_b_qkv"].reshape(1, 3072)}
    base.update(_host_tabs_l0())
    ctab, stab, lepw = _host_tabs_l1(inputs["L1_attn_w_lepe"], inputs["L1_attn_b_lepe"])
    base.update({"ctab": ctab, "stab": stab, "lepw": lepw})
    _ = inputs["L0_attn_raw_delta"]
    iot = np.arange(128, dtype=np.float32).reshape(128, 1)
    in_maps = [dict(base, x=_fm(x[b]),
                    riota1=iot + np.float32(b * 4),
                    riota2=iot + np.float32(b * 16)) for b in cores]
    res = run_bass_kernel_spmd(nc, in_maps, core_ids=cores)
    out = np.stack([res.results[b]["out"].T for b in cores])
    return np.ascontiguousarray(out, dtype=np.float32)


# version 06
# speedup vs baseline: 1.1429x; 1.0110x over previous
import math
import numpy as np
import ml_dtypes
import concourse.bass as bass
import concourse.mybir as mybir
from concourse.bass_utils import run_bass_kernel_spmd

F32 = mybir.dt.float32
BF16 = mybir.dt.bfloat16
U32 = mybir.dt.uint32
AF = mybir.ActivationFunctionType
ALU = mybir.AluOpType
AX = mybir.AxisListType
BF = ml_dtypes.bfloat16

L = 2048
D = 1024
NT = L // 128
ND = D // 128
FF = 4096
NFC = FF // 128
NEG = -3.0e38
N_CORES = 8
H1 = 16


_PREF = [""]
_SCOPE = []


class _phase:
    def __init__(self, pref):
        self.pref = pref

    def __enter__(self):
        _PREF[0] = self.pref + "_"
        _SCOPE.append([])
        return self

    def __exit__(self, *a):
        for cm in reversed(_SCOPE.pop()):
            cm.__exit__(None, None, None)
        _PREF[0] = ""
        return False


def sb(nc, name, shape, dtype):
    cm = nc.sbuf_tensor(_PREF[0] + name, shape, dtype)
    t = cm.__enter__()
    if _SCOPE:
        _SCOPE[-1].append(cm)
    return t


def ps(nc, name, shape, dtype=F32):
    cm = nc.psum_tensor(_PREF[0] + name, shape, dtype)
    t = cm.__enter__()
    if _SCOPE:
        _SCOPE[-1].append(cm)
    return t


def sem(nc, name):
    return nc.semaphore(_PREF[0] + name).__enter__()


def emit_ln(nc, x, xn):


    xt = sb(nc, "xt", [128, ND * L], F32)
    sq = sb(nc, "sq", [128, ND * L], F32)
    ones = sb(nc, "ones", [128, 1], F32)
    onesb = sb(nc, "onesb", [1, 128], F32)
    mrow = sb(nc, "mrow", [1, L], F32)
    qrow = sb(nc, "qrow", [1, L], F32)
    ve = sb(nc, "ve", [1, L], F32)
    rrow = sb(nc, "rrow", [1, L], F32)
    tmp = sb(nc, "tmp", [128, 2 * 1024], F32)
    xnt = sb(nc, "xnt", [128, 2 * 1024], BF16)
    ps_s = ps(nc, "ps_s", [1, 2048], F32)
    ps_b = ps(nc, "ps_b", [128, 2048], F32)
    ld = sem(nc, "ld")
    ap = sem(nc, "ap")
    st = sem(nc, "st")
    st0 = sem(nc, "st0")
    st1 = sem(nc, "st1")

    with nc.Block() as b:
        @b.sync
        def _(sync):
            for c in range(ND):
                sync.dma_start(out=xt[:, c * L:(c + 1) * L],
                               in_=x[c * 128:(c + 1) * 128, :]).then_inc(ld, 16)
            sync.wait_ge(ld, 16 * ND)

        @b.vector
        def _(vector):
            vector.memset(ones[:], 1.0)
            vector.memset(onesb[:], 1.0)

    with nc.Block() as b:
        @b.tensor
        def _(tensor):
            for n in range(4):
                for c in range(ND):
                    tensor.matmul(ps_s[:, n * 512:(n + 1) * 512], ones[:],
                                  xt[:, c * L + n * 512:c * L + (n + 1) * 512],
                                  start=(c == 0), stop=(c == ND - 1))
    with nc.Block() as b:
        @b.vector
        def _(vector):
            vector.tensor_scalar_mul(mrow[:], ps_s[:], 1.0 / D)
    with nc.Block() as b:
        @b.scalar
        def _(scalar):
            for c in range(ND):
                scalar.activation(out=sq[:, c * L:(c + 1) * L],
                                  in_=xt[:, c * L:(c + 1) * L], func=AF.Square)
    with nc.Block() as b:
        @b.tensor
        def _(tensor):
            for n in range(4):
                for c in range(ND):
                    tensor.matmul(ps_s[:, n * 512:(n + 1) * 512], ones[:],
                                  sq[:, c * L + n * 512:c * L + (n + 1) * 512],
                                  start=(c == 0), stop=(c == ND - 1))
    with nc.Block() as b:
        @b.vector
        def _(vector):
            vector.tensor_scalar_mul(qrow[:], ps_s[:], 1.0 / D)
    with nc.Block() as b:
        @b.vector
        def _(vector):
            vector.scalar_tensor_tensor(out=ve[:], in0=mrow[:], scalar=-1.0,
                                        in1=mrow[:], op0=ALU.mult, op1=ALU.mult)
    with nc.Block() as b:
        @b.vector
        def _(vector):
            vector.tensor_add(ve[:], ve[:], qrow[:])
    with nc.Block() as b:
        @b.vector
        def _(vector):
            vector.tensor_scalar_add(ve[:], ve[:], 1e-5)
    with nc.Block() as b:
        @b.scalar
        def _(scalar):
            scalar.activation(out=ve[:], in_=ve[:], func=AF.Ln)
    with nc.Block() as b:
        @b.scalar
        def _(scalar):
            scalar.activation(out=rrow[:], in_=ve[:], func=AF.Exp, scale=-0.5)

    for g in range(2):
        with nc.Block() as b:
            @b.tensor
            def _(tensor):
                for n in range(2):
                    tensor.matmul(ps_b[:, n * 512:(n + 1) * 512], onesb[:],
                                  mrow[:, g * 1024 + n * 512:g * 1024 + (n + 1) * 512],
                                  start=True, stop=True)
                    tensor.matmul(ps_b[:, 1024 + n * 512:1024 + (n + 1) * 512], onesb[:],
                                  rrow[:, g * 1024 + n * 512:g * 1024 + (n + 1) * 512],
                                  start=True, stop=True)
        with nc.Block() as b:
            @b.vector
            def _(vector):
                for c in range(ND):
                    s = c % 2
                    ssem = st0 if s == 0 else st1
                    nprior = (g * ND + c) // 2
                    if nprior > 0:
                        vector.wait_ge(ssem, 16 * nprior)
                    vector.tensor_sub(tmp[:, s * 1024:(s + 1) * 1024],
                                      xt[:, c * L + g * 1024:c * L + (g + 1) * 1024],
                                      ps_b[:, 0:1024]).then_inc(ap, 1)
                    vector.wait_ge(ap, g * 2 * ND + 2 * c + 1)
                    vector.tensor_mul(xnt[:, s * 1024:(s + 1) * 1024],
                                      tmp[:, s * 1024:(s + 1) * 1024],
                                      ps_b[:, 1024:2048]).then_inc(ap, 1)

            @b.sync
            def _(sync):
                for c in range(ND):
                    s = c % 2
                    ssem = st0 if s == 0 else st1
                    sync.wait_ge(ap, g * 2 * ND + 2 * c + 2)
                    sync.dma_start(out=xn[c * 128:(c + 1) * 128, g * 1024:(g + 1) * 1024],
                                   in_=xnt[:, s * 1024:(s + 1) * 1024]).then_inc(ssem, 16)
    with nc.Block() as b:
        @b.sync
        def _(sync):
            sync.wait_ge(st0, 16 * ND)
            sync.wait_ge(st1, 16 * ND)




def build_ln():
    nc = bass.Bass("TRN2", target_bir_lowering=False, debug=False)
    x = nc.dram_tensor("x", [D, L], F32, kind="ExternalInput").ap()
    xn = nc.dram_tensor("xn", [D, L], BF16, kind="ExternalOutput").ap()
    emit_ln(nc, x, xn)
    return nc

def emit_l0ffn(nc, xn, w1T, w2T, b1c, b2c, f0):


    sign1 = sb(nc, "sign1", [128, ND * FF], BF16)
    sign2 = sb(nc, "sign2", [128, NFC * D], BF16)
    xnt = sb(nc, "xnt", [128, 2 * ND * 512], BF16)
    hbuf = sb(nc, "hbuf", [128, NFC * 512], BF16)
    ones = sb(nc, "ones", [128, 1], F32)
    acol = sb(nc, "acol", [128, NFC + ND], F32)
    bcol = sb(nc, "bcol", [128, NFC + ND], F32)
    otile = sb(nc, "otile", [128, 2 * 512], F32)
    ld = sem(nc, "ld")
    wl = sem(nc, "wl")
    pe = sem(nc, "pe")
    ev = sem(nc, "ev")
    st0 = sem(nc, "st0")
    st1 = sem(nc, "st1")
    xl0 = sem(nc, "xl0")
    xl1 = sem(nc, "xl1")

    with nc.psum_tensor("ps_al", [128, 512], F32) as ps_al, \
         nc.sbuf_tensor("wld", [128, 2048], F32) as wld, \
         nc.sbuf_tensor("absb", [128, 2048], F32) as absb:
        with nc.Block() as b:
            @b.vector
            def _(vector):
                vector.memset(ones[:], 1.0)

            @b.sync
            def _(sync):
                sync.dma_start(out=bcol[:, 0:NFC], in_=b1c).then_inc(ld, 16)
                sync.dma_start(out=bcol[:, NFC:], in_=b2c).then_inc(ld, 16)
                sync.wait_ge(ld, 32)

        nwl = 0
        for c in range(ND):
            for g2 in range(2):
                nwl += 1
                with nc.Block() as b:
                    @b.sync
                    def _(sync):
                        sync.dma_start(out=wld[:],
                                       in_=w1T[c * 128:(c + 1) * 128, g2 * 2048:(g2 + 1) * 2048]).then_inc(wl, 16)
                        sync.wait_ge(wl, 16 * nwl)
                with nc.Block() as b:
                    @b.scalar
                    def _(scalar):
                        scalar.activation(out=absb[:], in_=wld[:], func=AF.Abs)
                with nc.Block() as b:
                    @b.scalar
                    def _(scalar):
                        scalar.activation(out=sign1[:, c * FF + g2 * 2048:c * FF + (g2 + 1) * 2048],
                                          in_=wld[:], func=AF.Sign)

                    @b.tensor
                    def _(tensor):
                        for hh in range(16):
                            hc = g2 * 16 + hh
                            tensor.matmul(ps_al[:, hc * 8 + c:hc * 8 + c + 1],
                                          absb[:, hh * 128:(hh + 1) * 128], ones[:],
                                          start=True, stop=True)
        for hc in range(NFC):
            nwl += 1
            with nc.Block() as b:
                @b.sync
                def _(sync):
                    sync.dma_start(out=wld[:, 0:D],
                                   in_=w2T[hc * 128:(hc + 1) * 128, :]).then_inc(wl, 16)
                    sync.wait_ge(wl, 16 * nwl)
            with nc.Block() as b:
                @b.scalar
                def _(scalar):
                    scalar.activation(out=absb[:, 0:D], in_=wld[:, 0:D], func=AF.Abs)
            with nc.Block() as b:
                @b.scalar
                def _(scalar):
                    scalar.activation(out=sign2[:, hc * D:(hc + 1) * D],
                                      in_=wld[:, 0:D], func=AF.Sign)

                @b.tensor
                def _(tensor):
                    for dm in range(ND):
                        tensor.matmul(ps_al[:, 256 + dm * 32 + hc:256 + dm * 32 + hc + 1],
                                      absb[:, dm * 128:(dm + 1) * 128], ones[:],
                                      start=True, stop=True)
        with nc.Block() as b:
            @b.vector
            def _(vector):
                vector.tensor_reduce(out=acol[:, 0:NFC],
                                     in_=ps_al[:, 0:256].rearrange("p (a b) -> p a b", a=NFC),
                                     axis=AX.X, op=ALU.add)
        with nc.Block() as b:
            @b.vector
            def _(vector):
                vector.tensor_reduce(out=acol[:, NFC:NFC + ND],
                                     in_=ps_al[:, 256:512].rearrange("p (a b) -> p a b", a=ND),
                                     axis=AX.X, op=ALU.add)
        with nc.Block() as b:
            @b.vector
            def _(vector):
                vector.tensor_scalar_mul(acol[:, 0:NFC], acol[:, 0:NFC], 1.0 / D)
        with nc.Block() as b:
            @b.vector
            def _(vector):
                vector.tensor_scalar_mul(acol[:, NFC:NFC + ND], acol[:, NFC:NFC + ND], 1.0 / FF)

    with nc.psum_tensor("ps_m", [128, 2 * 512], F32) as ps_m:
        pe_n = 0
        ev_n = 0
        st_n = 0
        with nc.Block() as b:
            evacs = []

            @b.sync
            def _(sync):
                for tc in range(4):
                    xls = xl0 if tc % 2 == 0 else xl1
                    if tc >= 2:
                        sync.wait_ge(pe, (tc - 2) * (NFC + ND) + NFC)
                    for c in range(ND):
                        sync.dma_start(
                            out=xnt[:, (tc % 2) * ND * 512 + c * 512:(tc % 2) * ND * 512 + (c + 1) * 512],
                            in_=xn[c * 128:(c + 1) * 128, tc * 512:(tc + 1) * 512]).then_inc(xls, 16)

            kindseq = (["h"] * NFC + ["o"] * ND) * 4
            nh_pre = [0]
            no_pre = [0]
            for kk in kindseq:
                nh_pre.append(nh_pre[-1] + (kk == "h"))
                no_pre.append(no_pre[-1] + (kk == "o"))
            ov = sem(nc, "ov")

            @b.tensor
            def _(tensor):
                nonlocal pe_n
                for tc in range(4):
                    xls = xl0 if tc % 2 == 0 else xl1
                    tensor.wait_ge(xls, 16 * ND * (tc // 2 + 1))
                    for m in range(NFC):
                        bank = (tc * (NFC + ND) + m) % 2
                        grp = tc * (NFC + ND) + m
                        if grp >= 2:
                            tensor.wait_ge(ev, nh_pre[grp - 1])
                            if no_pre[grp - 1]:
                                tensor.wait_ge(ov, no_pre[grp - 1])
                        for c in range(ND):
                            mm = tensor.matmul(ps_m[:, bank * 512:(bank + 1) * 512],
                                               sign1[:, c * FF + m * 128:c * FF + (m + 1) * 128],
                                               xnt[:, (tc % 2) * ND * 512 + c * 512:(tc % 2) * ND * 512 + (c + 1) * 512],
                                               start=(c == 0), stop=(c == ND - 1))
                        pe_n += 1
                        mm.then_inc(pe, 1)
                        evacs.append(("h", tc, m, pe_n, bank))
                    for dm in range(ND):
                        bank = (tc * (NFC + ND) + NFC + dm) % 2
                        grp = tc * (NFC + ND) + NFC + dm
                        tensor.wait_ge(ev, max(nh_pre[grp - 1], (tc + 1) * NFC))
                        if no_pre[max(grp - 1, 0)]:
                            tensor.wait_ge(ov, no_pre[grp - 1])
                        for hc in range(NFC):
                            mm = tensor.matmul(ps_m[:, bank * 512:(bank + 1) * 512],
                                               sign2[:, hc * D + dm * 128:hc * D + (dm + 1) * 128],
                                               hbuf[:, hc * 512:(hc + 1) * 512],
                                               start=(hc == 0), stop=(hc == NFC - 1))
                        pe_n += 1
                        mm.then_inc(pe, 1)
                        evacs.append(("o", tc, dm, pe_n, bank))

            @b.scalar
            def _(scalar):
                nonlocal ev_n
                for kind, tc, idx, pe_target, bank in evacs:
                    if kind != "h":
                        continue
                    scalar.wait_ge(pe, pe_target)
                    scalar.activation(out=hbuf[:, idx * 512:(idx + 1) * 512],
                                      in_=ps_m[:, bank * 512:(bank + 1) * 512],
                                      func=AF.Gelu,
                                      bias=bcol[:, idx:idx + 1],
                                      scale=acol[:, idx:idx + 1]).then_inc(ev, 1)
                    ev_n += 1

            @b.vector
            def _(vector):
                for kind, tc, idx, pe_target, bank in evacs:
                    if kind != "o":
                        continue
                    os_ = (tc * ND + idx) % 2
                    ssem = st0 if os_ == 0 else st1
                    nprior = (tc * ND + idx) // 2
                    if nprior > 0:
                        vector.wait_ge(ssem, 16 * nprior)
                    vector.wait_ge(pe, pe_target)
                    vector.tensor_scalar(out=otile[:, os_ * 512:(os_ + 1) * 512],
                                         in0=ps_m[:, bank * 512:(bank + 1) * 512],
                                         scalar1=acol[:, NFC + idx:NFC + idx + 1],
                                         scalar2=bcol[:, NFC + idx:NFC + idx + 1],
                                         op0=ALU.mult, op1=ALU.add).then_inc(ov, 1)

            @b.gpsimd
            def _(gpsimd):
                n_out = 0
                for ei, (kind, tc, idx, pe_target, bank) in enumerate(evacs):
                    if kind != "o":
                        continue
                    os_ = (tc * ND + idx) % 2
                    ssem = st0 if os_ == 0 else st1
                    gpsimd.wait_ge(ov, no_pre[ei + 1])
                    gpsimd.dma_start(out=f0[idx * 128:(idx + 1) * 128, tc * 512:(tc + 1) * 512],
                                     in_=otile[:, os_ * 512:(os_ + 1) * 512]).then_inc(ssem, 16)
                    n_out += 1
                gpsimd.wait_ge(st0, 16 * ((n_out + 1) // 2))
                gpsimd.wait_ge(st1, 16 * (n_out // 2))




def build_l0ffn():
    nc = bass.Bass("TRN2", target_bir_lowering=False, debug=False)
    xn = nc.dram_tensor("xn", [D, L], BF16, kind="ExternalInput").ap()
    w1T = nc.dram_tensor("w1T", [D, FF], F32, kind="ExternalInput").ap()
    w2T = nc.dram_tensor("w2T", [FF, D], F32, kind="ExternalInput").ap()
    b1c = nc.dram_tensor("b1c", [128, NFC], F32, kind="ExternalInput").ap()
    b2c = nc.dram_tensor("b2c", [128, ND], F32, kind="ExternalInput").ap()
    f0 = nc.dram_tensor("f0", [D, L], F32, kind="ExternalOutput").ap()
    emit_l0ffn(nc, xn, w1T, w2T, b1c, b2c, f0)
    return nc

def emit_res_ln(nc, x, a0, f0, x1, xn):


    xt = sb(nc, "xt", [128, ND * L], F32)
    sq = sb(nc, "sq", [128, ND * L], F32)
    ones = sb(nc, "ones", [128, 1], F32)
    onesb = sb(nc, "onesb", [1, 128], F32)
    mrow = sb(nc, "mrow", [1, L], F32)
    qrow = sb(nc, "qrow", [1, L], F32)
    ve = sb(nc, "ve", [1, L], F32)
    rrow = sb(nc, "rrow", [1, L], F32)
    tmp = sb(nc, "tmp", [128, 2 * 1024], F32)
    xnt = sb(nc, "xnt", [128, 2 * 1024], BF16)
    ps_s = ps(nc, "ps_s", [1, 2048], F32)
    ps_b = ps(nc, "ps_b", [128, 2048], F32)
    ld = sem(nc, "ld")
    ap = sem(nc, "ap")
    st = sem(nc, "st")
    st0 = sem(nc, "st0")
    st1 = sem(nc, "st1")

    with nc.Block() as b:
        @b.sync
        def _(sync):
            for c in range(ND):
                sync.dma_start(out=xt[:, c * L:(c + 1) * L],
                               in_=x[c * 128:(c + 1) * 128, :]).then_inc(ld, 16)
            for c in range(ND):
                sync.dma_start(out=sq[:, c * L:(c + 1) * L],
                               in_=a0[c * 128:(c + 1) * 128, :]).then_inc(ld, 16)
            sync.wait_ge(ld, 16 * 2 * ND)

        @b.vector
        def _(vector):
            vector.memset(ones[:], 1.0)
            vector.memset(onesb[:], 1.0)
    with nc.Block() as b:
        @b.vector
        def _(vector):
            for c in range(ND):
                vector.tensor_add(xt[:, c * L:(c + 1) * L], xt[:, c * L:(c + 1) * L],
                                  sq[:, c * L:(c + 1) * L])
    with nc.Block() as b:
        @b.sync
        def _(sync):
            for c in range(ND):
                sync.dma_start(out=sq[:, c * L:(c + 1) * L],
                               in_=f0[c * 128:(c + 1) * 128, :]).then_inc(ld, 16)
            sync.wait_ge(ld, 16 * 3 * ND)
    with nc.Block() as b:
        @b.vector
        def _(vector):
            for c in range(ND):
                vector.tensor_add(xt[:, c * L:(c + 1) * L], xt[:, c * L:(c + 1) * L],
                                  sq[:, c * L:(c + 1) * L])
    with nc.Block() as b:
        @b.sync
        def _(sync):
            for c in range(ND):
                sync.dma_start(out=x1[c * 128:(c + 1) * 128, :],
                               in_=xt[:, c * L:(c + 1) * L]).then_inc(st, 16)
            sync.wait_ge(st, 16 * ND)

    with nc.Block() as b:
        @b.tensor
        def _(tensor):
            for n in range(4):
                for c in range(ND):
                    tensor.matmul(ps_s[:, n * 512:(n + 1) * 512], ones[:],
                                  xt[:, c * L + n * 512:c * L + (n + 1) * 512],
                                  start=(c == 0), stop=(c == ND - 1))
    with nc.Block() as b:
        @b.vector
        def _(vector):
            vector.tensor_scalar_mul(mrow[:], ps_s[:], 1.0 / D)
    with nc.Block() as b:
        @b.scalar
        def _(scalar):
            for c in range(ND):
                scalar.activation(out=sq[:, c * L:(c + 1) * L],
                                  in_=xt[:, c * L:(c + 1) * L], func=AF.Square)
    with nc.Block() as b:
        @b.tensor
        def _(tensor):
            for n in range(4):
                for c in range(ND):
                    tensor.matmul(ps_s[:, n * 512:(n + 1) * 512], ones[:],
                                  sq[:, c * L + n * 512:c * L + (n + 1) * 512],
                                  start=(c == 0), stop=(c == ND - 1))
    with nc.Block() as b:
        @b.vector
        def _(vector):
            vector.tensor_scalar_mul(qrow[:], ps_s[:], 1.0 / D)
    with nc.Block() as b:
        @b.vector
        def _(vector):
            vector.scalar_tensor_tensor(out=ve[:], in0=mrow[:], scalar=-1.0,
                                        in1=mrow[:], op0=ALU.mult, op1=ALU.mult)
    with nc.Block() as b:
        @b.vector
        def _(vector):
            vector.tensor_add(ve[:], ve[:], qrow[:])
    with nc.Block() as b:
        @b.vector
        def _(vector):
            vector.tensor_scalar_add(ve[:], ve[:], 1e-5)
    with nc.Block() as b:
        @b.scalar
        def _(scalar):
            scalar.activation(out=ve[:], in_=ve[:], func=AF.Ln)
    with nc.Block() as b:
        @b.scalar
        def _(scalar):
            scalar.activation(out=rrow[:], in_=ve[:], func=AF.Exp, scale=-0.5)
    for g in range(2):
        with nc.Block() as b:
            @b.tensor
            def _(tensor):
                for n in range(2):
                    tensor.matmul(ps_b[:, n * 512:(n + 1) * 512], onesb[:],
                                  mrow[:, g * 1024 + n * 512:g * 1024 + (n + 1) * 512],
                                  start=True, stop=True)
                    tensor.matmul(ps_b[:, 1024 + n * 512:1024 + (n + 1) * 512], onesb[:],
                                  rrow[:, g * 1024 + n * 512:g * 1024 + (n + 1) * 512],
                                  start=True, stop=True)
        with nc.Block() as b:
            @b.vector
            def _(vector):
                for c in range(ND):
                    s = c % 2
                    ssem = st0 if s == 0 else st1
                    nprior = (g * ND + c) // 2
                    if nprior > 0:
                        vector.wait_ge(ssem, 16 * nprior)
                    vector.tensor_sub(tmp[:, s * 1024:(s + 1) * 1024],
                                      xt[:, c * L + g * 1024:c * L + (g + 1) * 1024],
                                      ps_b[:, 0:1024]).then_inc(ap, 1)
                    vector.wait_ge(ap, g * 2 * ND + 2 * c + 1)
                    vector.tensor_mul(xnt[:, s * 1024:(s + 1) * 1024],
                                      tmp[:, s * 1024:(s + 1) * 1024],
                                      ps_b[:, 1024:2048]).then_inc(ap, 1)

            @b.gpsimd
            def _(gpsimd):
                for c in range(ND):
                    s = c % 2
                    ssem = st0 if s == 0 else st1
                    gpsimd.wait_ge(ap, g * 2 * ND + 2 * c + 2)
                    gpsimd.dma_start(out=xn[c * 128:(c + 1) * 128, g * 1024:(g + 1) * 1024],
                                     in_=xnt[:, s * 1024:(s + 1) * 1024]).then_inc(ssem, 16)
    with nc.Block() as b:
        @b.sync
        def _(sync):
            sync.wait_ge(st0, 16 * ND)
            sync.wait_ge(st1, 16 * ND)




def build_res_ln():
    nc = bass.Bass("TRN2", target_bir_lowering=False, debug=False)
    x = nc.dram_tensor("x", [D, L], F32, kind="ExternalInput").ap()
    a0 = nc.dram_tensor("a0", [D, L], F32, kind="ExternalInput").ap()
    f0 = nc.dram_tensor("f0", [D, L], F32, kind="ExternalInput").ap()
    x1 = nc.dram_tensor("x1", [D, L], F32, kind="ExternalOutput").ap()
    xn = nc.dram_tensor("xn", [D, L], BF16, kind="ExternalOutput").ap()
    emit_res_ln(nc, x, a0, f0, x1, xn)
    return nc

def emit_res2(nc, x1, a1, f1, out):

    xt = sb(nc, "xt", [128, ND * L], F32)
    yt = sb(nc, "yt", [128, ND * L], F32)
    ld = sem(nc, "ld")
    st = sem(nc, "st")
    with nc.Block() as b:
        @b.sync
        def _(sync):
            for c in range(ND):
                sync.dma_start(out=xt[:, c * L:(c + 1) * L],
                               in_=x1[c * 128:(c + 1) * 128, :]).then_inc(ld, 16)
            for c in range(ND):
                sync.dma_start(out=yt[:, c * L:(c + 1) * L],
                               in_=a1[c * 128:(c + 1) * 128, :]).then_inc(ld, 16)
            sync.wait_ge(ld, 16 * 2 * ND)
    with nc.Block() as b:
        @b.vector
        def _(vector):
            for c in range(ND):
                vector.tensor_add(xt[:, c * L:(c + 1) * L], xt[:, c * L:(c + 1) * L],
                                  yt[:, c * L:(c + 1) * L])
    with nc.Block() as b:
        @b.sync
        def _(sync):
            for c in range(ND):
                sync.dma_start(out=yt[:, c * L:(c + 1) * L],
                               in_=f1[c * 128:(c + 1) * 128, :]).then_inc(ld, 16)
            sync.wait_ge(ld, 16 * 3 * ND)
    with nc.Block() as b:
        @b.vector
        def _(vector):
            for c in range(ND):
                vector.tensor_add(xt[:, c * L:(c + 1) * L], xt[:, c * L:(c + 1) * L],
                                  yt[:, c * L:(c + 1) * L])
    with nc.Block() as b:
        @b.sync
        def _(sync):
            for c in range(ND):
                sync.dma_start(out=out[c * 128:(c + 1) * 128, :],
                               in_=xt[:, c * L:(c + 1) * L]).then_inc(st, 16)
            sync.wait_ge(st, 16 * ND)




def build_res2():
    nc = bass.Bass("TRN2", target_bir_lowering=False, debug=False)
    x1 = nc.dram_tensor("x1", [D, L], F32, kind="ExternalInput").ap()
    a1 = nc.dram_tensor("a1", [D, L], F32, kind="ExternalInput").ap()
    f1 = nc.dram_tensor("f1", [D, L], F32, kind="ExternalInput").ap()
    out = nc.dram_tensor("out", [D, L], F32, kind="ExternalOutput").ap()
    emit_res2(nc, x1, a1, f1, out)
    return nc

def emit_l1ffn(nc, xn, fc1A, fc1B, fc2A, fc2B, lg1, lg2, riota1, riota2, f1):

    s_a1 = nc.dram_tensor("s_a1", [32, 128, 64], BF16, kind="Internal").ap()
    s_a2 = nc.dram_tensor("s_a2", [32, 512, 64], BF16, kind="Internal").ap()
    s_b1 = nc.dram_tensor("s_b1", [32, 64, FF], BF16, kind="Internal").ap()
    w1s = nc.dram_tensor("w1s", [D, FF], BF16, kind="Internal",
                         addr_space="Shared").ap()
    w2s = nc.dram_tensor("w2s", [FF, D], BF16, kind="Internal",
                         addr_space="Shared").ap()
    s_a2t = nc.dram_tensor("s_a2t", [32, 64, 512], BF16, kind="Internal").ap()
    w1loc = nc.dram_tensor("w1loc", [128, FF], BF16, kind="Internal").ap()
    w2loc = nc.dram_tensor("w2loc", [512, D], BF16, kind="Internal").ap()

    GEC = 1.5957691216
    GEB = 0.044715

    lgw = sb(nc, "lgw", [1, 2 * 256], F32)
    mval = sb(nc, "mval", [1, 2 * 32], F32)
    mv8 = sb(nc, "mv8", [1, 8], F32)
    mid = sb(nc, "mid", [1, 2 * 32], U32)
    midf = sb(nc, "midf", [1, 2 * 32], F32)
    wv = sb(nc, "wv", [1, 2 * 32], F32)
    selbc = sb(nc, "selbc", [128, 2 * 32], F32)
    wvbc = sb(nc, "wvbc", [128, 2 * 32], F32)
    onesb = sb(nc, "onesb", [1, 128], F32)
    iota_f = sb(nc, "iota_f", [128, 1], F32)
    riota1s = sb(nc, "riota1s", [128, 1], F32)
    riota2s = sb(nc, "riota2s", [128, 1], F32)
    ident = sb(nc, "ident", [128, 128], BF16)
    idxu = sb(nc, "idxu", [128, 8], U32)
    idxf = sb(nc, "idxf", [128, 8], F32)
    ssc = sb(nc, "ssc", [1, 8], F32)

    ld = sem(nc, "ld")
    gat = sem(nc, "gat")
    sto = sem(nc, "sto")
    ap = sem(nc, "ap")
    ini = sem(nc, "ini")

    with nc.Block() as b:
        @b.sync
        def _(sync):
            sync.dma_start(out=lgw[:, 0:256], in_=lg1).then_inc(ld, 16)
            sync.dma_start(out=lgw[:, 256:512], in_=lg2).then_inc(ld, 16)
            sync.dma_start(out=riota1s[:], in_=riota1).then_inc(ld, 16)
            sync.dma_start(out=riota2s[:], in_=riota2).then_inc(ld, 16)
            sync.wait_ge(ld, 64)

        @b.gpsimd
        def _(gpsimd):
            gpsimd.iota(iota_f[:], [[0, 1]], base=0, channel_multiplier=1,
                        allow_small_or_imprecise_dtypes=True)
            gpsimd.memset(ident[:], 0.0).then_inc(ini, 1)
            gpsimd.wait_ge(ini, 1)
            gpsimd.affine_select(out=ident[:], in_=ident[:],
                                 compare_op=ALU.not_equal, fill=1.0, base=0,
                                 pattern=[[-1, 128]], channel_multiplier=1)

        @b.vector
        def _(vector):
            vector.memset(onesb[:], 1.0)
    for fi in range(2):
        for r in range(4):
            with nc.Block() as b:
                @b.vector
                def _(vector):
                    vector.max(out=mv8[:], in_=lgw[:, fi * 256:(fi + 1) * 256])
            with nc.Block() as b:
                @b.vector
                def _(vector):
                    vector.max_index(out=mid[:, fi * 32 + r * 8:fi * 32 + (r + 1) * 8],
                                     in_max=mv8[:], in_values=lgw[:, fi * 256:(fi + 1) * 256])
            with nc.Block() as b:
                @b.vector
                def _(vector):
                    vector.tensor_copy(out=mval[:, fi * 32 + r * 8:fi * 32 + (r + 1) * 8],
                                       in_=mv8[:])
            with nc.Block() as b:
                @b.vector
                def _(vector):
                    vector.match_replace(out=lgw[:, fi * 256:(fi + 1) * 256],
                                         in_to_replace=mv8[:],
                                         in_values=lgw[:, fi * 256:(fi + 1) * 256],
                                         imm_value=-1e30)
        with nc.Block() as b:
            @b.vector
            def _(vector):
                vector.tensor_scalar_mul(ssc[:, 0:1], mval[:, fi * 32:fi * 32 + 1], -1.0)
        with nc.Block() as b:
            @b.scalar
            def _(scalar):
                scalar.activation(out=wv[:, fi * 32:(fi + 1) * 32],
                                  in_=mval[:, fi * 32:(fi + 1) * 32],
                                  func=AF.Exp, bias=ssc[:, 0:1])
        with nc.Block() as b:
            @b.vector
            def _(vector):
                vector.reduce_sum(out=ssc[:, 1:2], in_=wv[:, fi * 32:(fi + 1) * 32],
                                  axis=AX.X)
        with nc.Block() as b:
            @b.vector
            def _(vector):
                vector.reciprocal(out=ssc[:, 2:3], in_=ssc[:, 1:2])
        with nc.Block() as b:
            @b.vector
            def _(vector):
                vector.tensor_scalar_mul(wv[:, fi * 32:(fi + 1) * 32],
                                         wv[:, fi * 32:(fi + 1) * 32], ssc[:, 2:3])
        with nc.Block() as b:
            @b.vector
            def _(vector):
                vector.tensor_copy(out=midf[:, fi * 32:(fi + 1) * 32],
                                   in_=mid[:, fi * 32:(fi + 1) * 32])
    with nc.psum_tensor("ps_bc", [128, 128], F32) as ps_bc:
        with nc.Block() as b:
            @b.tensor
            def _(tensor):
                tensor.matmul(ps_bc[:, 0:64], onesb[:], midf[:], start=True, stop=True)
                tensor.matmul(ps_bc[:, 64:128], onesb[:], wv[:], start=True, stop=True)
        with nc.Block() as b:
            @b.vector
            def _(vector):
                vector.tensor_copy(out=selbc[:], in_=ps_bc[:, 0:64])
        with nc.Block() as b:
            @b.vector
            def _(vector):
                vector.tensor_copy(out=wvbc[:], in_=ps_bc[:, 64:128])

    cm_b2 = nc.sbuf_tensor("b2buf", [64, 32 * D], BF16)
    b2buf = cm_b2.__enter__()
    specs = []
    for k in range(32):
        specs.append(("b1", k))
        specs.append(("a2", k))
        specs.append(("a1", k))
        specs.append(("b2", k))
    STO = {"b1": 2, "a2": 1, "a1": 1, "b2": 0}
    slot_pfx = {0: [0], 1: [0]}
    b2_pfx = [0]
    for gi, (kind, _) in enumerate(specs):
        for sl in (0, 1):
            slot_pfx[sl].append(slot_pfx[sl][-1] + (STO[kind] if gi % 2 == sl else 0))
        b2_pfx.append(b2_pfx[-1] + (1 if kind == "b2" else 0))
    sto2 = [sem(nc, "sto_s0"), sem(nc, "sto_s1")]
    gat2 = [sem(nc, "gat_s0"), sem(nc, "gat_s1")]
    b2e = sem(nc, "b2e")

    with nc.sbuf_tensor("gbuf", [128, 2 * FF], F32) as gbuf:
        with nc.Block() as b:
            @b.vector
            def _(vector):
                for gi, (kind, k) in enumerate(specs):
                    fi = 0 if kind in ("b1", "a1") else 1
                    rk = {"b1": 64.0, "a2": 128.0, "a1": 32.0, "b2": 64.0}[kind]
                    si = gi % 8
                    if gi >= 8:
                        vector.wait_ge(gat2[gi % 2], 16 * ((gi - 8) // 2 + 1))
                    riot = (riota1s if kind == "a1" else
                            riota2s if kind == "a2" else iota_f)
                    vector.scalar_tensor_tensor(out=idxf[:, si:si + 1],
                                                in0=selbc[:, fi * 32 + k:fi * 32 + k + 1],
                                                scalar=rk, in1=riot[:],
                                                op0=ALU.mult, op1=ALU.add).then_inc(ap, 1)
                    vector.wait_ge(ap, 2 * gi + 1)
                    vector.tensor_copy(out=idxu[:, si:si + 1],
                                       in_=idxf[:, si:si + 1]).then_inc(ap, 1)

            @b.gpsimd
            def _(gpsimd):
                for gi, (kind, k) in enumerate(specs):
                    si = gi % 8
                    off = (gi % 2) * FF
                    gpsimd.wait_ge(ap, 2 * gi + 2)
                    if gi >= 2:
                        gpsimd.wait_ge(sto2[gi % 2], 16 * slot_pfx[gi % 2][gi - 1])
                        gpsimd.wait_ge(b2e, b2_pfx[gi - 1])
                    if kind == "b1":
                        gpsimd.indirect_dma_start(
                            out=gbuf[0:64, off:off + FF], out_offset=None,
                            in_=fc1B.rearrange("p r f -> (p r) f"),
                            in_offset=bass.IndirectOffsetOnAxis(ap=idxu[0:64, si:si + 1], axis=0),
                        ).then_inc(gat2[gi % 2], 16)
                        gpsimd.wait_ge(gat2[gi % 2], 16 * (gi // 2 + 1))
                        for hh in range(2):
                            gpsimd.dma_start(
                                out=s_b1[k][:, hh * 2048:(hh + 1) * 2048],
                                in_=gbuf[0:64, off + hh * 2048:off + (hh + 1) * 2048],
                            ).then_inc(sto2[gi % 2], 16)
                    elif kind == "a2":
                        gpsimd.indirect_dma_start(
                            out=gbuf[0:16, off:off + 2048], out_offset=None,
                            in_=fc2A.rearrange("p h r -> (p h r)").rearrange("(x y) -> x y", y=2048),
                            in_offset=bass.IndirectOffsetOnAxis(ap=idxu[0:16, si:si + 1], axis=0),
                        ).then_inc(gat2[gi % 2], 16)
                        gpsimd.wait_ge(gat2[gi % 2], 16 * (gi // 2 + 1))
                        gpsimd.dma_start(
                            out=s_a2[k].rearrange("h r -> (h r)").rearrange("(x y) -> x y", y=2048),
                            in_=gbuf[0:16, off:off + 2048]).then_inc(sto2[gi % 2], 16)
                    elif kind == "a1":
                        gpsimd.indirect_dma_start(
                            out=gbuf[0:4, off:off + 2048], out_offset=None,
                            in_=fc1A.rearrange("p d r -> (p d r)").rearrange("(x y) -> x y", y=2048),
                            in_offset=bass.IndirectOffsetOnAxis(ap=idxu[0:4, si:si + 1], axis=0),
                        ).then_inc(gat2[gi % 2], 16)
                        gpsimd.wait_ge(gat2[gi % 2], 16 * (gi // 2 + 1))
                        gpsimd.dma_start(
                            out=s_a1[k].rearrange("d r -> (d r)").rearrange("(x y) -> x y", y=2048),
                            in_=gbuf[0:4, off:off + 2048]).then_inc(sto2[gi % 2], 16)
                    else:
                        gpsimd.indirect_dma_start(
                            out=gbuf[0:64, off:off + D], out_offset=None,
                            in_=fc2B.rearrange("p r d -> (p r) d"),
                            in_offset=bass.IndirectOffsetOnAxis(ap=idxu[0:64, si:si + 1], axis=0),
                        ).then_inc(gat2[gi % 2], 16)

            @b.scalar
            def _(scalar):
                for gi, (kind, k) in enumerate(specs):
                    if kind != "b2":
                        continue
                    off = (gi % 2) * FF
                    scalar.wait_ge(gat2[gi % 2], 16 * (gi // 2 + 1))
                    scalar.activation(out=b2buf[:, k * D:(k + 1) * D],
                                      in_=gbuf[0:64, off:off + D],
                                      func=AF.Identity,
                                      scale=wvbc[0:64, 32 + k:32 + k + 1]).then_inc(b2e, 1)

    a1l = sem(nc, "a1l")
    t2l = [sem(nc, "t2l0"), sem(nc, "t2l1")]
    t2p = sem(nc, "t2p")
    t2e = sem(nc, "t2e")
    t2s = sem(nc, "t2s")
    b1l = [sem(nc, "b1l0"), sem(nc, "b1l1")]
    cpe = sem(nc, "cpe")
    cev = sem(nc, "cev")
    cst = [sem(nc, "cst0"), sem(nc, "cst1")]
    cm_a1 = nc.sbuf_tensor("a1t", [64, 32 * 128], BF16)
    a1t = cm_a1.__enter__()
    with nc.sbuf_tensor("a2ld", [128, 2 * 2048], BF16) as a2ld, \
         nc.sbuf_tensor("a2t", [64, 512], BF16) as a2t, \
         nc.psum_tensor("ps_t2", [64, 512], BF16) as ps_t2:
        with nc.Block() as b:
            @b.sync
            def _(sync):
                sync.wait_ge(sto2[0], 16 * slot_pfx[0][-1])
                sync.wait_ge(sto2[1], 16 * slot_pfx[1][-1])
                sync.wait_ge(b2e, b2_pfx[-1])
                for k in range(32):
                    sl = k % 2
                    if k >= 2:
                        sync.wait_ge(t2p, k - 1)
                    sync.dma_start(
                        out=a2ld[:, sl * 2048:sl * 2048 + 256].rearrange("p (c r) -> p c r", c=4),
                        in_=s_a2[k].rearrange("(c p) r -> p c r", p=128)).then_inc(t2l[sl], 16)

            @b.tensor
            def _(tensor):
                for k in range(32):
                    sl = k % 2
                    tensor.wait_ge(t2l[sl], 16 * (k // 2 + 1))
                    tensor.wait_ge(t2e, k)
                    for c in range(4):
                        mm = tensor.transpose(ps_t2[:, c * 128:(c + 1) * 128],
                                              a2ld[:, sl * 2048 + c * 64:sl * 2048 + (c + 1) * 64],
                                              ident[:])
                    mm.then_inc(t2p, 1)

            @b.vector
            def _(vector):
                for k in range(32):
                    vector.wait_ge(t2p, k + 1)
                    if k >= 1:
                        vector.wait_ge(t2s, 16 * k)
                    vector.tensor_copy(out=a2t[:], in_=ps_t2[:]).then_inc(t2e, 1)

            @b.gpsimd
            def _(gpsimd):
                for k in range(32):
                    gpsimd.wait_ge(t2e, k + 1)
                    gpsimd.dma_start(out=s_a2t[k], in_=a2t[:]).then_inc(t2s, 16)
                gpsimd.wait_ge(t2s, 16 * 32)

        with nc.Block() as b:
            @b.sync
            def _(sync):
                for k in range(32):
                    sl = k % 2
                    if k >= 2:
                        sync.wait_ge(t2p, 32 + k - 1)
                    sync.dma_start(
                        out=a2ld[:, sl * 2048:sl * 2048 + 64],
                        in_=s_a1[k]).then_inc(t2l[sl], 16)

            @b.tensor
            def _(tensor):
                for k in range(32):
                    sl = k % 2
                    tensor.wait_ge(t2l[sl], 16 * (16 + k // 2 + 1))
                    tensor.wait_ge(t2e, 32 + k)
                    mm = tensor.transpose(ps_t2[:, 0:128],
                                          a2ld[:, sl * 2048:sl * 2048 + 64],
                                          ident[:])
                    mm.then_inc(t2p, 1)

            @b.vector
            def _(vector):
                for k in range(32):
                    vector.wait_ge(t2p, 32 + k + 1)
                    vector.tensor_copy(out=a1t[:, k * 128:(k + 1) * 128],
                                       in_=ps_t2[:, 0:128]).then_inc(t2e, 1)

    with nc.sbuf_tensor("b1s", [64, 2 * 32 * 512], BF16) as b1s, \
         nc.sbuf_tensor("wtile", [128, 2 * 512], BF16) as wtile, \
         nc.psum_tensor("ps_c", [128, 2 * 512], F32) as ps_c:
        with nc.Block() as b:
            @b.vector
            def _(vector):
                for k in range(32):
                    vector.tensor_scalar_mul(a1t[:, k * 128:(k + 1) * 128],
                                             a1t[:, k * 128:(k + 1) * 128],
                                             wvbc[0:64, k:k + 1])
        with nc.Block() as b:
            @b.sync
            def _(sync):
                for fch in range(8):
                    sl = fch % 2
                    if fch >= 2:
                        sync.wait_ge(cpe, fch - 1)
                    for k in range(32):
                        sync.dma_start(
                            out=b1s[:, sl * 32 * 512 + k * 512:sl * 32 * 512 + (k + 1) * 512],
                            in_=s_b1[k][:, fch * 512:(fch + 1) * 512]).then_inc(b1l[sl], 16)

            @b.tensor
            def _(tensor):
                for fch in range(8):
                    sl = fch % 2
                    tensor.wait_ge(b1l[sl], 16 * 32 * (fch // 2 + 1))
                    grp = fch
                    bank = grp % 2
                    if grp >= 2:
                        tensor.wait_ge(cev, grp - 1)
                    for k in range(32):
                        mm = tensor.matmul(ps_c[:, bank * 512:(bank + 1) * 512],
                                           a1t[:, k * 128:(k + 1) * 128],
                                           b1s[:, sl * 32 * 512 + k * 512:sl * 32 * 512 + (k + 1) * 512],
                                           start=(k == 0), stop=(k == 31))
                    mm.then_inc(cpe, 1)

            @b.scalar
            def _(scalar):
                for grp in range(8):
                    bank = grp % 2
                    scalar.wait_ge(cpe, grp + 1)
                    if grp >= 2:
                        scalar.wait_ge(cst[bank], 16 * (grp // 2))
                    scalar.activation(out=wtile[:, bank * 512:(bank + 1) * 512],
                                      in_=ps_c[:, bank * 512:(bank + 1) * 512],
                                      func=AF.Identity).then_inc(cev, 1)

            @b.gpsimd
            def _(gpsimd):
                for grp in range(8):
                    fch = grp
                    bank = grp % 2
                    gpsimd.wait_ge(cev, grp + 1)
                    gpsimd.dma_start(out=w1loc[:, fch * 512:(fch + 1) * 512],
                                     in_=wtile[:, bank * 512:(bank + 1) * 512]).then_inc(cst[bank], 16)

        with nc.Block() as b:
            @b.sync
            def _(sync):
                for hc in range(4):
                    sl = hc % 2
                    if hc >= 2:
                        sync.wait_ge(cpe, 8 + (hc - 1) * 2)
                    for k in range(32):
                        sync.dma_start(
                            out=b1s[:, sl * 32 * 512 + k * 128:sl * 32 * 512 + k * 128 + 128],
                            in_=s_a2t[k][:, hc * 128:(hc + 1) * 128]).then_inc(b1l[sl], 16)

            @b.tensor
            def _(tensor):
                for hc in range(4):
                    sl = hc % 2
                    tensor.wait_ge(b1l[sl], 16 * 32 * (4 + hc // 2 + 1))
                    for dg in range(2):
                        grp = 8 + hc * 2 + dg
                        bank = grp % 2
                        tensor.wait_ge(cev, grp - 1)
                        for k in range(32):
                            mm = tensor.matmul(ps_c[:, bank * 512:(bank + 1) * 512],
                                               b1s[:, sl * 32 * 512 + k * 128:sl * 32 * 512 + k * 128 + 128],
                                               b2buf[:, k * D + dg * 512:k * D + (dg + 1) * 512],
                                               start=(k == 0), stop=(k == 31))
                        mm.then_inc(cpe, 1)

            @b.scalar
            def _(scalar):
                for grp in range(8, 16):
                    bank = grp % 2
                    scalar.wait_ge(cpe, grp + 1)
                    scalar.wait_ge(cst[bank], 16 * (grp // 2))
                    scalar.activation(out=wtile[:, bank * 512:(bank + 1) * 512],
                                      in_=ps_c[:, bank * 512:(bank + 1) * 512],
                                      func=AF.Identity).then_inc(cev, 1)

            @b.gpsimd
            def _(gpsimd):
                for grp in range(8, 16):
                    gl = grp - 8
                    hc, dg = divmod(gl, 2)
                    bank = grp % 2
                    gpsimd.wait_ge(cev, grp + 1)
                    gpsimd.dma_start(out=w2loc[hc * 128:(hc + 1) * 128, dg * 512:(dg + 1) * 512],
                                     in_=wtile[:, bank * 512:(bank + 1) * 512]).then_inc(cst[bank], 16)

    cm_a1.__exit__(None, None, None)
    cm_b2.__exit__(None, None, None)

    ccs1 = sem(nc, "ccs1")
    ccs2 = sem(nc, "ccs2")
    with nc.Block() as b:
        @b.gpsimd
        def _(gpsimd):
            gpsimd.wait_ge(cst[0], 16 * 8)
            gpsimd.wait_ge(cst[1], 16 * 8)
            gpsimd.collective_compute(
                "AllGather", ALU.bypass,
                replica_groups=[list(range(N_CORES))],
                ins=[w1loc.opt()], outs=[w1s.opt()]).then_inc(ccs1)
            gpsimd.collective_compute(
                "AllGather", ALU.bypass,
                replica_groups=[list(range(N_CORES))],
                ins=[w2loc.opt()], outs=[w2s.opt()]).then_inc(ccs2)
            gpsimd.wait_ge(ccs1, 1)
            gpsimd.wait_ge(ccs2, 1)

    w1l = sem(nc, "w1l")
    xl = [sem(nc, "xl0"), sem(nc, "xl1")]
    fst = [sem(nc, "fst0"), sem(nc, "fst1")]
    with nc.sbuf_tensor("w1r", [128, ND * FF], BF16) as w1r, \
         nc.sbuf_tensor("w2r", [128, NFC * D], BF16) as w2r, \
         nc.sbuf_tensor("xnt", [128, 2 * ND * 512], BF16) as xnt, \
         nc.sbuf_tensor("hbuf", [128, NFC * 512], BF16) as hbuf, \
         nc.sbuf_tensor("gsc", [128, 3 * 512], F32) as gsc, \
         nc.sbuf_tensor("otile", [128, 2 * 512], F32) as otile, \
         nc.psum_tensor("ps_mf", [128, 2 * 512], F32) as ps_m:
        with nc.Block() as b:
            @b.sync
            def _(sync):
                for c in range(ND):
                    sync.dma_start(out=w1r[:, c * FF:(c + 1) * FF],
                                   in_=w1s[c * 128:(c + 1) * 128, :]).then_inc(w1l, 16)
                for hc in range(NFC):
                    sync.dma_start(out=w2r[:, hc * D:(hc + 1) * D],
                                   in_=w2s[hc * 128:(hc + 1) * 128, :]).then_inc(w1l, 16)
                sync.wait_ge(w1l, 16 * (ND + NFC))
        pe_n = 0
        sqe = sem(nc, "sqe")
        sge = sem(nc, "sge")
        fme = sem(nc, "fme")
        oev = sem(nc, "oev")
        with nc.Block() as b:
            evacs = []

            @b.sync
            def _(sync):
                for tc in range(4):
                    if tc >= 2:
                        sync.wait_ge(cpe, 16 + (tc - 2) * (NFC + ND) + NFC)
                    for c in range(ND):
                        sync.dma_start(
                            out=xnt[:, (tc % 2) * ND * 512 + c * 512:(tc % 2) * ND * 512 + (c + 1) * 512],
                            in_=xn[c * 128:(c + 1) * 128, tc * 512:(tc + 1) * 512]).then_inc(xl[tc % 2], 16)

            @b.tensor
            def _(tensor):
                nonlocal pe_n
                pe_n = 16
                nh = 0
                no = 0
                kinds = []
                for tc in range(4):
                    tensor.wait_ge(xl[tc % 2], 16 * ND * (tc // 2 + 1))
                    for m in range(NFC):
                        grp = tc * (NFC + ND) + m
                        bank = grp % 2
                        if grp >= 2:
                            pk, pi = kinds[grp - 2]
                            tensor.wait_ge(fme if pk == "h" else oev, pi + 1)
                        for c in range(ND):
                            mm = tensor.matmul(ps_m[:, bank * 512:(bank + 1) * 512],
                                               w1r[:, c * FF + m * 128:c * FF + (m + 1) * 128],
                                               xnt[:, (tc % 2) * ND * 512 + c * 512:(tc % 2) * ND * 512 + (c + 1) * 512],
                                               start=(c == 0), stop=(c == ND - 1))
                        pe_n += 1
                        mm.then_inc(cpe, 1)
                        evacs.append(("h", tc, m, pe_n, bank, nh))
                        kinds.append(("h", nh))
                        nh += 1
                    for dm in range(ND):
                        grp = tc * (NFC + ND) + NFC + dm
                        bank = grp % 2
                        pk, pi = kinds[grp - 2]
                        tensor.wait_ge(fme if pk == "h" else oev, pi + 1)
                        tensor.wait_ge(fme, (tc + 1) * NFC)
                        for hc in range(NFC):
                            mm = tensor.matmul(ps_m[:, bank * 512:(bank + 1) * 512],
                                               w2r[:, hc * D + dm * 128:hc * D + (dm + 1) * 128],
                                               hbuf[:, hc * 512:(hc + 1) * 512],
                                               start=(hc == 0), stop=(hc == NFC - 1))
                        pe_n += 1
                        mm.then_inc(cpe, 1)
                        evacs.append(("o", tc, dm, pe_n, bank, no))
                        kinds.append(("o", no))
                        no += 1

            @b.scalar
            def _(scalar):
                for (kind, tc, idx, pe_target, bank, ni) in evacs:
                    scalar.wait_ge(cpe, pe_target)
                    if kind == "h":
                        scalar.activation(out=gsc[:, 0:512],
                                          in_=ps_m[:, bank * 512:(bank + 1) * 512],
                                          func=AF.Square).then_inc(sqe, 1)
                        scalar.wait_ge(ap, 256 + 2 * ni + 2)
                        scalar.activation(out=gsc[:, 1024:1536],
                                          in_=gsc[:, 512:1024],
                                          func=AF.Sigmoid, scale=GEC).then_inc(sge, 1)
                    else:
                        os_ = (tc * ND + idx) % 2
                        nprior = (tc * ND + idx) // 2
                        if nprior > 0:
                            scalar.wait_ge(fst[os_], 16 * nprior)
                        scalar.activation(out=otile[:, os_ * 512:(os_ + 1) * 512],
                                          in_=ps_m[:, bank * 512:(bank + 1) * 512],
                                          func=AF.Identity).then_inc(oev, 1)

            @b.vector
            def _(vector):
                for (kind, tc, idx, pe_target, bank, ni) in evacs:
                    if kind != "h":
                        continue
                    vector.wait_ge(sqe, ni + 1)
                    vector.tensor_scalar(out=gsc[:, 512:1024], in0=gsc[:, 0:512],
                                         scalar1=GEB, scalar2=1.0,
                                         op0=ALU.mult, op1=ALU.add).then_inc(ap, 1)
                    vector.wait_ge(ap, 256 + 2 * ni + 1)
                    vector.tensor_mul(gsc[:, 512:1024], gsc[:, 512:1024],
                                      ps_m[:, bank * 512:(bank + 1) * 512]).then_inc(ap, 1)
                    vector.wait_ge(sge, ni + 1)
                    vector.tensor_mul(hbuf[:, idx * 512:(idx + 1) * 512],
                                      gsc[:, 1024:1536],
                                      ps_m[:, bank * 512:(bank + 1) * 512]).then_inc(fme, 1)

            @b.gpsimd
            def _(gpsimd):
                n_out = 0
                for (kind, tc, idx, pe_target, bank, ni) in evacs:
                    if kind != "o":
                        continue
                    os_ = (tc * ND + idx) % 2
                    gpsimd.wait_ge(oev, ni + 1)
                    gpsimd.dma_start(out=f1[idx * 128:(idx + 1) * 128, tc * 512:(tc + 1) * 512],
                                     in_=otile[:, os_ * 512:(os_ + 1) * 512]).then_inc(fst[os_], 16)
                    n_out += 1
                gpsimd.wait_ge(fst[0], 16 * ((n_out + 1) // 2))
                gpsimd.wait_ge(fst[1], 16 * (n_out // 2))




def build_l1ffn():
    nc = bass.Bass("TRN2", target_bir_lowering=False, debug=False)
    xn = nc.dram_tensor("xn", [D, L], BF16, kind="ExternalInput").ap()
    fc1A = nc.dram_tensor("fc1A", [256, D, 64], F32, kind="ExternalInput").ap()
    fc1B = nc.dram_tensor("fc1B", [256, 64, FF], F32, kind="ExternalInput").ap()
    fc2A = nc.dram_tensor("fc2A", [256, FF, 64], F32, kind="ExternalInput").ap()
    fc2B = nc.dram_tensor("fc2B", [256, 64, D], F32, kind="ExternalInput").ap()
    lg1 = nc.dram_tensor("lg1", [1, 256], F32, kind="ExternalInput").ap()
    lg2 = nc.dram_tensor("lg2", [1, 256], F32, kind="ExternalInput").ap()
    riota1 = nc.dram_tensor("riota1", [128, 1], F32, kind="ExternalInput").ap()
    riota2 = nc.dram_tensor("riota2", [128, 1], F32, kind="ExternalInput").ap()
    f1 = nc.dram_tensor("f1", [D, L], F32, kind="ExternalOutput").ap()
    emit_l1ffn(nc, xn, fc1A, fc1B, fc2A, fc2B, lg1, lg2, riota1, riota2, f1)
    return nc

def cev_before(evacs, ei):
    return 3 * ei


def n_h(evacs, ei):
    return sum(1 for e in evacs[:ei] if e[0] == "h")


def emit_l1attn(nc, xn, wqkvT, brow, ctab, stab, lepw, a1):


    SCALE = 0.125
    EBIAS = -30.0

    qfm = sb(nc, "qfm", [128, 8 * L], BF16)
    kfm = sb(nc, "kfm", [128, 8 * L], BF16)
    vfm = sb(nc, "vfm", [128, 8 * L], BF16)
    vbuf = sb(nc, "vbuf", [128, 16 * 1056], BF16)
    cost = sb(nc, "cost", [128, 16 * 32], F32)
    sint = sb(nc, "sint", [128, 16 * 32], F32)
    lw = sb(nc, "lw", [128, 32], F32)
    br = sb(nc, "br", [1, 3072], BF16)
    ones1 = sb(nc, "ones1", [1, 128], F32)
    ones1b = sb(nc, "ones1b", [1, 128], BF16)
    nb30 = sb(nc, "nb30", [128, 1], F32)
    ident = sb(nc, "ident", [128, 128], BF16)

    ld = sem(nc, "ld")
    wl = sem(nc, "wl")
    xl = [sem(nc, "xl0"), sem(nc, "xl1")]
    qpe = sem(nc, "qpe")
    cev = sem(nc, "cev")
    ape = sem(nc, "ape")
    ini = sem(nc, "ini")

    with nc.sbuf_tensor("wq", [128, 8 * 3072], BF16) as wq, \
         nc.sbuf_tensor("xs", [128, 128 * 8], BF16) as xs, \
         nc.sbuf_tensor("mn", [128, 1024], F32) as mnb, \
         nc.sbuf_tensor("ex", [128, 1024], F32) as exb, \
         nc.sbuf_tensor("rp", [128, 2048], BF16) as rpb, \
         nc.sbuf_tensor("vb", [128, 1024], BF16) as vbb, \
         nc.psum_tensor("ps_q", [128, 3072], F32) as ps_q, \
         nc.psum_tensor("ps_t", [128, 1024], BF16) as ps_t:
        with nc.Block() as b:
            @b.gpsimd
            def _(gpsimd):
                for c in range(ND):
                    for hh in range(2):
                        gpsimd.dma_start(
                            out=wq[:, c * 3072 + hh * 1536:c * 3072 + (hh + 1) * 1536],
                            in_=wqkvT[c * 128:(c + 1) * 128, hh * 1536:(hh + 1) * 1536],
                        ).then_inc(wl, 16)
                gpsimd.dma_start(out=br[:], in_=brow).then_inc(wl, 16)
                gpsimd.memset(ident[:], 0.0).then_inc(ini, 1)
                gpsimd.wait_ge(ini, 1)
                gpsimd.affine_select(out=ident[:], in_=ident[:],
                                     compare_op=ALU.not_equal, fill=1.0, base=0,
                                     pattern=[[-1, 128]], channel_multiplier=1)

            @b.sync
            def _(sync):
                sync.dma_start(out=cost[:], in_=ctab).then_inc(ld, 16)
                sync.dma_start(out=sint[:], in_=stab).then_inc(ld, 16)
                sync.dma_start(out=lw[:], in_=lepw).then_inc(ld, 16)
                sync.wait_ge(ld, 48)

            @b.vector
            def _(vector):
                vector.memset(ones1[:], 1.0)
                vector.memset(ones1b[:], 1.0)
                vector.memset(nb30[:], -30.0)

        nDVE = [0]
        nACT = [0]
        nPE = [0]
        with nc.Block() as b:
            @b.sync
            def _(sync):
                for tt in range(NT):
                    if tt >= 1:
                        sync.wait_ge(qpe, (tt - 1) * 9 + 6)
                    for c in range(ND):
                        sync.dma_start(
                            out=xs[:, c * 128:(c + 1) * 128],
                            in_=xn[c * 128:(c + 1) * 128, tt * 128:(tt + 1) * 128],
                        ).then_inc(xl[0], 16)

            @b.tensor
            def _(tensor):
                for tt in range(NT):
                    tensor.wait_ge(xl[0], 16 * ND * (tt + 1))
                    if tt == 0:
                        tensor.wait_ge(wl, 16 * 17)
                    if tt >= 1:
                        tensor.wait_ge(ape, (tt - 1) * 16 + 10)
                        tensor.wait_ge(cev, (tt - 1) * 29 + 5)
                    for fo in range(6):
                        for c in range(ND):
                            tensor.matmul(ps_q[:, fo * 512:(fo + 1) * 512],
                                          xs[:, c * 128:(c + 1) * 128],
                                          wq[:, c * 3072 + fo * 512:c * 3072 + (fo + 1) * 512],
                                          start=(c == 0), stop=False)
                        mm = tensor.matmul(ps_q[:, fo * 512:(fo + 1) * 512],
                                           ones1b[:], br[:, fo * 512:(fo + 1) * 512],
                                           start=False, stop=True)
                        mm.then_inc(qpe, 1)
                    tensor.wait_ge(ape, tt * 16 + 8)
                    if tt >= 1:
                        tensor.wait_ge(cev, tt * 29)
                    for ct in range(8):
                        mm = tensor.transpose(ps_t[:, ct * 128:(ct + 1) * 128],
                                              rpb[:, ct * 128:(ct + 1) * 128], ident[:])
                    mm.then_inc(qpe, 1)
                    tensor.wait_ge(ape, tt * 16 + 16)
                    tensor.wait_ge(cev, tt * 29 + 13)
                    for ct in range(8):
                        mm = tensor.transpose(ps_t[:, ct * 128:(ct + 1) * 128],
                                              rpb[:, 1024 + ct * 128:1024 + (ct + 1) * 128],
                                              ident[:])
                    mm.then_inc(qpe, 1)
                    tensor.wait_ge(cev, tt * 29 + 21)
                    for ct in range(8):
                        mm = tensor.transpose(ps_t[:, ct * 128:(ct + 1) * 128],
                                              vbb[:, ct * 128:(ct + 1) * 128], ident[:])
                    mm.then_inc(qpe, 1)

            @b.vector
            def _(vector):
                for tt in range(NT):
                    base = tt * 16
                    nin = 0
                    for g in range(2):
                        gs = slice(g * 1024, (g + 1) * 1024)
                        vector.wait_ge(qpe, tt * 9 + 2 + 2 * g)
                        if g == 0 and tt > 0:
                            pass
                        vector.tensor_scalar_min(exb[:], ps_q[:, gs], 0.0).then_inc(ape, 1)
                        nin += 1
                        vector.wait_ge(cev, tt * 29 + g + 1)
                        vector.scalar_tensor_tensor(out=exb[:], in0=ps_q[:, gs],
                                                    scalar=0.0, in1=vbb[:],
                                                    op0=ALU.max, op1=ALU.add).then_inc(ape, 1)
                        nin += 1
                        ep3 = exb[:].rearrange("p (h d two) -> p h two d", two=2, d=32)
                        x1 = ep3[:, :, 0, :]
                        x2 = ep3[:, :, 1, :]
                        cosb = cost[:, tt * 32:(tt + 1) * 32].unsqueeze(1).to_broadcast([128, 16, 32])
                        sinb = sint[:, tt * 32:(tt + 1) * 32].unsqueeze(1).to_broadcast([128, 16, 32])
                        rp3 = rpb[:, gs].rearrange("p (h d two) -> p h two d", two=2, d=32)
                        sc1 = mnb[:, 0:512].rearrange("p (h d) -> p h d", d=32)
                        sc2 = mnb[:, 512:1024].rearrange("p (h d) -> p h d", d=32)
                        for (dst, m1, t1, m2, t2, op) in (
                                (0, x1, cosb, x2, sinb, ALU.subtract),
                                (1, x1, sinb, x2, cosb, ALU.add)):
                            vector.wait_ge(ape, base + nin)
                            vector.tensor_mul(sc1, m1, t1).then_inc(ape, 1)
                            nin += 1
                            vector.wait_ge(ape, base + nin)
                            vector.tensor_mul(sc2, m2, t2).then_inc(ape, 1)
                            nin += 1
                            vector.wait_ge(ape, base + nin)
                            vector.tensor_tensor(rp3[:, :, dst, :], sc1, sc2, op).then_inc(ape, 1)
                            nin += 1
                            vector.wait_ge(ape, base + nin)

            @b.scalar
            def _(scalar):
                for tt in range(NT):
                    base = tt * 29
                    for g in range(2):
                        scalar.wait_ge(ape, tt * 16 + 1 + g * 8)
                        scalar.activation(out=vbb[:], in_=exb[:], func=AF.Exp).then_inc(cev, 1)
                    scalar.wait_ge(qpe, tt * 9 + 6)
                    scalar.wait_ge(ape, tt * 16 + 10)
                    scalar.activation(out=vbb[:], in_=ps_q[:, 2048:3072],
                                      func=AF.Identity).then_inc(cev, 1)
                    scalar.activation(out=vbuf[:, tt * 1056:tt * 1056 + 1056].rearrange(
                                          "p (h e) -> p h e", e=66)[:, :, 0:32],
                                      in_=ps_q[:, 2048:3072].rearrange("p (h e) -> p h e", e=64)[:, :, 0:32],
                                      func=AF.Identity).then_inc(cev, 1)
                    scalar.activation(out=vbuf[:, tt * 1056:tt * 1056 + 1056].rearrange(
                                          "p (h e) -> p h e", e=66)[:, :, 32:64],
                                      in_=ps_q[:, 2048:3072].rearrange("p (h e) -> p h e", e=64)[:, :, 32:64],
                                      func=AF.Identity).then_inc(cev, 1)
                    scalar.wait_ge(qpe, tt * 9 + 7)
                    for ct in range(8):
                        scalar.activation(out=qfm[:, ct * L + tt * 128:ct * L + (tt + 1) * 128],
                                          in_=ps_t[:, ct * 128:(ct + 1) * 128],
                                          func=AF.Identity).then_inc(cev, 1)
                    scalar.wait_ge(qpe, tt * 9 + 8)
                    for ct in range(8):
                        scalar.activation(out=kfm[:, ct * L + tt * 128:ct * L + (tt + 1) * 128],
                                          in_=ps_t[:, ct * 128:(ct + 1) * 128],
                                          func=AF.Identity).then_inc(cev, 1)
                    scalar.wait_ge(qpe, tt * 9 + 9)
                    for ct in range(8):
                        scalar.activation(out=vfm[:, ct * L + tt * 128:ct * L + (tt + 1) * 128],
                                          in_=ps_t[:, ct * 128:(ct + 1) * 128],
                                          func=AF.Identity).then_inc(cev, 1)
        with nc.Block() as b:
            @b.vector
            def _(vector):
                for tt in range(NT):
                    vector.memset(
                        vbuf[:, tt * 1056:tt * 1056 + 1056].rearrange(
                            "p (h e) -> p h e", e=66)[:, :, 64:66], 1.0)

    with nc.sbuf_tensor("lptmp", [128, L], BF16) as lptmp:
      with nc.Block() as b:
        @b.vector
        def _(vector):
            nn = 0
            for ct in range(8):
                sl = slice(ct * L, (ct + 1) * L)
                vector.tensor_scalar_mul(lptmp[:], vfm[:, sl],
                                         lw[:, 8 + ct:8 + ct + 1]).then_inc(ape, 1)
                nn += 1
                vector.wait_ge(ape, 256 + nn)
                vector.scalar_tensor_tensor(out=lptmp[:, 1:L],
                                            in0=vfm[:, ct * L:(ct + 1) * L - 1],
                                            scalar=lw[:, ct:ct + 1],
                                            in1=lptmp[:, 1:L],
                                            op0=ALU.mult, op1=ALU.add).then_inc(ape, 1)
                nn += 1
                vector.wait_ge(ape, 256 + nn)
                vector.scalar_tensor_tensor(out=lptmp[:, 0:L - 1],
                                            in0=vfm[:, ct * L + 1:(ct + 1) * L],
                                            scalar=lw[:, 16 + ct:16 + ct + 1],
                                            in1=lptmp[:, 0:L - 1],
                                            op0=ALU.mult, op1=ALU.add).then_inc(ape, 1)
                nn += 1
                vector.wait_ge(ape, 256 + nn)
                vector.tensor_scalar_add(lptmp[:], lptmp[:],
                                         lw[:, 24 + ct:24 + ct + 1]).then_inc(ape, 1)
                nn += 1
                vector.wait_ge(ape, 256 + nn)
                vector.tensor_copy(out=vfm[:, sl], in_=lptmp[:]).then_inc(ape, 1)
                nn += 1
                vector.wait_ge(ape, 256 + nn)

    avp = sem(nc, "avp")
    nrc = sem(nc, "nrc")
    bcp = sem(nc, "bcp")
    rbe = sem(nc, "rbe")
    nrm = sem(nc, "nrm")
    fst = [sem(nc, "fst0"), sem(nc, "fst1")]
    with nc.sbuf_tensor("atb", [128, 2 * 16 * 512], BF16) as atb, \
         nc.sbuf_tensor("rc", [1, 2 * 512], F32) as rc, \
         nc.sbuf_tensor("rb", [64, 2 * 512], F32) as rb, \
         nc.sbuf_tensor("ob", [128, 2 * 512], F32) as ob, \
         nc.psum_tensor("ps_s", [128, 2 * 512], F32) as ps_s, \
         nc.psum_tensor("ps_a", [128, 2 * 512], F32) as ps_a, \
         nc.psum_tensor("ps_b2", [64, 2 * 512], F32) as ps_b2:
        with nc.Block() as b:
            pairs = [(h, tq) for h in range(H1) for tq in range(4)]

            @b.tensor
            def _(tensor):
                for p, (h, tq) in enumerate(pairs):
                    hb = (h % 2) * 64
                    ht = h // 2
                    asl = p % 2
                    for kc in range(16):
                        bank = kc % 2
                        gexp = 16 * p + kc
                        if gexp >= 2:
                            tensor.wait_ge(cev, 464 + gexp - 1)
                        tensor.matmul(ps_s[:, bank * 512:(bank + 1) * 512],
                                      kfm[hb:hb + 64, ht * L + kc * 128:ht * L + (kc + 1) * 128],
                                      qfm[hb:hb + 64, ht * L + tq * 512:ht * L + (tq + 1) * 512],
                                      start=True, stop=True).then_inc(qpe, 1)
                        if kc >= 1:
                            tensor.wait_ge(cev, 464 + 16 * p + kc)
                            if kc == 1 and p >= 2:
                                tensor.wait_ge(nrm, p - 1)
                            mm = tensor.matmul(ps_a[0:65, asl * 512:(asl + 1) * 512],
                                               vbuf[:, (kc - 1) * 1056 + h * 66:(kc - 1) * 1056 + h * 66 + 65],
                                               atb[:, asl * 16 * 512 + (kc - 1) * 512:asl * 16 * 512 + kc * 512],
                                               start=(kc == 1), stop=False)
                    tensor.wait_ge(cev, 464 + 16 * p + 16)
                    mm = tensor.matmul(ps_a[0:65, asl * 512:(asl + 1) * 512],
                                       vbuf[:, 15 * 1056 + h * 66:15 * 1056 + h * 66 + 65],
                                       atb[:, asl * 16 * 512 + 15 * 512:asl * 16 * 512 + 16 * 512],
                                       start=False, stop=True)
                    mm.then_inc(avp, 1)
                    tensor.wait_ge(nrc, 2 * p + 1)
                    if p >= 2:
                        tensor.wait_ge(rbe, p - 1)
                    tensor.matmul(ps_b2[:, asl * 512:(asl + 1) * 512],
                                  ones1[:, 0:64], rc[:, asl * 512:(asl + 1) * 512],
                                  start=True, stop=True).then_inc(bcp, 1)

            @b.scalar
            def _(scalar):
                for p, (h, tq) in enumerate(pairs):
                    asl = p % 2
                    for kc in range(16):
                        gexp = 16 * p + kc
                        scalar.wait_ge(qpe, 144 + gexp + 1)
                        if kc == 0 and p >= 2:
                            scalar.wait_ge(avp, p - 1)
                        scalar.activation(out=atb[:, asl * 16 * 512 + kc * 512:asl * 16 * 512 + (kc + 1) * 512],
                                          in_=ps_s[:, (kc % 2) * 512:(kc % 2 + 1) * 512],
                                          func=AF.Exp, bias=nb30[:], scale=SCALE).then_inc(cev, 1)
                    scalar.wait_ge(bcp, p + 1)
                    if p >= 2:
                        scalar.wait_ge(nrm, p - 1)
                    scalar.activation(out=rb[:, asl * 512:(asl + 1) * 512],
                                      in_=ps_b2[0:64, asl * 512:(asl + 1) * 512],
                                      func=AF.Identity).then_inc(rbe, 1)

            @b.vector
            def _(vector):
                for p, (h, tq) in enumerate(pairs):
                    asl = p % 2
                    hb = (h % 2) * 64
                    vector.wait_ge(avp, p + 1)
                    if p >= 2:
                        vector.wait_ge(bcp, p - 1)
                    vector.reciprocal(out=rc[:, asl * 512:(asl + 1) * 512],
                                      in_=ps_a[64:65, asl * 512:(asl + 1) * 512]).then_inc(nrc, 1)
                    vector.wait_ge(rbe, p + 1)
                    if p >= 2:
                        vector.wait_ge(fst[asl], 16 * (p // 2))
                    vector.tensor_mul(ob[hb:hb + 64, asl * 512:(asl + 1) * 512],
                                      ps_a[0:64, asl * 512:(asl + 1) * 512],
                                      rb[:, asl * 512:(asl + 1) * 512]).then_inc(nrc, 1)
                    vector.wait_ge(nrc, 2 * p + 2)
                    vector.tensor_add(ob[hb:hb + 64, asl * 512:(asl + 1) * 512],
                                      ob[hb:hb + 64, asl * 512:(asl + 1) * 512],
                                      vfm[hb:hb + 64, (h // 2) * L + tq * 512:(h // 2) * L + (tq + 1) * 512]
                                      ).then_inc(nrm, 1)

            @b.gpsimd
            def _(gpsimd):
                for p, (h, tq) in enumerate(pairs):
                    asl = p % 2
                    gpsimd.wait_ge(nrm, p + 1)
                    hb2 = (h % 2) * 64
                    gpsimd.dma_start(out=a1[h * 64:(h + 1) * 64, tq * 512:(tq + 1) * 512],
                                     in_=ob[hb2:hb2 + 64, asl * 512:(asl + 1) * 512]).then_inc(fst[asl], 16)
                gpsimd.wait_ge(fst[0], 16 * 32)
                gpsimd.wait_ge(fst[1], 16 * 32)


SCALE = float((64 + 32) ** (-0.5))
GSW = 1056




def build_l1attn():
    nc = bass.Bass("TRN2", target_bir_lowering=False, debug=False)
    xn = nc.dram_tensor("xn", [D, L], BF16, kind="ExternalInput").ap()
    wqkvT = nc.dram_tensor("wqkvT", [D, 3072], F32, kind="ExternalInput").ap()
    brow = nc.dram_tensor("brow", [1, 3072], F32, kind="ExternalInput").ap()
    ctab = nc.dram_tensor("ctab", [128, 16 * 32], F32, kind="ExternalInput").ap()
    stab = nc.dram_tensor("stab", [128, 16 * 32], F32, kind="ExternalInput").ap()
    lepw = nc.dram_tensor("lepw", [128, 4 * 8], F32, kind="ExternalInput").ap()
    a1 = nc.dram_tensor("a1", [D, L], F32, kind="ExternalOutput").ap()
    emit_l1attn(nc, xn, wqkvT, brow, ctab, stab, lepw, a1)
    return nc

def wc_of(tt):
    need = tt * 128 + 64
    return min(2048, 512 * ((need + 511) // 512))


WSHAPES = [("wdkv", (D, 256)), ("wuk", (256, 512)), ("wuv", (256, 512)),
           ("wdq", (D, 512)), ("wuq", (512, 512)), ("wqr", (512, 256)),
           ("wkr", (D, 32)), ("wout", (512, D)), ("iwq", (D, 256)),
           ("iwk", (D, 64))]


def emit_l0attn(nc, xn, win, iwrow, tabs, a0):
    gs = nc.dram_tensor("gs", [L, GSW], BF16, kind="Internal").ap()

    kfm = sb(nc, "kfm", [128, 4 * L], BF16)
    vsh = sb(nc, "vsh", [128, 17 * 512], BF16)
    vf0 = sb(nc, "vf0", [128, 512], BF16)
    qcfm = sb(nc, "qcfm", [128, 4 * L], BF16)
    qctm = sb(nc, "qctm", [128, 16 * 512], BF16)
    qtfm = sb(nc, "qtfm", [128, 2 * L], BF16)
    qotm = sb(nc, "qotm", [128, 16 * 256], BF16)
    qofm = sb(nc, "qofm", [128, 2 * 128], BF16)
    qifm = sb(nc, "qifm", [128, 2 * L], BF16)
    kir = sb(nc, "kir", [128, L], BF16)
    ktr = sb(nc, "ktr", [128, L], BF16)
    kor = sb(nc, "kor", [128, 96], BF16)
    wo = sb(nc, "wo", [128, 4 * D], BF16)
    tcs = sb(nc, "tcs", [128, 512], BF16)
    tss = sb(nc, "tss", [128, 512], BF16)
    trp = sb(nc, "trp", [128, 2048], F32)
    isb = sb(nc, "isb", [128, 2048], F32)
    rbuf = sb(nc, "rbuf", [128, 2048], F32)
    vals = sb(nc, "vals", [128, 32], F32)
    midx = sb(nc, "midx", [128, 32], U32)
    mv8 = sb(nc, "mv8", [128, 8], F32)
    wcol = sb(nc, "wcol", [128, 4], F32)
    gb = sb(nc, "gb", [128, 8 * 544], BF16)
    dsc = sb(nc, "dsc", [128, 256], F32)
    kap = sb(nc, "kap", [128, 32], F32)
    pb = sb(nc, "pb", [128, 32], F32)
    prod = sb(nc, "prod", [128, 64], F32)
    prodb = sb(nc, "prodb", [128, 512], BF16)
    AE = sb(nc, "AE", [128, 8 * 32], BF16)
    scb = sb(nc, "scb", [128, 2 * 352], F32)
    ab = sb(nc, "ab", [128, 2 * 352], BF16)
    atb = sb(nc, "atb", [128, 2 * 384], BF16)
    eac = sb(nc, "eac", [128, 512], F32)
    att = sb(nc, "att", [128, 512], BF16)
    afm = sb(nc, "afm", [128, 4 * 128], BF16)
    otb = sb(nc, "otb", [128, 2 * 128], F32)
    mband = sb(nc, "mband", [128, 192], F32)
    mband0 = sb(nc, "mband0", [128, 192], F32)
    mf0 = sb(nc, "mf0", [128, 96], F32)
    mrow = sb(nc, "mrow", [128, 2], F32)
    srw8 = sb(nc, "srw8", [128, 8], F32)
    rcp8 = sb(nc, "rcp8", [128, 8], F32)
    onesc = sb(nc, "onesc", [128, 1], F32)
    ones1 = sb(nc, "ones1", [1, 128], F32)
    iwsb = sb(nc, "iwsb", [1, 4], F32)
    ident = sb(nc, "ident", [128, 128], BF16)

    ld = sem(nc, "ld")
    wl = sem(nc, "wl")
    pe = sem(nc, "pe")
    ae = sem(nc, "ae")
    ve = sem(nc, "ve")
    ge = sem(nc, "ge")
    g2 = [sem(nc, "gt0"), sem(nc, "gt1")]
    gw = sem(nc, "gw")
    gws = [sem(nc, "gwa"), sem(nc, "gwb")]
    stm = [sem(nc, "st0"), sem(nc, "st1")]

    C = {"pe": 0, "ae": 0, "ve": 0, "ge": 0, "gw": 0, "st0": 0, "st1": 0,
         "g20": 0, "g21": 0, "wl": 0, "ld": 0, "gwa": 0, "gwb": 0}

    with nc.Block() as b:
        @b.gpsimd
        def _(gpsimd):
            gpsimd.memset(ident[:], 0.0).then_inc(ge, 1)
            gpsimd.memset(mband[:], 0.0).then_inc(ge, 1)
            gpsimd.memset(mf0[:], 0.0).then_inc(ge, 1)
            gpsimd.memset(vsh[0:64, 0:512], 0.0).then_inc(ge, 1)
            gpsimd.wait_ge(ge, 4)
            gpsimd.affine_select(out=ident[:], in_=ident[:], compare_op=ALU.not_equal,
                                 fill=1.0, base=0, pattern=[[-1, 128]],
                                 channel_multiplier=1).then_inc(ge, 1)
            gpsimd.wait_ge(ge, 5)
            gpsimd.affine_select(out=mband[:], in_=mband[:], compare_op=ALU.is_ge,
                                 fill=NEG, base=-1, pattern=[[1, 192]],
                                 channel_multiplier=-1).then_inc(ge, 1)
            gpsimd.wait_ge(ge, 6)
            gpsimd.affine_select(out=mband[:], in_=mband[:], compare_op=ALU.is_ge,
                                 fill=NEG, base=64, pattern=[[-1, 192]],
                                 channel_multiplier=1).then_inc(ge, 1)
            gpsimd.wait_ge(ge, 7)
            gpsimd.affine_select(out=mband0[:], in_=mband[:], compare_op=ALU.is_ge,
                                 fill=NEG, base=-64, pattern=[[0, 192]],
                                 channel_multiplier=1).then_inc(ge, 1)
            gpsimd.wait_ge(ge, 8)
            gpsimd.affine_select(out=mf0[:], in_=mf0[:], compare_op=ALU.is_ge,
                                 fill=NEG, base=-1, pattern=[[1, 96]],
                                 channel_multiplier=-1).then_inc(ge, 1)
            gpsimd.wait_ge(ge, 9)
            gpsimd.memset(mf0[0:64, :], 0.0).then_inc(ge, 1)
        C["ge"] = 10

        @b.sync
        def _(sync):
            for nm, dst in [("csl", tcs), ("ssl", tss), ("ramp", trp)]:
                sync.dma_start(out=dst[:], in_=tabs[nm]).then_inc(ld, 16)
                C["ld"] += 16
            sync.dma_start(out=iwsb[:], in_=iwrow).then_inc(ld, 16)
            C["ld"] += 16
            sync.wait_ge(ld, C["ld"])

        @b.vector
        def _(vector):
            vector.memset(onesc[:], 1.0)
            vector.memset(ones1[:], 1.0)

    import contextlib
    isb_r = None
    _s1 = contextlib.ExitStack()
    qrtm = _s1.enter_context(nc.sbuf_tensor("qrtm", [128, 16 * 256], BF16))
    krtm = _s1.enter_context(nc.sbuf_tensor("krtm", [128, 16 * 32], BF16))
    cqfm = _s1.enter_context(nc.sbuf_tensor("cqfm", [128, 4 * L], BF16))
    ckvfm = _s1.enter_context(nc.sbuf_tensor("ckvfm", [128, 2 * L], BF16))
    musq = qrtm
    muk = krtm

    def load_w(gpsimd_or_sync, wname, wbuf, wcols, Kc):
        for ch in range(Kc):
            gpsimd_or_sync.dma_start(out=wbuf[:, ch * wcols:(ch + 1) * wcols],
                                     in_=win[wname][ch * 128:(ch + 1) * 128, :]
                                     ).then_inc(wl, 16)
            C["wl"] += 16

    def fm_proj(wname, Kc, Mtot, wcols, rhs_sb, rhs_stride, out_sb, out_stride,
                wbuf, xns=None):
        pe0 = C["pe"]
        ae0 = C["ae"]
        ld0 = C["ld"]
        nm_ = max(1, Mtot // 128)
        mw = min(128, Mtot)
        with nc.Block() as blk:
            @blk.gpsimd
            def _(gpsimd):
                load_w(gpsimd, wname, wbuf, wcols, Kc)
                gpsimd.wait_ge(wl, C["wl"])

            if rhs_sb is None:
                @blk.sync
                def _(sync):
                    for n in range(4):
                        if n >= 1:
                            sync.wait_ge(pe, pe0 + n * nm_)
                        for cch in range(Kc):
                            sync.dma_start(
                                out=xns[:, cch * 512:(cch + 1) * 512],
                                in_=xn[cch * 128:(cch + 1) * 128, n * 512:(n + 1) * 512]
                                ).then_inc(ld, 16)
                            C["ld"] += 16

            @blk.tensor
            def _(tensor):
                g = 0
                for n in range(4):
                    if rhs_sb is None:
                        tensor.wait_ge(ld, ld0 + 16 * Kc * (n + 1))
                    if n == 0:
                        tensor.wait_ge(wl, C["wl"])
                    for m in range(nm_):
                        slot = g % 2
                        if g >= 2:
                            tensor.wait_ge(ae, ae0 + g - 1)
                        for cch in range(Kc):
                            rhs = (xns[:, cch * 512:(cch + 1) * 512]
                                   if rhs_sb is None else
                                   rhs_sb[:, cch * rhs_stride + n * 512:cch * rhs_stride + (n + 1) * 512])
                            mm = tensor.matmul(
                                ps_p[0:mw, slot * 512:(slot + 1) * 512],
                                wbuf[:, cch * wcols + m * mw:cch * wcols + m * mw + mw],
                                rhs, start=(cch == 0), stop=(cch == Kc - 1))
                        g += 1
                        mm.then_inc(pe, 1)

            @blk.scalar
            def _(scalar):
                g = 0
                for n in range(4):
                    for m in range(nm_):
                        slot = g % 2
                        scalar.wait_ge(pe, pe0 + g + 1)
                        scalar.activation(
                            out=out_sb[0:mw, m * out_stride + n * 512:m * out_stride + (n + 1) * 512],
                            in_=ps_p[0:mw, slot * 512:(slot + 1) * 512],
                            func=AF.Identity).then_inc(ae, 1)
                        g += 1
        C["pe"] += nm_ * 4
        C["ae"] += nm_ * 4

    def tm_proj(wname, Kc, Mtot, wcols, lhs_sb, lhs_stride, out_sb, wbuf,
                xns=None, skip_wload=False):
        pe0 = C["pe"]
        ae0 = C["ae"]
        ld0 = C["ld"]
        with nc.Block() as blk:
            @blk.gpsimd
            def _(gpsimd):
                if not skip_wload:
                    load_w(gpsimd, wname, wbuf, wcols, Kc)
                gpsimd.wait_ge(wl, C["wl"])

            if lhs_sb is None:
                @blk.sync
                def _(sync):
                    for tt in range(NT):
                        if tt >= 1:
                            sync.wait_ge(pe, pe0 + tt)
                        for cch in range(Kc):
                            sync.dma_start(
                                out=xns[:, cch * 128:(cch + 1) * 128],
                                in_=xn[cch * 128:(cch + 1) * 128, tt * 128:(tt + 1) * 128]
                                ).then_inc(ld, 16)
                            C["ld"] += 16

            @blk.tensor
            def _(tensor):
                for tt in range(NT):
                    slot = tt % 2
                    if lhs_sb is None:
                        tensor.wait_ge(ld, ld0 + 16 * Kc * (tt + 1))
                    if tt == 0:
                        tensor.wait_ge(wl, C["wl"])
                    if tt >= 2:
                        tensor.wait_ge(ae, ae0 + tt - 1)
                    for cch in range(Kc):
                        lhs = (xns[:, cch * 128:(cch + 1) * 128]
                               if lhs_sb is None else
                               lhs_sb[:, cch * lhs_stride + tt * 128:cch * lhs_stride + (tt + 1) * 128])
                        mm = tensor.matmul(
                            ps_p[:, slot * 512:slot * 512 + Mtot], lhs,
                            wbuf[:, cch * wcols:(cch + 1) * wcols],
                            start=(cch == 0), stop=(cch == Kc - 1))
                    mm.then_inc(pe, 1)

            @blk.scalar
            def _(scalar):
                for tt in range(NT):
                    slot = tt % 2
                    scalar.wait_ge(pe, pe0 + tt + 1)
                    scalar.activation(out=out_sb[:, tt * Mtot:(tt + 1) * Mtot],
                                      in_=ps_p[:, slot * 512:slot * 512 + Mtot],
                                      func=AF.Identity).then_inc(ae, 1)
        C["pe"] += NT
        C["ae"] += NT

    with nc.psum_tensor("ps_p", [128, 2 * 512], F32) as ps_p, \
         nc.psum_tensor("ps_ta", [128, 512], BF16) as ps_t2, \
         nc.psum_tensor("ps_w4", [128, 4], F32) as ps_w4:
        with nc.sbuf_tensor("wbuf", [128, 4096], BF16) as wbuf, \
             nc.sbuf_tensor("xns", [128, 4096], BF16) as xns:
            fm_proj("wdkv", 8, 256, 256, None, 0, ckvfm, L, wbuf, xns)
            fm_proj("wdq", 8, 512, 512, None, 0, cqfm, L, wbuf, xns)
            fm_proj("iwq", 8, 256, 256, None, 0, qifm, L, wbuf, xns)
            fm_proj("iwk", 8, 64, 64, None, 0, kir, 0, wbuf, xns)
            tm_proj("wkr", 8, 32, 32, None, 0, krtm, wbuf, xns)
            with nc.Block() as blk:
                @blk.gpsimd
                def _(gpsimd):
                    load_w(gpsimd, "wout", wo, D, 4)
                    gpsimd.wait_ge(wl, C["wl"])
        with nc.sbuf_tensor("wbuf2", [128, 4096], BF16) as wbuf, \
             nc.sbuf_tensor("ktb", [128, 2 * 512], BF16) as ktb:
            fm_proj("wuk", 2, 512, 512, ckvfm, L, kfm, L, wbuf)
            pe0 = C["pe"]
            ae0 = C["ae"]
            gwk = {0: C["gwa"], 1: C["gwb"]}
            with nc.Block() as blk:
                @blk.tensor
                def _(tensor):
                    for tt in range(NT):
                        slot = tt % 2
                        if tt >= 2:
                            tensor.wait_ge(gws[slot], gwk[slot] + 16 * (tt // 2))
                        for cch in range(2):
                            mm = tensor.matmul(
                                ps_p[:, slot * 512:(slot + 1) * 512],
                                ckvfm[:, cch * L + tt * 128:cch * L + (tt + 1) * 128],
                                wbuf[:, cch * 512:(cch + 1) * 512],
                                start=(cch == 0), stop=(cch == 1))
                        mm.then_inc(pe, 1)

                @blk.scalar
                def _(scalar):
                    for tt in range(NT):
                        slot = tt % 2
                        scalar.wait_ge(pe, pe0 + tt + 1)
                        scalar.activation(out=ktb[:, slot * 512:(slot + 1) * 512],
                                          in_=ps_p[:, slot * 512:(slot + 1) * 512],
                                          func=AF.Identity).then_inc(ae, 1)

                @blk.gpsimd
                def _(gpsimd):
                    for tt in range(NT):
                        slot = tt % 2
                        gpsimd.wait_ge(ae, ae0 + tt + 1)
                        gpsimd.dma_start(out=gs[tt * 128:(tt + 1) * 128, 0:512],
                                         in_=ktb[:, slot * 512:(slot + 1) * 512]
                                         ).then_inc(gws[slot], 16)
            C["pe"] += NT
            C["ae"] += NT
            C["gwa"] += 16 * 8
            C["gwb"] += 16 * 8
            pe0 = C["pe"]
            ae0 = C["ae"]
            gwk = {0: C["gwa"], 1: C["gwb"]}
            with nc.Block() as blk:
                @blk.gpsimd
                def _(gpsimd):
                    load_w(gpsimd, "wuv", wbuf, 512, 2)
                    gpsimd.wait_ge(wl, C["wl"])
                    for tt in range(NT):
                        slot = tt % 2
                        gpsimd.wait_ge(ae, ae0 + tt + 1)
                        gpsimd.dma_start(out=gs[tt * 128:(tt + 1) * 128, 544:1056],
                                         in_=ktb[:, slot * 512:(slot + 1) * 512]
                                         ).then_inc(gws[slot], 16)

                @blk.tensor
                def _(tensor):
                    for tt in range(NT):
                        slot = tt % 2
                        if tt == 0:
                            tensor.wait_ge(wl, C["wl"])
                        if tt >= 2:
                            tensor.wait_ge(gws[slot], gwk[slot] + 16 * (tt // 2))
                        for cch in range(2):
                            mm = tensor.matmul(
                                ps_p[:, slot * 512:(slot + 1) * 512],
                                ckvfm[:, cch * L + tt * 128:cch * L + (tt + 1) * 128],
                                wbuf[:, cch * 512:(cch + 1) * 512],
                                start=(cch == 0), stop=(cch == 1))
                        mm.then_inc(pe, 1)

                @blk.scalar
                def _(scalar):
                    scalar.wait_ge(gws[0], gwk[0])
                    scalar.wait_ge(gws[1], gwk[1])
                    for tt in range(NT):
                        slot = tt % 2
                        scalar.wait_ge(pe, pe0 + tt + 1)
                        scalar.activation(out=ktb[:, slot * 512:(slot + 1) * 512],
                                          in_=ps_p[:, slot * 512:(slot + 1) * 512],
                                          func=AF.Identity).then_inc(ae, 1)
            C["pe"] += NT
            C["ae"] += NT
            C["gwa"] += 16 * 8
            C["gwb"] += 16 * 8
            with nc.Block() as blk:
                @blk.gpsimd
                def _(gpsimd):
                    gpsimd.wait_ge(gws[0], C["gwa"])
                    gpsimd.wait_ge(gws[1], C["gwb"])
                    for g in range(17):
                        p0 = g * 128 - 64
                        lo = max(p0, 0)
                        hi = min(p0 + 128, L)
                        gpsimd.dma_start(
                            out=vsh[lo - p0:hi - p0, g * 512:(g + 1) * 512],
                            in_=gs[lo:hi, 544:1056]).then_inc(gw, 16)
                        C["gw"] += 16
                    gpsimd.dma_start(out=vf0[:], in_=gs[0:128, 544:1056]).then_inc(gw, 16)
                    C["gw"] += 16
                    gpsimd.wait_ge(gw, C["gw"])
        with nc.sbuf_tensor("wbuf3", [128, 4096], BF16) as wbuf:
            fm_proj("wuq", 4, 512, 512, cqfm, L, qcfm, L, wbuf)
            tm_proj("wuq", 4, 512, 512, cqfm, L, qctm, wbuf, skip_wload=True)
            tm_proj("wqr", 4, 256, 256, cqfm, L, qrtm, wbuf)
        with nc.sbuf_tensor("qttm", [128, 16 * 256], BF16) as qttm, \
             nc.sbuf_tensor("kttm", [128, 16 * 32], BF16) as kttm, \
             nc.sbuf_tensor("tabc", [128, 256], F32) as tabc, \
             nc.sbuf_tensor("tabs", [128, 256], F32) as tabs_b, \
             nc.sbuf_tensor("rsc", [128, 256], F32) as rsc, \
             nc.sbuf_tensor("kdg", [128, 32], BF16) as kdg:

            def softplus_block(src, dst, width):
                ae0 = C["ae"]
                with nc.Block() as blk:
                    @blk.scalar
                    def _(scalar):
                        scalar.activation(out=rbuf[:, 0:width], in_=src, func=AF.Abs
                                          ).then_inc(ae, 1)
                        scalar.wait_ge(ae, ae0 + 1)
                        scalar.activation(out=isb[:, 0:width], in_=rbuf[:, 0:width],
                                          func=AF.Exp, scale=-1.0).then_inc(ae, 1)
                        scalar.wait_ge(ae, ae0 + 2)
                        scalar.activation(out=rbuf[:, 0:width], in_=isb[:, 0:width],
                                          func=AF.Ln, bias=onesc[:]).then_inc(ae, 1)

                    @blk.vector
                    def _(vector):
                        vector.wait_ge(ae, ae0 + 3)
                        vector.scalar_tensor_tensor(out=dst, in0=src, scalar=0.0,
                                                    in1=rbuf[:, 0:width],
                                                    op0=ALU.max, op1=ALU.add
                                                    ).then_inc(ve, 1)
                C["ae"] += 3
                C["ve"] += 1

            for half in range(2):
                softplus_block(qrtm[:, half * 2048:(half + 1) * 2048],
                               musq[:, half * 2048:(half + 1) * 2048], 2048)
            softplus_block(krtm[:], muk[:], 512)

            def load_tab(cn, sn):
                ld0 = C["ld"]
                with nc.Block() as blk:
                    @blk.sync
                    def _(sync):
                        sync.dma_start(out=tabc[:], in_=tabs[cn]).then_inc(ld, 16)
                        sync.dma_start(out=tabs_b[:], in_=tabs[sn]).then_inc(ld, 16)
                        sync.wait_ge(ld, ld0 + 32)
                C["ld"] += 32

            def rot_block(mu, nh, out_sb):
                with nc.Block() as blk:
                    @blk.vector
                    def _(vector):
                        for tt in range(NT):
                            mu3 = mu[:, tt * nh * 32:(tt + 1) * nh * 32].rearrange(
                                "p (h f sx) -> p h f sx", f=2, sx=16)
                            o3 = out_sb[:, tt * nh * 32:(tt + 1) * nh * 32].rearrange(
                                "p (h f sx) -> p h f sx", f=2, sx=16)
                            cb = tabc[:, tt * 16:(tt + 1) * 16].unsqueeze(1).to_broadcast([128, nh, 16])
                            sbb = tabs_b[:, tt * 16:(tt + 1) * 16].unsqueeze(1).to_broadcast([128, nh, 16])
                            m1 = mu3[:, :, 0, :]
                            m2 = mu3[:, :, 1, :]
                            s1 = rsc[:, 0:nh * 16].rearrange("p (h sx) -> p h sx", sx=16)
                            s2 = rsc[:, 128:128 + nh * 16].rearrange("p (h sx) -> p h sx", sx=16)
                            for (dst, a1, t1, a2, t2, op) in (
                                    (0, m1, cb, m2, sbb, ALU.subtract),
                                    (1, m1, sbb, m2, cb, ALU.add)):
                                vector.tensor_mul(s1, a1, t1).then_inc(ve, 1)
                                C["ve"] += 1
                                vector.wait_ge(ve, C["ve"])
                                vector.tensor_mul(s2, a2, t2).then_inc(ve, 1)
                                C["ve"] += 1
                                vector.wait_ge(ve, C["ve"])
                                vector.tensor_tensor(o3[:, :, dst, :], s1, s2, op).then_inc(ve, 1)
                                C["ve"] += 1
                                vector.wait_ge(ve, C["ve"])

            load_tab("c2t", "s2t")
            rot_block(musq, 8, qttm)
            pe0 = C["pe"]
            ae0 = C["ae"]
            with nc.Block() as blk:
                @blk.tensor
                def _(tensor):
                    for tt in range(NT):
                        if tt >= 1:
                            tensor.wait_ge(ae, ae0 + 2 * tt)
                        for chunk in range(2):
                            mm = tensor.transpose(ps_t2[:, chunk * 128:(chunk + 1) * 128],
                                                  qttm[:, tt * 256 + chunk * 128:tt * 256 + (chunk + 1) * 128],
                                                  ident[:])
                        mm.then_inc(pe, 1)

                @blk.scalar
                def _(scalar):
                    for tt in range(NT):
                        scalar.wait_ge(pe, pe0 + tt + 1)
                        scalar.activation(out=qtfm[:, 0 * L + tt * 128:0 * L + (tt + 1) * 128],
                                          in_=ps_t2[:, 0:128], func=AF.Identity).then_inc(ae, 1)
                        scalar.activation(out=qtfm[:, 1 * L + tt * 128:1 * L + (tt + 1) * 128],
                                          in_=ps_t2[:, 128:256], func=AF.Identity).then_inc(ae, 1)
            C["pe"] += NT
            C["ae"] += 2 * NT
            load_tab("ct_", "st_")
            rot_block(musq, 8, qotm)
            with nc.Block() as blk:
                @blk.vector
                def _(vector):
                    mu3 = muk[:, 0:32].rearrange("p (f sx) -> p f sx", f=2)
                    m1 = mu3[:, 0, :]
                    m2 = mu3[:, 1, :]
                    for (dst, a1, t1, a2, t2, op) in (
                            (0, m1, tabc[:, 0:16], m2, tabs_b[:, 0:16], ALU.subtract),
                            (1, m1, tabs_b[:, 0:16], m2, tabc[:, 0:16], ALU.add)):
                        vector.tensor_mul(rsc[:, 0:16], a1, t1).then_inc(ve, 1)
                        C["ve"] += 1
                        vector.wait_ge(ve, C["ve"])
                        vector.tensor_mul(rsc[:, 16:32], a2, t2).then_inc(ve, 1)
                        C["ve"] += 1
                        vector.wait_ge(ve, C["ve"])
                        vector.tensor_tensor(kdg[:, dst * 16:(dst + 1) * 16],
                                             rsc[:, 0:16], rsc[:, 16:32], op).then_inc(ve, 1)
                        C["ve"] += 1
                        vector.wait_ge(ve, C["ve"])
            load_tab("cp63", "sp63")
            rot_block(muk, 1, kttm)
            pe0 = C["pe"]
            ae0 = C["ae"]
            with nc.Block() as blk:
                @blk.tensor
                def _(tensor):
                    for tt in range(NT):
                        if tt >= 1:
                            tensor.wait_ge(ae, ae0 + tt)
                        mm = tensor.transpose(ps_t2[0:32, 0:128],
                                              kttm[:, tt * 32:(tt + 1) * 32], ident[:])
                        mm.then_inc(pe, 1)

                @blk.scalar
                def _(scalar):
                    for tt in range(NT):
                        scalar.wait_ge(pe, pe0 + tt + 1)
                        scalar.activation(out=ktr[0:32, tt * 128:(tt + 1) * 128],
                                          in_=ps_t2[0:32, 0:128], func=AF.Identity
                                          ).then_inc(ae, 1)
            C["pe"] += NT
            C["ae"] += NT
            pe0 = C["pe"]
            ae0 = C["ae"]
            with nc.Block() as blk:
                @blk.tensor
                def _(tensor):
                    for chunk in range(2):
                        mm = tensor.transpose(ps_t2[:, chunk * 128:(chunk + 1) * 128],
                                              qotm[:, chunk * 128:(chunk + 1) * 128],
                                              ident[:])
                    mm2 = tensor.transpose(ps_t2[0:32, 384:512], kdg[:], ident[:])
                    mm2.then_inc(pe, 1)

                @blk.scalar
                def _(scalar):
                    scalar.wait_ge(pe, pe0 + 1)
                    scalar.activation(out=qofm[:, 0:128], in_=ps_t2[:, 0:128],
                                      func=AF.Identity).then_inc(ae, 1)
                    scalar.activation(out=qofm[:, 128:256], in_=ps_t2[:, 128:256],
                                      func=AF.Identity).then_inc(ae, 1)
                    scalar.activation(out=kor[0:32, 0:96], in_=ps_t2[0:32, 384:480],
                                      func=AF.Identity).then_inc(ae, 1)
            C["pe"] += 1
            C["ae"] += 3
            pe0 = C["pe"]
            with nc.Block() as blk:
                @blk.gpsimd
                def _(gpsimd):
                    for dst0 in (32, 64, 96):
                        gpsimd.dma_start(out=ktr[dst0:dst0 + 32, :], in_=ktr[0:32, :]
                                         ).then_inc(gw, 16)
                        C["gw"] += 16
                        gpsimd.dma_start(out=kor[dst0:dst0 + 32, :], in_=kor[0:32, :]
                                         ).then_inc(gw, 16)
                        C["gw"] += 16
                    gpsimd.dma_start(out=kir[64:128, :], in_=kir[0:64, :]).then_inc(gw, 16)
                    C["gw"] += 16
                    for tt in range(NT):
                        gpsimd.dma_start(out=gs[tt * 128:(tt + 1) * 128, 512:544],
                                         in_=muk[:, tt * 32:(tt + 1) * 32]).then_inc(gw, 16)
                        C["gw"] += 16
                    gpsimd.wait_ge(gw, C["gw"])

                @blk.tensor
                def _(tensor):
                    mm = tensor.matmul(ps_w4[:], ones1[:], iwsb[:], start=True, stop=True)
                    mm.then_inc(pe, 1)

                @blk.scalar
                def _(scalar):
                    scalar.wait_ge(pe, pe0 + 1)
                    scalar.activation(out=wcol[:], in_=ps_w4[:], func=AF.Identity
                                      ).then_inc(ae, 1)
            C["pe"] += 1
            C["ae"] += 1
    _s1.close()

    prev_dots_end = 0
    prev_eav_end = 0
    prev_combine_end = 0
    with nc.psum_tensor("ps_i", [128, 2 * 512], F32) as ps_i, \
         nc.psum_tensor("ps_sc", [128, 2 * 512], F32) as ps_sc, \
         nc.psum_tensor("ps_av", [128, 512], F32) as ps_av, \
         nc.psum_tensor("ps_tp", [128, 512], BF16) as ps_tp, \
         nc.psum_tensor("ps_o", [128, 2 * 512], F32) as ps_o:
        for tt in range(NT):
            wc = wc_of(tt)
            nch = wc // 512
            pe0 = C["pe"]
            ae0 = C["ae"]
            ve0 = C["ve"]
            with nc.Block() as blk:
                @blk.tensor
                def _(tensor):
                    for n in range(nch):
                        for h in range(4):
                            g = n * 4 + h
                            slot = g % 2
                            if g >= 2:
                                tensor.wait_ge(ae, ae0 + g - 1)
                            hb = (h % 2) * 64
                            mm = tensor.matmul(
                                ps_i[:, slot * 512:(slot + 1) * 512],
                                qifm[hb:hb + 64, (h // 2) * L + tt * 128:(h // 2) * L + (tt + 1) * 128],
                                kir[hb:hb + 64, n * 512:(n + 1) * 512],
                                start=True, stop=True)
                            mm.then_inc(pe, 1)

                @blk.scalar
                def _(scalar):
                    for n in range(nch):
                        for h in range(4):
                            g = n * 4 + h
                            slot = g % 2
                            scalar.wait_ge(pe, pe0 + g + 1)
                            if n >= 1:
                                scalar.wait_ge(ve, ve0 + (n - 1) * 4 + h + 1)
                            scalar.activation(out=rbuf[:, h * 512:(h + 1) * 512],
                                              in_=ps_i[:, slot * 512:(slot + 1) * 512],
                                              func=AF.Relu).then_inc(ae, 1)

                @blk.vector
                def _(vector):
                    for n in range(nch):
                        vector.wait_ge(ae, ae0 + n * 4 + 4)
                        vector.tensor_scalar(out=isb[:, n * 512:(n + 1) * 512],
                                             in0=rbuf[:, 0:512], scalar1=wcol[:, 0:1],
                                             scalar2=None, op0=ALU.mult).then_inc(ve, 1)
                        for h in range(1, 4):
                            vector.wait_ge(ve, ve0 + n * 4 + h)
                            vector.scalar_tensor_tensor(
                                out=isb[:, n * 512:(n + 1) * 512],
                                in0=rbuf[:, h * 512:(h + 1) * 512],
                                scalar=wcol[:, h:h + 1],
                                in1=isb[:, n * 512:(n + 1) * 512],
                                op0=ALU.mult, op1=ALU.add).then_inc(ve, 1)
                        vector.wait_ge(ve, ve0 + n * 4 + 4)
            C["pe"] += nch * 4
            C["ae"] += nch * 4
            C["ve"] += nch * 4

            with nc.Block() as blk:
                @blk.gpsimd
                def _(gpsimd):
                    gpsimd.affine_select(out=isb[:, 0:wc], in_=isb[:, 0:wc],
                                         compare_op=ALU.is_ge, fill=NEG,
                                         base=tt * 128 - 64, pattern=[[-1, wc]],
                                         channel_multiplier=1).then_inc(ge, 1)
            C["ge"] += 1

            ve0 = C["ve"]
            with nc.Block() as blk:
                @blk.vector
                def _(vector):
                    vector.tensor_scalar(out=rbuf[:, 0:wc], in0=isb[:, 0:wc],
                                         scalar1=0.0, scalar2=None,
                                         op0=ALU.is_equal).then_inc(ve, 1)
                    vector.wait_ge(ve, ve0 + 1)
                    vector.tensor_mul(rbuf[:, 0:wc], rbuf[:, 0:wc], trp[:, 0:wc]
                                      ).then_inc(ve, 1)
                    vector.wait_ge(ve, ve0 + 2)
                    vector.tensor_add(isb[:, 0:wc], isb[:, 0:wc], rbuf[:, 0:wc]
                                      ).then_inc(ve, 1)
                    vector.wait_ge(ve, ve0 + 3)
            C["ve"] += 3

            ve0 = C["ve"]
            with nc.Block() as blk:
                @blk.vector
                def _(vector):
                    nv = 0
                    for r in range(4):
                        vector.max(out=mv8[:], in_=isb[:, 0:wc]).then_inc(ve, 1)
                        nv += 1
                        vector.wait_ge(ve, ve0 + nv)
                        vector.max_index(out=midx[:, r * 8:(r + 1) * 8], in_max=mv8[:],
                                         in_values=isb[:, 0:wc]).then_inc(ve, 1)
                        nv += 1
                        vector.wait_ge(ve, ve0 + nv)
                        vector.tensor_copy(out=vals[:, r * 8:(r + 1) * 8], in_=mv8[:]
                                           ).then_inc(ve, 1)
                        nv += 1
                        vector.wait_ge(ve, ve0 + nv)
                        if r < 3:
                            vector.match_replace(out=isb[:, 0:wc], in_to_replace=mv8[:],
                                                 in_values=isb[:, 0:wc], imm_value=NEG
                                                 ).then_inc(ve, 1)
                            nv += 1
                            vector.wait_ge(ve, ve0 + nv)
                    vector.tensor_scalar(out=pb[:], in0=vals[:], scalar1=-1e37,
                                         scalar2=NEG, op0=ALU.is_lt, op1=ALU.mult
                                         ).then_inc(ve, 1)
                    nv += 1
                    vector.wait_ge(ve, ve0 + nv)
            C["ve"] += 16

            ve0 = C["ve"]
            g0 = {0: C["g20"], 1: C["g21"]}
            pdots = prev_dots_end
            with nc.Block() as blk:
                @blk.gpsimd
                def _(gpsimd):
                    for bi in range(8):
                        slot = bi % 2
                        if bi >= 2:
                            gpsimd.wait_ge(ve, ve0 + 12 * 4 * (bi - 1))
                        elif pdots:
                            gpsimd.wait_ge(ve, pdots)
                        for j4 in range(4):
                            j = bi * 4 + j4
                            gpsimd.indirect_dma_start(
                                out=gb[:, (slot * 4 + j4) * 544:(slot * 4 + j4 + 1) * 544],
                                out_offset=None, in_=gs,
                                in_offset=bass.IndirectOffsetOnAxis(
                                    ap=midx[:, j:j + 1], axis=0),
                            ).then_inc(g2[slot], 16)
                            C["g2%d" % slot] += 16

                @blk.vector
                def _(vector):
                    nv = 0
                    for bi in range(8):
                        slot = bi % 2
                        vector.wait_ge(g2[slot], g0[slot] + 16 * 4 * (bi // 2 + 1))
                        for j4 in range(4):
                            j = bi * 4 + j4
                            go = (slot * 4 + j4) * 544
                            vector.tensor_mul(prodb[:], gb[:, go:go + 512],
                                              qctm[:, tt * 512:(tt + 1) * 512]
                                              ).then_inc(ve, 1)
                            nv += 1
                            vector.wait_ge(ve, ve0 + nv)
                            vector.tensor_reduce(out=dsc[:, j * 8:(j + 1) * 8],
                                                 in_=prodb[:].rearrange("p (h d) -> p h d", d=64),
                                                 axis=AX.X, op=ALU.add).then_inc(ve, 1)
                            nv += 1
                            vector.wait_ge(ve, ve0 + nv)
                            mug = gb[:, go + 512:go + 544].rearrange("p (f s) -> p f s", f=2)
                            for (dst, a1, t1, a2, t2, op) in (
                                    (0, mug[:, 0, :], tcs[:, j * 16:(j + 1) * 16],
                                     mug[:, 1, :], tss[:, j * 16:(j + 1) * 16], ALU.subtract),
                                    (1, mug[:, 0, :], tss[:, j * 16:(j + 1) * 16],
                                     mug[:, 1, :], tcs[:, j * 16:(j + 1) * 16], ALU.add)):
                                vector.tensor_mul(rbuf[:, 0:16], a1, t1).then_inc(ve, 1)
                                nv += 1
                                vector.wait_ge(ve, ve0 + nv)
                                vector.tensor_mul(rbuf[:, 16:32], a2, t2).then_inc(ve, 1)
                                nv += 1
                                vector.wait_ge(ve, ve0 + nv)
                                vector.tensor_tensor(kap[:, dst * 16:(dst + 1) * 16],
                                                     rbuf[:, 0:16], rbuf[:, 16:32], op
                                                     ).then_inc(ve, 1)
                                nv += 1
                                vector.wait_ge(ve, ve0 + nv)
                            vector.tensor_mul(
                                prodb[:, 0:256].rearrange("p (h s) -> p h s", s=32),
                                qotm[:, tt * 256:(tt + 1) * 256].rearrange("p (h s) -> p h s", s=32),
                                kap[:].unsqueeze(1).to_broadcast([128, 8, 32])
                                ).then_inc(ve, 1)
                            nv += 1
                            vector.wait_ge(ve, ve0 + nv)
                            vector.tensor_reduce(out=rbuf[:, 32:40],
                                                 in_=prodb[:, 0:256].rearrange("p (h s) -> p h s", s=32),
                                                 axis=AX.X, op=ALU.add).then_inc(ve, 1)
                            nv += 1
                            vector.wait_ge(ve, ve0 + nv)
                            vector.tensor_add(dsc[:, j * 8:(j + 1) * 8],
                                              dsc[:, j * 8:(j + 1) * 8], rbuf[:, 32:40]
                                              ).then_inc(ve, 1)
                            nv += 1
                            vector.wait_ge(ve, ve0 + nv)
                            vector.scalar_tensor_tensor(
                                out=dsc[:, j * 8:(j + 1) * 8],
                                in0=pb[:, j:j + 1].to_broadcast([128, 8]),
                                scalar=1.0,
                                in1=dsc[:, j * 8:(j + 1) * 8],
                                op0=ALU.mult, op1=ALU.add).then_inc(ve, 1)
                            nv += 1
                            vector.wait_ge(ve, ve0 + nv)
            C["ve"] += 12 * 32
            prev_dots_end = C["ve"]

            pe0 = C["pe"]
            ae0 = C["ae"]
            ve0 = C["ve"]
            nch_t = 3 if tt == 0 else 2
            g0 = {0: C["g20"], 1: C["g21"]}
            def pe_score(h):
                return pe0 + (h + 1 if h < 3 else 3 * h - 3)

            def pe_tp(h):
                return pe0 + (4 + 3 * h if h <= 5 else (21 if h == 6 else 23))

            def pe_av(h):
                return pe_tp(h) - pe0 + 1 + pe0

            act_order = []
            for _h in range(8):
                act_order += [("e", _h), ("c", _h)]
                if _h >= 2:
                    act_order += [("tb", _h - 2)]
            act_order += [("tb", 6), ("tb", 7)]
            _ae_map = {}
            _pos = 0
            for kind, _h in act_order:
                _pos += (nch_t if kind == "tb" else 1)
                _ae_map[(kind, _h)] = _pos
            AE_TOTAL = _pos

            def ae_exp(h):
                return ae0 + _ae_map[("e", h)]

            def ae_aec(h):
                return ae0 + _ae_map[("c", h)]

            def ae_atbe(h):
                return ae0 + _ae_map[("tb", h)]

            def ae_afm():
                return ae0 + AE_TOTAL + 4

            def ae_oev(dm):
                return ae0 + AE_TOTAL + 4 + dm + 1

            NSEC = 7 if tt == 0 else 6

            def ve_sec(h, k=None):
                return ve0 + NSEC * h + (NSEC if k is None else k)

            def ve_recip(h):
                return ve0 + 8 * NSEC + h + 1

            def ve_eav(bi):
                return ve0 + 8 * NSEC + 8 + 1 + 8 * (bi + 1)

            ve_att_done = ve0 + 8 * NSEC + 8 + 1 + 64 + 16
            with nc.Block() as blk:
                @blk.tensor
                def _(tensor):
                    def emit_score(h):
                        slot = h % 2
                        hb = (h % 2) * 64
                        pt = h // 2
                        qb = (h % 4) * 32
                        qpt = h // 4
                        if h >= 2:
                            tensor.wait_ge(ve, ve_sec(h - 2))
                            tensor.wait_ge(ae, ae_exp(h - 2))
                        if tt == 0:
                            tensor.matmul(
                                ps_sc[:, slot * 512 + 64:slot * 512 + 192],
                                qcfm[hb:hb + 64, pt * L + tt * 128:pt * L + (tt + 1) * 128],
                                kfm[hb:hb + 64, pt * L + 0:pt * L + 128],
                                start=True, stop=False)
                            tensor.matmul(
                                ps_sc[:, slot * 512 + 64:slot * 512 + 192],
                                qtfm[qb:qb + 32, qpt * L + tt * 128:qpt * L + (tt + 1) * 128],
                                ktr[qb:qb + 32, 0:128],
                                start=False, stop=False, tile_position=(qb, 0))
                            tensor.matmul(
                                ps_sc[:, slot * 512 + 192:slot * 512 + 288],
                                qcfm[hb:hb + 64, pt * L + 0:pt * L + 128],
                                kfm[hb:hb + 64, pt * L + 0:pt * L + 96],
                                start=False, stop=False)
                            mm = tensor.matmul(
                                ps_sc[:, slot * 512 + 192:slot * 512 + 288],
                                qofm[qb:qb + 32, qpt * 128:(qpt + 1) * 128],
                                kor[qb:qb + 32, 0:96],
                                start=False, stop=True, tile_position=(qb, 0))
                        else:
                            w0 = tt * 128 - 64
                            tensor.matmul(
                                ps_sc[:, slot * 512:slot * 512 + 192],
                                qcfm[hb:hb + 64, pt * L + tt * 128:pt * L + (tt + 1) * 128],
                                kfm[hb:hb + 64, pt * L + w0:pt * L + w0 + 192],
                                start=True, stop=False)
                            mm = tensor.matmul(
                                ps_sc[:, slot * 512:slot * 512 + 192],
                                qtfm[qb:qb + 32, qpt * L + tt * 128:qpt * L + (tt + 1) * 128],
                                ktr[qb:qb + 32, w0:w0 + 192],
                                start=False, stop=True, tile_position=(qb, 0))
                        mm.then_inc(pe, 1)

                    def emit_tp(h):
                        slot = h % 2
                        tensor.wait_ge(ae, ae_aec(h))
                        if h >= 1:
                            tensor.wait_ge(ae, ae_atbe(h - 1))
                        if h <= 1 and prev_combine_end:
                            tensor.wait_ge(ve, prev_combine_end)
                        for chunk in range(nch_t):
                            cw = 128 if chunk < 2 else 96
                            mm = tensor.transpose(
                                ps_tp[0:cw, chunk * 128:chunk * 128 + 128],
                                ab[:, slot * 352 + chunk * 128:slot * 352 + chunk * 128 + cw],
                                ident[:])
                        mm.then_inc(pe, 1)

                    def emit_av(h):
                        slot = h % 2
                        tensor.wait_ge(ae, ae_atbe(h))
                        if h == 0 and prev_combine_end:
                            tensor.wait_ge(ve, prev_combine_end)
                        mm = tensor.matmul(ps_av[:, h * 64:(h + 1) * 64],
                                           atb[:, slot * 384:slot * 384 + 128],
                                           vsh[:, tt * 512 + h * 64:tt * 512 + (h + 1) * 64],
                                           start=(h == 0), stop=False)
                        mm = tensor.matmul(ps_av[:, h * 64:(h + 1) * 64],
                                           atb[0:64, slot * 384 + 128:slot * 384 + 256],
                                           vsh[0:64, (tt + 1) * 512 + h * 64:(tt + 1) * 512 + (h + 1) * 64],
                                           start=False, stop=(tt != 0 and h == 7))
                        if tt == 0:
                            mm = tensor.matmul(ps_av[:, h * 64:(h + 1) * 64],
                                               atb[0:96, slot * 384 + 256:slot * 384 + 384],
                                               vf0[0:96, h * 64:(h + 1) * 64],
                                               start=False, stop=(h == 7))
                        mm.then_inc(pe, 1)

                    for h in range(8):
                        emit_score(h)
                        if h >= 2:
                            emit_tp(h - 2)
                            emit_av(h - 2)
                    for h in (6, 7):
                        emit_tp(h)
                        emit_av(h)

                @blk.vector
                def _(vector):
                    for h in range(8):
                        slot = h % 2
                        vector.wait_ge(pe, pe_score(h))
                        if h >= 2:
                            vector.wait_ge(ae, ae_exp(h - 2))
                        ko = 0
                        if tt == 0:
                            vector.memset(ps_sc[:, slot * 512:slot * 512 + 64], 0.0
                                          ).then_inc(ve, 1)
                            ko = 1
                            vector.wait_ge(ve, ve_sec(h, ko))
                        vector.tensor_add(scb[:, slot * 352:slot * 352 + 192],
                                          ps_sc[:, slot * 512:slot * 512 + 192],
                                          mband0[:] if tt == 0 else mband[:]
                                          ).then_inc(ve, 1)
                        vector.wait_ge(ve, ve_sec(h, ko + 1))
                        vector.tensor_copy(
                            out=scb[:, slot * 352 + 192:slot * 352 + 224],
                            in_=dsc[:].rearrange("p (j h) -> p h j", h=8)[:, h, :]
                            ).then_inc(ve, 1)
                        vector.wait_ge(ve, ve_sec(h, ko + 2))
                        if tt == 0:
                            vector.tensor_add(scb[:, slot * 352 + 256:slot * 352 + 352],
                                              ps_sc[:, slot * 512 + 192:slot * 512 + 288],
                                              mf0[:]).then_inc(ve, 1)
                        else:
                            vector.memset(scb[:, slot * 352 + 256:slot * 352 + 352], NEG
                                          ).then_inc(ve, 1)
                        vector.wait_ge(ve, ve_sec(h, ko + 3))
                        vector.memset(scb[:, slot * 352 + 224:slot * 352 + 256], NEG
                                      ).then_inc(ve, 1)
                        vector.wait_ge(ve, ve_sec(h, ko + 4))
                        vector.reduce_max(out=mrow[:, 0:1],
                                          in_=scb[:, slot * 352:(slot + 1) * 352],
                                          axis=AX.X).then_inc(ve, 1)
                        vector.wait_ge(ve, ve_sec(h, ko + 5))
                        if h >= 1:
                            vector.wait_ge(ae, ae_exp(h - 1))
                        vector.tensor_scalar_mul(mrow[:, 1:2], mrow[:, 0:1], -SCALE
                                                 ).then_inc(ve, 1)
                        vector.wait_ge(ve, ve_sec(h, ko + 6))

                @blk.scalar
                def _(scalar):
                    for kind, h in act_order:
                        slot = h % 2
                        if kind == "e":
                            scalar.wait_ge(ve, ve_sec(h))
                            if h >= 2:
                                scalar.wait_ge(pe, pe_tp(h - 2))
                            scalar.activation(out=ab[:, slot * 352:(slot + 1) * 352],
                                              in_=scb[:, slot * 352:(slot + 1) * 352],
                                              func=AF.Exp, bias=mrow[:, 1:2], scale=SCALE,
                                              accum_out=srw8[:, h:h + 1]).then_inc(ae, 1)
                        elif kind == "c":
                            scalar.wait_ge(ae, ae_exp(h))
                            scalar.activation(out=AE[:, h * 32:(h + 1) * 32],
                                              in_=ab[:, slot * 352 + 192:slot * 352 + 224],
                                              func=AF.Identity).then_inc(ae, 1)
                        else:
                            scalar.wait_ge(pe, pe_tp(h))
                            if h >= 2:
                                scalar.wait_ge(pe, pe_av(h - 2))
                            for chunk in range(nch_t):
                                cw = 128 if chunk < 2 else 96
                                scalar.activation(
                                    out=atb[0:cw, slot * 384 + chunk * 128:slot * 384 + (chunk + 1) * 128],
                                    in_=ps_tp[0:cw, chunk * 128:(chunk + 1) * 128],
                                    func=AF.Identity).then_inc(ae, 1)

                @blk.vector
                def _(vector):
                    for h in range(8):
                        vector.wait_ge(ae, ae_exp(h))
                        vector.reciprocal(out=rcp8[:, h:h + 1], in_=srw8[:, h:h + 1]
                                          ).then_inc(ve, 1)
                        vector.wait_ge(ve, ve_recip(h))

                @blk.gpsimd
                def _(gpsimd):
                    for bi in range(8):
                        slot = bi % 2
                        if bi >= 2:
                            gpsimd.wait_ge(ve, ve_eav(bi - 2))
                        else:
                            gpsimd.wait_ge(ve, prev_dots_end)
                        for j4 in range(4):
                            j = bi * 4 + j4
                            gpsimd.indirect_dma_start(
                                out=gb[:, (slot * 4 + j4) * 544:(slot * 4 + j4) * 544 + 512],
                                out_offset=None, in_=gs,
                                in_offset=bass.IndirectOffsetOnAxis(
                                    ap=midx[:, j:j + 1], axis=0),
                                element_offset=544,
                            ).then_inc(g2[slot], 16)
                            C["g2%d" % slot] += 16

                @blk.vector
                def _(vector):
                    nv = 8 * NSEC + 8
                    vector.wait_ge(ae, ae_aec(7))
                    vector.memset(eac[:], 0.0).then_inc(ve, 1)
                    nv += 1
                    vector.wait_ge(ve, ve0 + nv)
                    for bi in range(8):
                        slot = bi % 2
                        vector.wait_ge(g2[slot], g0[slot] + 16 * 4 * (bi // 2 + 1))
                        for j4 in range(4):
                            j = bi * 4 + j4
                            go = (slot * 4 + j4) * 544
                            scl = AE[:].rearrange("p (h j) -> p h j", j=32)[:, :, j:j + 1].to_broadcast([128, 8, 64])
                            vector.tensor_mul(prodb[:].rearrange("p (h d) -> p h d", d=64),
                                              gb[:, go:go + 512].rearrange("p (h d) -> p h d", d=64),
                                              scl).then_inc(ve, 1)
                            nv += 1
                            vector.wait_ge(ve, ve0 + nv)
                            vector.tensor_add(eac[:], eac[:], prodb[:]).then_inc(ve, 1)
                            nv += 1
                            vector.wait_ge(ve, ve0 + nv)
                    vector.wait_ge(pe, pe_av(7))
                    for h in range(8):
                        vector.tensor_add(prod[:, 0:64], ps_av[:, h * 64:(h + 1) * 64],
                                          eac[:, h * 64:(h + 1) * 64]).then_inc(ve, 1)
                        nv += 1
                        vector.wait_ge(ve, ve0 + nv)
                        vector.tensor_scalar_mul(att[:, h * 64:(h + 1) * 64],
                                                 prod[:, 0:64], rcp8[:, h:h + 1]
                                                 ).then_inc(ve, 1)
                        nv += 1
                        vector.wait_ge(ve, ve0 + nv)

                @blk.tensor
                def _(tensor):
                    tensor.wait_ge(ve, ve_att_done)
                    tensor.wait_ge(ae, ae_atbe(7))
                    for chunk in range(4):
                        mm = tensor.transpose(ps_tp[:, chunk * 128:(chunk + 1) * 128],
                                              att[:, chunk * 128:(chunk + 1) * 128],
                                              ident[:])
                    mm.then_inc(pe, 1)

                @blk.scalar
                def _(scalar):
                    scalar.wait_ge(pe, pe0 + 25)
                    for chunk in range(4):
                        scalar.activation(out=afm[:, chunk * 128:(chunk + 1) * 128],
                                          in_=ps_tp[:, chunk * 128:(chunk + 1) * 128],
                                          func=AF.Identity).then_inc(ae, 1)

                @blk.tensor
                def _(tensor):
                    tensor.wait_ge(ae, ae_afm())
                    for dm in range(8):
                        slot = dm % 2
                        if dm >= 2:
                            tensor.wait_ge(ae, ae_oev(dm - 2))
                        for kc in range(4):
                            mm = tensor.matmul(ps_o[:, slot * 512:slot * 512 + 128],
                                               wo[:, kc * D + dm * 128:kc * D + (dm + 1) * 128],
                                               afm[:, kc * 128:(kc + 1) * 128],
                                               start=(kc == 0), stop=(kc == 3))
                        mm.then_inc(pe, 1)

                @blk.scalar
                def _(scalar):
                    for dm in range(8):
                        slot = dm % 2
                        scalar.wait_ge(pe, pe0 + 25 + dm + 1)
                        nst = tt * 8 + dm
                        if nst >= 2:
                            scalar.wait_ge(stm[nst % 2], 16 * (nst // 2))
                        scalar.activation(out=otb[:, slot * 128:slot * 128 + 128],
                                          in_=ps_o[:, slot * 512:slot * 512 + 128],
                                          func=AF.Identity).then_inc(ae, 1)

                @blk.gpsimd
                def _(gpsimd):
                    for dm in range(8):
                        slot = dm % 2
                        nst = tt * 8 + dm
                        gpsimd.wait_ge(ae, ae_oev(dm))
                        gpsimd.dma_start(
                            out=a0[dm * 128:(dm + 1) * 128, tt * 128:(tt + 1) * 128],
                            in_=otb[:, slot * 128:slot * 128 + 128]
                            ).then_inc(stm[nst % 2], 16)
                        C["st%d" % (nst % 2)] += 16
            C["pe"] += 33
            C["ae"] += AE_TOTAL + 4 + 8
            C["ve"] += 8 * (7 if tt == 0 else 6) + 8 + 1 + 64 + 16
            prev_eav_end = ve_eav(7)
            prev_combine_end = ve_att_done

        with nc.Block() as blk:
            @blk.sync
            def _(sync):
                sync.wait_ge(stm[0], C["st0"])
                sync.wait_ge(stm[1], C["st1"])


def build_l0attn():
    nc = bass.Bass("TRN2", target_bir_lowering=False, debug=False)
    xn = nc.dram_tensor("xn", [D, L], BF16, kind="ExternalInput").ap()
    win = {nm: nc.dram_tensor(nm, list(shp), F32, kind="ExternalInput").ap()
           for nm, shp in WSHAPES}
    iwrow = nc.dram_tensor("iwrow", [1, 4], F32, kind="ExternalInput").ap()
    tabs = {nm: nc.dram_tensor(nm, [128, cols],
                               BF16 if nm in ("csl", "ssl") else F32,
                               kind="ExternalInput").ap()
            for nm, cols in [("c2t", 256), ("s2t", 256), ("ct_", 256), ("st_", 256),
                             ("cp63", 256), ("sp63", 256), ("csl", 512), ("ssl", 512),
                             ("ramp", 2048)]}
    a0 = nc.dram_tensor("a0", [D, L], F32, kind="ExternalOutput").ap()
    emit_l0attn(nc, xn, win, iwrow, tabs, a0)
    return nc



def build_fused():
    nc = bass.Bass("TRN2", target_bir_lowering=False, debug=False)
    x = nc.dram_tensor("x", [D, L], F32, kind="ExternalInput").ap()
    w1T = nc.dram_tensor("w1T", [D, FF], F32, kind="ExternalInput").ap()
    w2T = nc.dram_tensor("w2T", [FF, D], F32, kind="ExternalInput").ap()
    b1c = nc.dram_tensor("b1c", [128, NFC], F32, kind="ExternalInput").ap()
    b2c = nc.dram_tensor("b2c", [128, ND], F32, kind="ExternalInput").ap()
    win = {nm: nc.dram_tensor(nm, list(shp), F32, kind="ExternalInput").ap()
           for nm, shp in WSHAPES}
    iwrow = nc.dram_tensor("iwrow", [1, 4], F32, kind="ExternalInput").ap()
    tabs = {nm: nc.dram_tensor(nm, [128, cols],
                               BF16 if nm in ("csl", "ssl") else F32,
                               kind="ExternalInput").ap()
            for nm, cols in [("c2t", 256), ("s2t", 256), ("ct_", 256), ("st_", 256),
                             ("cp63", 256), ("sp63", 256), ("csl", 512), ("ssl", 512),
                             ("ramp", 2048)]}
    fc1A = nc.dram_tensor("fc1A", [256, D, 64], F32, kind="ExternalInput").ap()
    fc1B = nc.dram_tensor("fc1B", [256, 64, FF], F32, kind="ExternalInput").ap()
    fc2A = nc.dram_tensor("fc2A", [256, FF, 64], F32, kind="ExternalInput").ap()
    fc2B = nc.dram_tensor("fc2B", [256, 64, D], F32, kind="ExternalInput").ap()
    lg1 = nc.dram_tensor("lg1", [1, 256], F32, kind="ExternalInput").ap()
    lg2 = nc.dram_tensor("lg2", [1, 256], F32, kind="ExternalInput").ap()
    riota1 = nc.dram_tensor("riota1", [128, 1], F32, kind="ExternalInput").ap()
    riota2 = nc.dram_tensor("riota2", [128, 1], F32, kind="ExternalInput").ap()
    wqkvT = nc.dram_tensor("wqkvT", [D, 3072], F32, kind="ExternalInput").ap()
    brow = nc.dram_tensor("brow", [1, 3072], F32, kind="ExternalInput").ap()
    ctab = nc.dram_tensor("ctab", [128, 16 * 32], F32, kind="ExternalInput").ap()
    stab = nc.dram_tensor("stab", [128, 16 * 32], F32, kind="ExternalInput").ap()
    lepw = nc.dram_tensor("lepw", [128, 4 * 8], F32, kind="ExternalInput").ap()
    out = nc.dram_tensor("out", [D, L], F32, kind="ExternalOutput").ap()
    xn0 = nc.dram_tensor("xn0", [D, L], BF16, kind="Internal").ap()
    f0 = nc.dram_tensor("f0", [D, L], F32, kind="Internal").ap()
    a0 = nc.dram_tensor("a0", [D, L], F32, kind="Internal").ap()
    x1 = nc.dram_tensor("x1", [D, L], F32, kind="Internal").ap()
    xn1 = nc.dram_tensor("xn1", [D, L], BF16, kind="Internal").ap()
    f1 = nc.dram_tensor("f1", [D, L], F32, kind="Internal").ap()
    a1 = nc.dram_tensor("a1", [D, L], F32, kind="Internal").ap()

    with _phase("p0"):
        emit_ln(nc, x, xn0)
    with _phase("p1"):
        emit_l0ffn(nc, xn0, w1T, w2T, b1c, b2c, f0)
    with _phase("p2"):
        emit_l0attn(nc, xn0, win, iwrow, tabs, a0)
    with _phase("p3"):
        emit_res_ln(nc, x, a0, f0, x1, xn1)
    with _phase("p4"):
        emit_l1ffn(nc, xn1, fc1A, fc1B, fc2A, fc2B, lg1, lg2, riota1, riota2, f1)
    with _phase("p5"):
        emit_l1attn(nc, xn1, wqkvT, brow, ctab, stab, lepw, a1)
    with _phase("p6"):
        emit_res2(nc, x1, a1, f1, out)
    return nc


def _fm(a):
    return np.ascontiguousarray(a.T)


LAUNCH_CACHE = {}


def _get(name, builder):
    if name not in LAUNCH_CACHE:
        LAUNCH_CACHE[name] = builder()
    return LAUNCH_CACHE[name]


def _host_tabs_l0():
    th = 1.0 / 10000.0 ** (2.0 * np.arange(16, dtype=np.float32) / 32)
    t = np.arange(L, dtype=np.float32)[:, None]

    def tile128(m):
        return np.ascontiguousarray(
            m.reshape(16, 128, 16).transpose(1, 0, 2).reshape(128, 256)
        ).astype(np.float32)

    tabs = {}
    tabs["c2t"] = tile128(np.cos(2 * t * th))
    tabs["s2t"] = tile128(np.sin(2 * t * th))
    tabs["ct_"] = tile128(np.cos(t * th))
    tabs["st_"] = tile128(np.sin(t * th))
    tabs["cp63"] = tile128(np.cos((t + 63) * th))
    tabs["sp63"] = tile128(np.sin((t + 63) * th))
    j = np.arange(32, dtype=np.float32)[:, None]
    tabs["csl"] = np.broadcast_to(
        np.cos((64 + j) * th).reshape(1, 512), (128, 512)).astype(BF).copy()
    tabs["ssl"] = np.broadcast_to(
        np.sin((64 + j) * th).reshape(1, 512), (128, 512)).astype(BF).copy()
    ramp = -(np.arange(L, dtype=np.float64) + 1) * 1e-35
    tabs["ramp"] = np.broadcast_to(
        ramp.astype(np.float32).reshape(1, L), (128, L)).copy()
    return tabs


def _host_tabs_l1(w_lepe, b_lepe):
    pos = np.arange(L, dtype=np.float32)[:, None]
    div = np.exp(np.arange(0, 64, 2, dtype=np.float32) * (-np.log(10000.0) / 64))
    ang = pos * div
    ctab = np.ascontiguousarray(
        np.cos(ang).astype(np.float32).reshape(16, 128, 32).transpose(1, 0, 2).reshape(128, 512))
    stab = np.ascontiguousarray(
        np.sin(ang).astype(np.float32).reshape(16, 128, 32).transpose(1, 0, 2).reshape(128, 512))
    wl = w_lepe.reshape(D, 3)
    lepw = np.zeros((128, 32), np.float32)
    for ct in range(8):
        for j in range(3):
            lepw[:, j * 8 + ct] = wl[ct * 128:(ct + 1) * 128, j]
        lepw[:, 24 + ct] = b_lepe[ct * 128:(ct + 1) * 128]
    return ctab, stab, lepw


def kernel(**inputs):
    x = inputs["L0_attn_x"]
    cores = list(range(N_CORES))
    nc = _get("fused", build_fused)
    base = {"w1T": np.ascontiguousarray(inputs["L0_ffn_w1"].T),
            "w2T": np.ascontiguousarray(inputs["L0_ffn_w2"].T),
            "b1c": np.ascontiguousarray(inputs["L0_ffn_b1"].reshape(NFC, 128).T),
            "b2c": np.ascontiguousarray(inputs["L0_ffn_b2"].reshape(ND, 128).T),
            "wdkv": inputs["L0_attn_W_dkv"], "wuk": inputs["L0_attn_W_uk"],
            "wuv": inputs["L0_attn_W_uv"], "wdq": inputs["L0_attn_W_dq"],
            "wuq": inputs["L0_attn_W_uq"], "wqr": inputs["L0_attn_W_qr"],
            "wkr": inputs["L0_attn_W_kr"], "wout": inputs["L0_attn_W_out"],
            "iwq": inputs["L0_attn_idx_wq"], "iwk": inputs["L0_attn_idx_wk"],
            "iwrow": inputs["L0_attn_idx_w"].reshape(1, 4),
            "fc1A": inputs["L1_ffn_fc1_A"], "fc1B": inputs["L1_ffn_fc1_B"],
            "fc2A": inputs["L1_ffn_fc2_A"], "fc2B": inputs["L1_ffn_fc2_B"],
            "lg1": inputs["L1_ffn_fc1_logits"].reshape(1, 256),
            "lg2": inputs["L1_ffn_fc2_logits"].reshape(1, 256),
            "wqkvT": np.ascontiguousarray(inputs["L1_attn_w_qkv"].T),
            "brow": inputs["L1_attn_b_qkv"].reshape(1, 3072)}
    base.update(_host_tabs_l0())
    ctab, stab, lepw = _host_tabs_l1(inputs["L1_attn_w_lepe"], inputs["L1_attn_b_lepe"])
    base.update({"ctab": ctab, "stab": stab, "lepw": lepw})
    _ = inputs["L0_attn_raw_delta"]
    iot = np.arange(128, dtype=np.float32).reshape(128, 1)
    in_maps = [dict(base, x=_fm(x[b]),
                    riota1=iot + np.float32(b * 4),
                    riota2=iot + np.float32(b * 16)) for b in cores]
    res = run_bass_kernel_spmd(nc, in_maps, core_ids=cores)
    out = np.stack([res.results[b]["out"].T for b in cores])
    return np.ascontiguousarray(out, dtype=np.float32)
